# version 11
# baseline (speedup 1.0000x reference)
import math
import contextlib
import numpy as np
import concourse.bass as bass
import concourse.mybir as mybir
from concourse.bass_utils import run_bass_kernel_spmd

F32 = mybir.dt.float32
I32 = mybir.dt.int32
U32 = mybir.dt.uint32
ALU = mybir.AluOpType
AF = mybir.ActivationFunctionType
AX = mybir.AxisListType

D = 2048
HH = 2816
NCC = 16
NHC = 22
NJ = 8
TL = 1024
NH = 16
RG = [[0, 1, 2, 3], [4, 5, 6, 7]]
SS = 1.0 / math.sqrt(128.0)
EPS = 1e-5
NEG = -1e30


def build_nc():
    nc = bass.Bass("TRN2", target_bir_lowering=False)
    xT = nc.dram_tensor("xT", [D, TL], F32, kind="ExternalInput").ap()
    xTs = nc.dram_tensor("xTs", [D, TL], F32, kind="ExternalInput").ap()
    xtok = nc.dram_tensor("xtok", [TL, D], F32, kind="ExternalInput").ap()
    wkT = nc.dram_tensor("wkT", [D, HH], F32, kind="ExternalInput").ap()
    wrT = nc.dram_tensor("wrT", [D, HH], F32, kind="ExternalInput").ap()
    wvT = nc.dram_tensor("wvT", [HH, D], F32, kind="ExternalInput").ap()
    wqT = nc.dram_tensor("wqT", [D, D], F32, kind="ExternalInput").ap()
    wkvgT = nc.dram_tensor("wkvgT", [D, 4112], F32, kind="ExternalInput").ap()
    tmkI = nc.dram_tensor("tmk", [128, NCC], F32, kind="ExternalInput").ap()
    tmrI = nc.dram_tensor("tmr", [128, NCC], F32, kind="ExternalInput").ap()
    tdI = nc.dram_tensor("td", [128, NHC], F32, kind="ExternalInput").ap()
    gamI = nc.dram_tensor("gam", [1, D], F32, kind="ExternalInput").ap()
    betI = nc.dram_tensor("bet", [1, D], F32, kind="ExternalInput").ap()
    out = nc.dram_tensor("out", [TL, D], F32, kind="ExternalOutput").ap()
    rkd = nc.dram_tensor("rkd", [1, 1], I32).ap()
    ag1i = nc.dram_tensor("ag1i", [1, 176 * 128], F32).ap()
    ag1o = nc.dram_tensor("ag1o", [4, 176 * 128], F32).ap()
    ag2i = nc.dram_tensor("ag2i", [1, 2048 * 128], F32).ap()
    ag2o = nc.dram_tensor("ag2o", [4, 2048 * 128], F32).ap()

    with contextlib.ExitStack() as ctx:
        En = ctx.enter_context
        A = En(nc.sbuf_tensor("A", [128, 16384], F32))
        U = En(nc.sbuf_tensor("U", [128, 22528], F32))
        P = En(nc.sbuf_tensor("P", [128, 10240], F32))
        tmk = En(nc.sbuf_tensor("tmkS", [128, NCC], F32))
        tmr = En(nc.sbuf_tensor("tmrS", [128, NCC], F32))
        tdv = En(nc.sbuf_tensor("tdv", [128, NHC], F32))
        lnw = En(nc.sbuf_tensor("lnw", [128, NHC], F32))
        wvec = En(nc.sbuf_tensor("wvec", [128, NHC], F32))
        w128 = En(nc.sbuf_tensor("w128", [128, NHC], F32))
        wpow = En(nc.sbuf_tensor("wpow", [128, 256], F32))
        iocol = En(nc.sbuf_tensor("iocol", [128, 128], F32))
        iorow = En(nc.sbuf_tensor("iorow", [128, 1], F32))
        ident = En(nc.sbuf_tensor("ident", [128, 128], F32))
        dadd = En(nc.sbuf_tensor("dadd", [128, 128], F32))
        madd = En(nc.sbuf_tensor("madd", [128, 512], F32))
        rkb = En(nc.sbuf_tensor("rkb", [128, 1], I32))
        rmeq = En(nc.sbuf_tensor("rmeq", [128, 4], F32))
        rmsc = En(nc.sbuf_tensor("rmsc", [128, 4], F32))
        stage = En(nc.sbuf_tensor("stage", [128, 1024], F32))
        gbuf = En(nc.sbuf_tensor("gbuf", [128, 16], F32))
        gm8 = En(nc.sbuf_tensor("gm8", [128, 8], F32))
        gidx = En(nc.sbuf_tensor("gidx", [128, 8], U32))
        gmask = En(nc.sbuf_tensor("gmask", [128, NJ * 16], F32))
        vmask = En(nc.sbuf_tensor("vmask", [128, NJ * 16], F32))
        gnb = En(nc.sbuf_tensor("gnb", [128, 1], F32))
        gsum = En(nc.sbuf_tensor("gsum", [128, 1], F32))
        top1 = En(nc.sbuf_tensor("top1", [128, NJ], F32))
        bnst = En(nc.sbuf_tensor("bnst", [128, 24], F32))
        bnag = En(nc.sbuf_tensor("bnag", [128, 2], F32))
        rstd = En(nc.sbuf_tensor("rstd", [128, 1], F32))
        nmu = En(nc.sbuf_tensor("nmu", [128, 1], F32))
        varb = En(nc.sbuf_tensor("varb", [128, 1], F32))
        mrun = En(nc.sbuf_tensor("mrun", [128, 1], F32))
        mnew = En(nc.sbuf_tensor("mnew", [128, 2], F32))
        mtmp = En(nc.sbuf_tensor("mtmp", [128, 4], F32))
        sbias = En(nc.sbuf_tensor("sbias", [128, 2], F32))
        alpha = En(nc.sbuf_tensor("alpha", [128, 1], F32))
        drun = En(nc.sbuf_tensor("drun", [128, 1], F32))
        dtmp = En(nc.sbuf_tensor("dtmp", [128, 4], F32))
        dblk = En(nc.sbuf_tensor("dblk", [128, 1], F32))
        rdd = En(nc.sbuf_tensor("rdd", [128, 1], F32))
        ysb = En(nc.sbuf_tensor("ysb", [128, 128], F32))
        ysb2 = En(nc.sbuf_tensor("ysb2", [128, 128], F32))
        st1s = En(nc.sbuf_tensor("st1s", [128, 8], F32))
        wgs = En(nc.sbuf_tensor("wgs", [128, 256], F32))
        PS = En(nc.psum_tensor("PS", [128, 4096], F32))

        sems, cnt = {}, {}
        for name in ("s_x0", "s_x1", "s_w0", "s_w1", "s_t0", "s_t1",
                     "s_g", "s_o", "s_q",
                     "t_m", "v_p", "a_p", "g_p", "c_c"):
            sems[name] = En(nc.semaphore(name))
            cnt[name] = 0

        def inc(op, name, n=None):
            if n is None:
                n = 16 if name.startswith("s_") else 1
            cnt[name] += n
            op.then_inc(sems[name], n)
            return cnt[name]

        def wt(eng, name, v):
            if v > 0:
                eng.wait_ge(sems[name], v)

        block = En(nc.Block())

        def u_m(m):
            return U[:, m * 1024:(m + 1) * 1024]

        def xk_cc(n, cc):
            return A[:, n * 8192 + cc * 512: n * 8192 + (cc + 1) * 512]

        def a_tile(j):
            return A[:, j * 2048:(j + 1) * 2048]

        def x1t_cc(cc):
            return U[:, cc * 1024:(cc + 1) * 1024]

        qh_v = lambda p: U[:, 16384 + p * 1024: 16384 + (p + 1) * 1024]
        P32 = U[:, 18432:20480]
        PT = U[:, 20480:22528]
        ks_loc = U[:, 18432:19456]
        vs_loc = U[:, 19456:20480]
        ksT_loc = U[:, 20480:21504]
        xs_v = lambda p: P[:, p * 1024:(p + 1) * 1024]
        wsl_v = lambda p: P[:, 2048 + p * 2816: 2048 + (p + 1) * 2816]
        lend = P[:, 7680:7856]
        ends = P[:, 7856:8560]
        chz = P[:, 8560:9286]
        cary = P[:, 9286:9462]
        eraw = P[:, 9462:10166]
        gamBC = P[:, 0:2048]
        betBC = P[:, 2048:4096]
        xtok_v = lambda p: P[:, 4096 + p * 2048: 4096 + (p + 1) * 2048]
        w2_v = lambda p: P[:, p * 4096:(p + 1) * 4096]
        wq_v = P[:, 0:2048]
        ksF = P[:, 2048:6144]
        vsF = P[:, 6144:10240]

        def bank(k, w=512):
            return PS[:, k * 512: k * 512 + w]

        def _(s):
            op = s.dma_start(out=tmk[:], in_=tmkI); inc(op, "s_w0")
            op = s.dma_start(out=tmr[:], in_=tmrI); inc(op, "s_w0")
            op = s.dma_start(out=tdv[:], in_=tdI); inc(op, "s_w0")
        block.sync(_)
        W_TD = cnt["s_w0"]

        def _(g):
            op = g.iota(iocol[:], pattern=[[1, 128]], base=0,
                        channel_multiplier=0,
                        allow_small_or_imprecise_dtypes=True)
            inc(op, "g_p")
            op = g.iota(iorow[:], pattern=[[1, 1]], base=0,
                        channel_multiplier=1,
                        allow_small_or_imprecise_dtypes=True)
            inc(op, "g_p")
            rk = g.cc_rank(RG)
            op = g.store(rkb[0:1, 0:1], rk); inc(op, "g_p")
            g.wait_ge(sems["g_p"], cnt["g_p"])
            op = g.dma_start(out=rkd, in_=rkb[0:1, :]); inc(op, "s_g")
            g.wait_ge(sems["s_g"], cnt["s_g"])
            op = g.dma_start(out=rkb[:], in_=rkd.partition_broadcast(128))
            inc(op, "s_g")
        block.gpsimd(_)
        G_IOTA = 3
        G_RANKB = cnt["s_g"]

        def _(s):
            wt(s, "s_w0", W_TD)
            s.activation(lnw[:], tdv[:], AF.Exp, bias=0.0, scale=1.0)
            op = s.drain(); inc(op, "a_p")
        block.scalar(_)
        A_ETD = cnt["a_p"]

        def _(v):
            wt(v, "a_p", A_ETD)
            v.tensor_scalar(lnw[:], lnw[:], -1.0, None, op0=ALU.mult)
            v.drain()
            op = v.engine_nop(); inc(op, "v_p")
        block.vector(_)
        V_LNW = cnt["v_p"]

        def _(s):
            wt(s, "v_p", V_LNW)
            s.activation(wvec[:], lnw[:], AF.Exp, bias=0.0, scale=1.0)
            s.activation(w128[:], lnw[:], AF.Exp, bias=0.0, scale=128.0)
            op = s.drain(); inc(op, "a_p")
        block.scalar(_)
        A_WV = cnt["a_p"]

        def _(v):
            wt(v, "g_p", G_IOTA)
            v.tensor_scalar(ident[:], iocol[:], iorow[:], None, op0=ALU.is_equal)
            v.tensor_scalar(dadd[:], iocol[:], iorow[:], NEG,
                            op0=ALU.is_gt, op1=ALU.mult)
            wt(v, "s_g", G_RANKB)
            for rp in range(4):
                v.tensor_scalar(rmeq[:, rp:rp + 1], rkb[:], rp, None,
                                op0=ALU.is_equal)
                v.tensor_scalar(rmsc[:, rp:rp + 1], rkb[:], rp, NEG,
                                op0=ALU.is_lt, op1=ALU.mult)
            v.drain()
            for rp in range(4):
                v.tensor_scalar(madd[:, rp * 128:(rp + 1) * 128], dadd[:],
                                rmeq[:, rp:rp + 1], rmsc[:, rp:rp + 1],
                                op0=ALU.mult, op1=ALU.add)
            v.drain()
            op = v.engine_nop(); inc(op, "v_p")
        block.vector(_)
        V_CONST = cnt["v_p"]

        def emit_xbuild(tm, first_wait_tm, first_wait_v):
            built = []
            xs_ld = []
            units = [(n, cc) for n in range(2) for cc in range(NCC)]
            for i, (n, cc) in enumerate(units):
                p = i % 2

                def _(s, n=n, cc=cc, p=p, i=i):
                    if i < 2 and first_wait_v:
                        wt(s, "v_p", first_wait_v)
                    if i >= 2:
                        wt(s, "v_p", built[i - 2])
                    op = s.dma_start(out=xs_v(p)[:, 0:512],
                                     in_=xT[cc * 128:(cc + 1) * 128,
                                            n * 512:(n + 1) * 512])
                    inc(op, "s_x%d" % p)
                    op = s.dma_start(out=xs_v(p)[:, 512:1024],
                                     in_=xTs[cc * 128:(cc + 1) * 128,
                                             n * 512:(n + 1) * 512])
                    xs_ld.append(inc(op, "s_x%d" % p))
                block.sync(_)

                def _(v, n=n, cc=cc, p=p, i=i):
                    wt(v, "s_x%d" % p, xs_ld[i])
                    if i == 0:
                        wt(v, "s_w0", W_TD)
                        if first_wait_tm:
                            wt(v, "t_m", first_wait_tm)
                    v.tensor_tensor(xk_cc(n, cc), xs_v(p)[:, 512:1024],
                                    xs_v(p)[:, 0:512], op=ALU.subtract)
                    v.drain()
                    op = v.scalar_tensor_tensor(xk_cc(n, cc), xk_cc(n, cc),
                                                tm[:, cc:cc + 1],
                                                xs_v(p)[:, 0:512],
                                                op0=ALU.mult, op1=ALU.add)
                    built.append(inc(op, "v_p"))
                block.vector(_)
            return built[-1]

        def emit_proj(wsrc, nm, nk, rhs_fn, evac_emit, evac_list,
                      ready_v=None, ready_extra=None, slab_cols=128,
                      init_tm=0):
            t_groups = []
            slab_ld = []
            for m in range(nm):
                p = m % 2

                def _(s, m=m, p=p):
                    if m < 2:
                        wt(s, "t_m", init_tm)
                    if m >= 2:
                        wt(s, "t_m", t_groups[(m - 2) * 2 + 1])
                    op = s.dma_start(
                        out=wsl_v(p)[:, 0:nk * 128].rearrange(
                            "p (kc q) -> p kc q", kc=nk),
                        in_=wsrc.rearrange("(kc p) q -> p kc q", p=128)[
                            :, :, m * slab_cols:(m + 1) * slab_cols])
                    slab_ld.append(inc(op, "s_w%d" % p))
                block.sync(_)

                for n in range(2):
                    gi = m * 2 + n

                    def _(t, m=m, n=n, p=p, gi=gi):
                        if n == 0:
                            wt(t, "s_w%d" % p, slab_ld[m])
                        if gi == 0:
                            if ready_v is not None:
                                wt(t, "v_p", ready_v)
                            if ready_extra is not None:
                                ready_extra(t)
                        if gi >= 2:
                            kind, val = evac_list[gi - 2]
                            wt(t, kind, val)
                        for kc in range(nk):
                            op = t.matmul(bank(gi % 2),
                                          wsl_v(p)[:, kc * 128:(kc + 1) * 128],
                                          rhs_fn(kc, n),
                                          start=(kc == 0), stop=(kc == nk - 1))
                        t_groups.append(inc(op, "t_m"))
                    block.tensor(_)
                    evac_emit(m, n, gi, t_groups)
            return t_groups

        xk_ready = emit_xbuild(tmk, 0, 0)

        k_ev = []

        def k_evac(m, n, gi, t_groups):
            def _(s, m=m, n=n, gi=gi):
                wt(s, "t_m", t_groups[gi])
                op = s.activation(u_m(m)[:, n * 512:(n + 1) * 512],
                                  bank(gi % 2), AF.Relu, bias=0.0, scale=1.0)
                k_ev.append(("a_p", inc(op, "a_p")))
            block.scalar(_)

        tg_k = emit_proj(wkT, NHC, NCC, lambda kc, n: xk_cc(n, kc),
                         k_evac, k_ev, ready_v=xk_ready)
        T_KALL = tg_k[-1]

        scan_done = []
        for m in range(NHC):
            def _(v, m=m):
                wt(v, "a_p", k_ev[m * 2 + 1][1])
                for j in range(NJ):
                    v.tensor_tensor_scan(
                        u_m(m)[:, j * 128:(j + 1) * 128],
                        wvec[:, m:m + 1].broadcast_to([128, 128]),
                        u_m(m)[:, j * 128:(j + 1) * 128], 0.0,
                        op0=ALU.mult, op1=ALU.add)
                v.drain()
                op = v.tensor_copy(lend[:, m * 8:(m + 1) * 8],
                                   u_m(m)[:, 127::128])
                scan_done.append(inc(op, "v_p"))
            block.vector(_)
        V_SCANALL = scan_done[-1]

        def _(g):
            wt(g, "v_p", V_SCANALL)
            op = g.dma_start(out=ag1i.rearrange("o (p c) -> p (o c)", p=128),
                             in_=lend[:])
            inc(op, "s_g")
            g.wait_ge(sems["s_g"], cnt["s_g"])
            op = g.collective_compute("AllGather", ALU.bypass,
                                      replica_groups=RG,
                                      ins=[ag1i.opt()], outs=[ag1o.opt()])
            inc(op, "c_c")
            g.wait_ge(sems["c_c"], cnt["c_c"])
            for r in range(4):
                op = g.dma_start(
                    out=eraw[:, r * 176:(r + 1) * 176],
                    in_=ag1o[r:r + 1, :].rearrange(
                        "o (p c) -> p (o c)", p=128))
                inc(op, "s_g")
        block.gpsimd(_)
        G_ENDS = cnt["s_g"]

        xr_ready = emit_xbuild(tmr, T_KALL, xk_ready)

        def _(v):
            wt(v, "s_g", G_ENDS)
            wt(v, "a_p", A_WV)
            for r in range(4):
                v.tensor_copy(
                    ends[:].rearrange("p (m j) -> p m j", m=NHC)[:, :, r::4],
                    eraw[:, r * 176:(r + 1) * 176].rearrange(
                        "p (m j) -> p m j", m=NHC))
            v.memset(chz[:], 0.0)
            v.drain()
            for m in range(NHC):
                v.tensor_tensor_scan(
                    chz[:, m * 33 + 1: m * 33 + 33],
                    w128[:, m:m + 1].broadcast_to([128, 32]),
                    ends[:, m * 32:(m + 1) * 32], 0.0,
                    op0=ALU.mult, op1=ALU.add)
            v.drain()
            for m in range(NHC):
                v.tensor_scalar(cary[:, m * 8:(m + 1) * 8],
                                chz[:, m * 33: m * 33 + 29: 4],
                                rmeq[:, 0:1], None, op0=ALU.mult)
            v.drain()
            for rp in range(1, 4):
                for m in range(NHC):
                    v.scalar_tensor_tensor(
                        cary[:, m * 8:(m + 1) * 8],
                        chz[:, m * 33 + rp: m * 33 + rp + 29: 4],
                        rmeq[:, rp:rp + 1],
                        cary[:, m * 8:(m + 1) * 8],
                        op0=ALU.mult, op1=ALU.add)
                v.drain()
            op = v.engine_nop(); inc(op, "v_p")
        block.vector(_)
        V_CARY = cnt["v_p"]

        wpow_done = []
        corr_done = []
        for m in range(NHC):
            def _(s, m=m):
                if m == 0:
                    wt(s, "a_p", A_WV)
                    wt(s, "g_p", G_IOTA)
                if m >= 2:
                    wt(s, "v_p", corr_done[m - 2])
                op = s.activation(wpow[:, (m % 2) * 128:(m % 2 + 1) * 128],
                                  iocol[:], AF.Exp, bias=lnw[:, m:m + 1],
                                  scale=lnw[:, m:m + 1])
                wpow_done.append(inc(op, "a_p"))
            block.scalar(_)

            def _(v, m=m):
                if m == 0:
                    wt(v, "v_p", V_CARY)
                wt(v, "a_p", wpow_done[m])
                for j in range(NJ):
                    v.scalar_tensor_tensor(
                        u_m(m)[:, j * 128:(j + 1) * 128],
                        wpow[:, (m % 2) * 128:(m % 2 + 1) * 128],
                        cary[:, m * 8 + j: m * 8 + j + 1],
                        u_m(m)[:, j * 128:(j + 1) * 128],
                        op0=ALU.mult, op1=ALU.add)
                v.drain()
                op = v.engine_nop()
                corr_done.append(inc(op, "v_p"))
            block.vector(_)


        r_ev = []

        def r_evac(m, n, gi, t_groups):
            def _(v, m=m, n=n, gi=gi):
                wt(v, "t_m", t_groups[gi])
                wt(v, "v_p", corr_done[m])
                v.drain()
                op = v.tensor_tensor(u_m(m)[:, n * 512:(n + 1) * 512],
                                     bank(gi % 2),
                                     u_m(m)[:, n * 512:(n + 1) * 512],
                                     op=ALU.mult)
                r_ev.append(("v_p", inc(op, "v_p")))
            block.vector(_)

        tg_r = emit_proj(wrT, NHC, NCC, lambda kc, n: xk_cc(n, kc),
                         r_evac, r_ev, ready_v=xr_ready, init_tm=T_KALL)
        T_RALL = tg_r[-1]
        V_RLAST = r_ev[-1][1]

        wv_ev = []
        wv_tr = []
        wv_at = []

        def wv_evac(m2, n, gi, t_groups):
            def _(v, gi=gi):
                wt(v, "t_m", t_groups[gi])
                if gi >= 2:
                    wt(v, "t_m", wv_tr[gi - 2])
                op = v.tensor_copy(stage[:, (gi % 2) * 512:(gi % 2 + 1) * 512],
                                   bank(gi % 2))
                wv_ev.append(("v_p", inc(op, "v_p")))
            block.vector(_)

            def _(t, m2=m2, n=n, gi=gi):
                wt(t, "v_p", wv_ev[gi][1])
                if gi == 0:
                    wt(t, "v_p", V_CONST)
                if gi >= 1:
                    wt(t, "a_p", wv_at[gi - 1])
                for jj in range(4):
                    op = t.transpose(bank(6)[:, jj * 128:(jj + 1) * 128],
                                     stage[:, (gi % 2) * 512 + jj * 128:
                                           (gi % 2) * 512 + (jj + 1) * 128],
                                     ident[:])
                wv_tr.append(inc(op, "t_m"))
            block.tensor(_)

            def _(s, m2=m2, n=n, gi=gi):
                wt(s, "t_m", wv_tr[gi])
                op = s.activation(
                    A[:, 0:16384].rearrange("p (j c) -> p j c", j=8)[
                        :, 4 * n: 4 * n + 4, m2 * 128:(m2 + 1) * 128],
                    bank(6).rearrange("p (jj c) -> p jj c", jj=4),
                    AF.Copy, bias=0.0, scale=1.0)
                wv_at.append(inc(op, "a_p"))
            block.scalar(_)

        def wv_first_wait(t):
            wt(t, "v_p", V_RLAST)

        tg_wv = emit_proj(wvT, NCC, NHC,
                          lambda hc, n: u_m(hc)[:, n * 512:(n + 1) * 512],
                          wv_evac, wv_ev, ready_extra=wv_first_wait,
                          init_tm=T_RALL)
        T_WVALL = tg_wv[-1]
        A_ATTNTOK = wv_at[-1]

        def _(g):
            wt(g, "v_p", xr_ready)
            wt(g, "t_m", T_WVALL)
            op = g.dma_start(out=gamBC, in_=gamI.partition_broadcast(128))
            inc(op, "s_g")
            op = g.dma_start(out=betBC, in_=betI.partition_broadcast(128))
            inc(op, "s_g")
        block.gpsimd(_)
        G_GB = cnt["s_g"]

        def emit_lnstats(v, src):
            for c4 in range(4):
                v.bn_stats(bnst[:, c4 * 6:(c4 + 1) * 6],
                           src[:, c4 * 512:(c4 + 1) * 512])
            v.drain()
            v.bn_aggr(bnag[:], bnst[:])
            v.drain()
            v.tensor_scalar(varb[:], bnag[:, 1:2], 1.0, EPS,
                            op0=ALU.mult, op1=ALU.add)
            v.drain()

        def _(s):
            wt(s, "t_m", T_WVALL)
            op = s.dma_start(
                out=wgs[:].rearrange("p (kc q) -> p kc q", kc=16),
                in_=wkvgT.rearrange("(kc p) q -> p kc q", p=128)[
                    :, :, 4096:4112])
            inc(op, "s_w0")
        block.sync(_)
        W_GS = cnt["s_w0"]

        xtok_ld = []
        ln_done = []
        x1t_done = []
        gate_ev = []
        for j in range(NJ):
            def _(g, j=j):
                if j == 0:
                    wt(g, "t_m", T_WVALL)
                if j >= 2:
                    wt(g, "v_p", ln_done[j - 2])
                op = g.dma_start(out=xtok_v(j % 2),
                                 in_=xtok[j * 128:(j + 1) * 128, :])
                xtok_ld.append(inc(op, "s_t%d" % (j % 2)))
            block.gpsimd(_)

            def _(v, j=j):
                wt(v, "a_p", A_ATTNTOK)
                emit_lnstats(v, a_tile(j))
                op = v.engine_nop(); inc(op, "v_p")
            block.vector(_)
            v_st1 = cnt["v_p"]

            def _(s, j=j, v_st1=v_st1):
                wt(s, "v_p", v_st1)
                op = s.activation(rstd[:], varb[:], AF.Sqrt, bias=0.0, scale=1.0)
                inc(op, "a_p")
            block.scalar(_)
            a_rs1 = cnt["a_p"]

            def _(v, j=j, a_rs1=a_rs1):
                wt(v, "a_p", a_rs1)
                src = a_tile(j)
                v.reciprocal(rstd[:], rstd[:])
                v.drain()
                v.tensor_scalar(nmu[:], bnag[:, 0:1], rstd[:], -1.0,
                                op0=ALU.mult, op1=ALU.mult)
                v.drain()
                v.tensor_scalar(src, src, rstd[:], nmu[:],
                                op0=ALU.mult, op1=ALU.add)
                v.drain()
                wt(v, "s_g", G_GB)
                v.tensor_tensor(src, src, gamBC, op=ALU.mult)
                v.drain()
                v.tensor_tensor(src, src, betBC, op=ALU.add)
                v.drain()
                wt(v, "s_t%d" % (j % 2), xtok_ld[j])
                v.tensor_tensor(src, src, xtok_v(j % 2), op=ALU.add)
                v.drain()
                emit_lnstats(v, src)
                op = v.engine_nop(); inc(op, "v_p")
            block.vector(_)
            v_st2 = cnt["v_p"]

            def _(s, j=j, v_st2=v_st2):
                wt(s, "v_p", v_st2)
                op = s.activation(rstd[:], varb[:], AF.Sqrt, bias=0.0, scale=1.0)
                inc(op, "a_p")
            block.scalar(_)
            a_rs2 = cnt["a_p"]

            def _(v, j=j, a_rs2=a_rs2):
                wt(v, "a_p", a_rs2)
                src = a_tile(j)
                v.reciprocal(rstd[:], rstd[:])
                v.drain()
                v.tensor_scalar(nmu[:], bnag[:, 0:1], rstd[:], -1.0,
                                op0=ALU.mult, op1=ALU.mult)
                v.drain()
                op = v.tensor_scalar(src, src, rstd[:], nmu[:],
                                     op0=ALU.mult, op1=ALU.add)
                ln_done.append(inc(op, "v_p"))
            block.vector(_)

            for q4 in range(4):
                def _(t, j=j, q4=q4):
                    if q4 == 0:
                        wt(t, "v_p", ln_done[j])
                    if x1t_done:
                        wt(t, "a_p", x1t_done[-1])
                    for mm in range(4):
                        m2 = q4 * 4 + mm
                        op = t.transpose(bank(6)[:, mm * 128:(mm + 1) * 128],
                                         a_tile(j)[:, m2 * 128:(m2 + 1) * 128],
                                         ident[:])
                    inc(op, "t_m")
                block.tensor(_)
                t_x1 = cnt["t_m"]

                def _(s, j=j, q4=q4, t_x1=t_x1):
                    wt(s, "t_m", t_x1)
                    op = s.activation(
                        U[:, 0:16384].rearrange("p (cc c) -> p cc c", cc=16)[
                            :, q4 * 4: q4 * 4 + 4, j * 128:(j + 1) * 128],
                        bank(6).rearrange("p (mm c) -> p mm c", mm=4),
                        AF.Copy, bias=0.0, scale=1.0)
                    x1t_done.append(inc(op, "a_p"))
                block.scalar(_)
            def _(t, j=j):
                wt(t, "s_w0", W_GS)
                wt(t, "a_p", x1t_done[-1])
                if gate_ev:
                    wt(v if False else t, "v_p", gate_ev[-1])
                for kc in range(NCC):
                    op = t.matmul(bank(5, 16),
                                  U[:, kc * 1024 + j * 128: kc * 1024 + (j + 1) * 128],
                                  wgs[:, kc * 16:(kc + 1) * 16],
                                  start=(kc == 0), stop=(kc == NCC - 1))
                inc(op, "t_m")
            block.tensor(_)
            t_g = cnt["t_m"]

            def _(v, j=j, t_g=t_g):
                wt(v, "t_m", t_g)
                v.tensor_copy(gbuf[:], bank(5, 16))
                v.drain()
                v.max(gm8[:], gbuf[:])
                v.drain()
                v.max_index(gidx[:], gm8[:], gbuf[:])
                v.tensor_scalar(gnb[:], gm8[:, 0:1], -1.0, None, op0=ALU.mult)
                v.drain()
                op = v.engine_nop(); inc(op, "v_p")
            block.vector(_)
            v_g = cnt["v_p"]

            def _(s, j=j, v_g=v_g):
                wt(s, "v_p", v_g)
                op = s.activation(gbuf[:], gbuf[:], AF.Exp, bias=gnb[:],
                                  scale=1.0, accum_out=gsum[:])
                inc(op, "a_p")
            block.scalar(_)
            a_g = cnt["a_p"]

            def _(v, j=j, a_g=a_g):
                wt(v, "a_p", a_g)
                v.reciprocal(top1[:, j:j + 1], gsum[:])
                for h in range(16):
                    v.tensor_scalar(gmask[:, j * 16 + h: j * 16 + h + 1],
                                    gidx[:, 0:1], h, None, op0=ALU.is_equal)
                v.drain()
                op = v.tensor_scalar(vmask[:, j * 16:(j + 1) * 16],
                                     gmask[:, j * 16:(j + 1) * 16],
                                     top1[:, j:j + 1], None, op0=ALU.mult)
                gate_ev.append(inc(op, "v_p"))
            block.vector(_)

        A_X1T = x1t_done[-1]
        V_LNALL = ln_done[-1]
        V_GATEALL = gate_ev[-1]

        kv_chunks = [(i * 256, 256) for i in range(16)]
        kv_slab_ld = []
        kv_tg = []
        kv_ev = []
        for ci, (c0, w) in enumerate(kv_chunks):
            p = ci % 2

            def _(s, ci=ci, c0=c0, w=w, p=p):
                if ci < 2:
                    wt(s, "v_p", V_LNALL)
                if ci >= 2:
                    wt(s, "t_m", kv_tg[(ci - 2) * NJ + NJ - 1])
                op = s.dma_start(
                    out=w2_v(p)[:, 0:16 * w].rearrange(
                        "p (kc q) -> p kc q", kc=16),
                    in_=wkvgT.rearrange("(kc p) q -> p kc q", p=128)[
                        :, :, c0:c0 + w])
                kv_slab_ld.append(inc(op, "s_w%d" % p))
            block.sync(_)

            for j in range(NJ):
                def _(t, ci=ci, j=j, p=p, w=w):
                    if j == 0:
                        wt(t, "s_w%d" % p, kv_slab_ld[ci])
                        if ci == 0:
                            wt(t, "a_p", A_X1T)
                            wt(t, "v_p", V_GATEALL)
                    if ci >= 1:
                        wt(t, "v_p", kv_ev[(ci - 1) * NJ + j])
                    for kc in range(NCC):
                        op = t.matmul(bank(j, w),
                                      x1t_cc(kc)[:, j * 128:(j + 1) * 128],
                                      w2_v(p)[:, kc * w:(kc + 1) * w],
                                      start=(kc == 0), stop=(kc == NCC - 1))
                    kv_tg.append(inc(op, "t_m"))
                block.tensor(_)
                t_here = kv_tg[-1]

                if True:
                    def _(v, ci=ci, j=j, t_here=t_here):
                        wt(v, "t_m", t_here)
                        nn = ci
                        if nn < 8:
                            dst = ks_loc[:, j * 128:(j + 1) * 128]
                            msk = gmask
                            h0 = 2 * nn
                        else:
                            dst = vs_loc[:, j * 128:(j + 1) * 128]
                            msk = vmask
                            h0 = 2 * (nn - 8)
                        if h0 == 0:
                            v.tensor_scalar(dst, bank(j, 256)[:, 0:128],
                                            msk[:, j * 16: j * 16 + 1],
                                            None, op0=ALU.mult)
                        else:
                            v.scalar_tensor_tensor(
                                dst, bank(j, 256)[:, 0:128],
                                msk[:, j * 16 + h0: j * 16 + h0 + 1],
                                dst, op0=ALU.mult, op1=ALU.add)
                        v.drain()
                        op = v.scalar_tensor_tensor(
                            dst, bank(j, 256)[:, 128:256],
                            msk[:, j * 16 + h0 + 1: j * 16 + h0 + 2],
                            dst, op0=ALU.mult, op1=ALU.add)
                        kv_ev.append(inc(op, "v_p"))
                    block.vector(_)
        V_KVDONE = kv_ev[-1]
        T_KVALL = kv_tg[-1]

        ksT_tr = []
        ksT_ev = []
        for j in range(NJ):
            def _(t, j=j):
                if j == 0:
                    wt(t, "v_p", V_KVDONE)
                if j >= 4:
                    wt(t, "a_p", ksT_ev[j - 4])
                op = t.transpose(bank(6)[:, (j % 4) * 128:(j % 4 + 1) * 128],
                                 ks_loc[:, j * 128:(j + 1) * 128], ident[:])
                ksT_tr.append(inc(op, "t_m"))
            block.tensor(_)

            def _(s, j=j):
                wt(s, "t_m", ksT_tr[j])
                op = s.activation(ksT_loc[:, j * 128:(j + 1) * 128],
                                  bank(6)[:, (j % 4) * 128:(j % 4 + 1) * 128],
                                  AF.Copy, bias=0.0, scale=1.0)
                ksT_ev.append(inc(op, "a_p"))
            block.scalar(_)

        def _(g):
            wt(g, "a_p", ksT_ev[-1])
            op = g.dma_start(out=ag2i[:, 0:131072].rearrange(
                "o (p c) -> p (o c)", p=128), in_=ksT_loc[:])
            inc(op, "s_g")
            op = g.dma_start(out=ag2i[:, 131072:262144].rearrange(
                "o (p c) -> p (o c)", p=128), in_=vs_loc[:])
            inc(op, "s_g")
            g.wait_ge(sems["s_g"], cnt["s_g"])
            op = g.collective_compute("AllGather", ALU.bypass,
                                      replica_groups=RG,
                                      ins=[ag2i.opt()], outs=[ag2o.opt()])
            inc(op, "c_c")
            g.wait_ge(sems["c_c"], cnt["c_c"])
            wt(g, "t_m", T_KVALL)
            for r in range(4):
                op = g.dma_start(
                    out=ksF.rearrange("p (j q) -> p j q", j=32)[:, r::4, :],
                    in_=ag2o[r:r + 1, 0:131072].rearrange(
                        "o (p c) -> p (o c)", p=128).rearrange(
                        "p (j q) -> p j q", j=8))
                inc(op, "s_g")
                op = g.dma_start(
                    out=vsF.rearrange("p (j q) -> p j q", j=32)[:, r::4, :],
                    in_=ag2o[r:r + 1, 131072:262144].rearrange(
                        "o (p c) -> p (o c)", p=128).rearrange(
                        "p (j q) -> p j q", j=8))
                inc(op, "s_g")
        block.gpsimd(_)
        G_KSVS = cnt["s_g"]

        BLOCKS_PER_HEAD = 20
        wq_ld = []
        t_q_done = []
        y_last = [0] * NJ

        sblocks = []
        for hh in range(8):
            for j in range(NJ):
                G = j + 1
                nb = (G + 1) // 2
                for b in range(nb):
                    g0, g1 = 2 * b, min(2 * b + 2, G)
                    sblocks.append((hh, j, b, g0, g1, b == nb - 1))
        NSB = len(sblocks)

        def sb_par(st):
            return PS[:, 1024 + st * 1024: 2048 + st * 1024]

        def p32h(st):
            return P32[:, st * 1024:(st + 1) * 1024]

        def pth(st):
            return PT[:, st * 1024:(st + 1) * 1024]

        ybank = lambda st: PS[:, 3584 + st * 128: 3584 + (st + 1) * 128]

        MRUN = [mrun[:, 0:1], st1s[:, 0:1]]
        DRUN = [drun[:, 0:1], st1s[:, 1:2]]
        DBLK = [dblk[:, 0:1], st1s[:, 2:3]]
        ALPH = [alpha[:, 0:1], st1s[:, 3:4]]
        RDD = [rdd[:, 0:1], st1s[:, 4:5]]
        YSB = [ysb[:, :], ysb2[:, :]]
        MT = [mtmp[:, 0:2], mtmp[:, 2:4]]

        expd = [{}, {}]
        ptev = [{}, {}]
        tpt = [{}, {}]
        tpv = [{}, {}]
        vacc = [{}, {}]
        vmx = [{}, {}]
        tqk = [{}, {}]
        vq = [{}, {}]
        vq_last = [0]
        yread_last = [0]
        last_ptev = [0]
        last_banks6_user = [0]

        def emit_qmm(st, hh):
            h = st * 8 + hh
            def _(s, st=st, hh=hh, h=h):
                if not wq_ld:
                    wt(s, "t_m", T_KVALL)
                else:
                    wt(s, "t_m", t_q_done[-1])
                op = s.dma_start(
                    out=wq_v[:, :].rearrange("p (kc q) -> p kc q", kc=16),
                    in_=wqT.rearrange("(kc p) q -> p kc q", p=128)[
                        :, :, h * 128:(h + 1) * 128])
                wq_ld.append(inc(op, "s_q"))
            block.sync(_)
            for n in range(2):
                def _(t, st=st, hh=hh, n=n):
                    if n == 0:
                        wt(t, "s_q", wq_ld[-1])
                        if vq_last[0]:
                            wt(t, "v_p", vq_last[0])
                        if last_ptev[0]:
                            wt(t, "v_p", last_ptev[0])
                    for kc in range(NCC):
                        op = t.matmul(bank(n),
                                      wq_v[:, kc * 128:(kc + 1) * 128],
                                      x1t_cc(kc)[:, n * 512:(n + 1) * 512],
                                      start=(kc == 0), stop=(kc == NCC - 1))
                    inc(op, "t_m")
                block.tensor(_)
                t_qn = cnt["t_m"]

                def _(v, st=st, hh=hh, n=n, t_qn=t_qn):
                    wt(v, "t_m", t_qn)
                    if hh >= 1 and n == 0:
                        wt(v, "t_m", tqk[st][hh * BLOCKS_PER_HEAD - 1])
                    op = v.tensor_copy(qh_v(st)[:, n * 512:(n + 1) * 512],
                                       bank(n))
                    inc(op, "v_p")
                block.vector(_)
            t_q_done.append(cnt["t_m"])
            vq[st][hh] = cnt["v_p"]
            vq_last[0] = cnt["v_p"]

        def emit_qk(st, k):
            hh, j, b, g0, g1, last = sblocks[k]
            def _(t, st=st, k=k, hh=hh, j=j, g0=g0, g1=g1):
                if k == 0 and st == 0:
                    wt(t, "s_g", G_KSVS)
                wt(t, "v_p", vq[st][hh])
                if k >= 1:
                    wt(t, "a_p", expd[st][k - 1])
                for g in range(g0, g1):
                    op = t.matmul(sb_par(st)[:, (g - g0) * 512:(g - g0 + 1) * 512],
                                  qh_v(st)[:, j * 128:(j + 1) * 128],
                                  ksF[:, g * 512:(g + 1) * 512],
                                  start=True, stop=True)
                tqk[st][k] = inc(op, "t_m")
            block.tensor(_)

        def emit_maxbias(st, k):
            hh, j, b, g0, g1, last = sblocks[k]
            G = j + 1
            def _(v, st=st, k=k, g0=g0, g1=g1, b=b, G=G):
                wt(v, "t_m", tqk[st][k])
                if k >= 1:
                    wt(v, "a_p", expd[st][k - 1])
                ng = g1 - g0
                if g1 == G:
                    v.tensor_tensor(sb_par(st)[:, (ng - 1) * 512:ng * 512],
                                    sb_par(st)[:, (ng - 1) * 512:ng * 512],
                                    madd[:], op=ALU.add)
                    v.drain()
                v.reduce_max(mnew[:, st:st + 1], sb_par(st)[:, 0:ng * 512],
                             axis=AX.X)
                v.drain()
                if b > 0:
                    v.tensor_tensor(mnew[:, st:st + 1], mnew[:, st:st + 1],
                                    MRUN[st], op=ALU.max)
                    v.drain()
                op = v.tensor_scalar(sbias[:, st:st + 1], mnew[:, st:st + 1],
                                     -SS, None, op0=ALU.mult)
                vmx[st][k] = inc(op, "v_p")
            block.vector(_)

        def emit_exp(st, k):
            hh, j, b, g0, g1, last = sblocks[k]
            ng = g1 - g0
            def _(s, st=st, k=k, ng=ng, b=b):
                wt(s, "v_p", vmx[st][k])
                if k >= 1:
                    wt(s, "t_m", tpt[st][k - 1])
                op = s.activation(
                    p32h(st)[:, 0:ng * 512], sb_par(st)[:, 0:ng * 512],
                    AF.Exp, bias=sbias[:, st:st + 1], scale=SS,
                    accum_out=dtmp[:, st * 2: st * 2 + 1])
                if b > 0:
                    op = s.activation(ALPH[st], MRUN[st], AF.Exp,
                                      bias=sbias[:, st:st + 1], scale=SS)
                op = s.drain()
                expd[st][k] = inc(op, "a_p")
            block.scalar(_)

        def emit_ptpv(st, k):
            hh, j, b, g0, g1, last = sblocks[k]
            ng = g1 - g0
            ntile = ng * 4

            def _(t, st=st, k=k, ntile=ntile):
                wt(t, "a_p", expd[st][k])
                if last_ptev[0]:
                    wt(t, "v_p", last_ptev[0])
                if vq_last[0]:
                    wt(t, "v_p", vq_last[0])
                for tt in range(min(4, ntile)):
                    op = t.transpose(bank(6)[:, tt * 128:(tt + 1) * 128],
                                     p32h(st)[:, tt * 128:(tt + 1) * 128],
                                     ident[:])
                for tt in range(4, ntile):
                    op = t.transpose(bank(0)[:, (tt - 4) * 128:(tt - 3) * 128],
                                     p32h(st)[:, tt * 128:(tt + 1) * 128],
                                     ident[:])
                tpt[st][k] = inc(op, "t_m")
            block.tensor(_)

            def _(v, st=st, k=k, ntile=ntile):
                wt(v, "t_m", tpt[st][k])
                if k >= 1 and (k - 1) in tpv[st]:
                    wt(v, "t_m", tpv[st][k - 1])
                v.tensor_copy(pth(st)[:, 0:min(4, ntile) * 128],
                              bank(6)[:, 0:min(4, ntile) * 128])
                if ntile > 4:
                    op = v.tensor_copy(pth(st)[:, 512:ntile * 128],
                                       bank(0)[:, 0:(ntile - 4) * 128])
                else:
                    op = v.engine_nop()
                last_ptev[0] = inc(op, "v_p")
            block.vector(_)
            pt_ev_here = last_ptev[0]

            def _(t, st=st, k=k, ntile=ntile, b=b, pt_ev_here=pt_ev_here):
                wt(t, "v_p", pt_ev_here)
                if yread_last[0]:
                    wt(t, "v_p", yread_last[0])
                for tt in range(ntile):
                    kt = b * 8 + tt
                    op = t.matmul(ybank(st),
                                  pth(st)[:, tt * 128:(tt + 1) * 128],
                                  vsF[:, kt * 128:(kt + 1) * 128],
                                  start=(tt == 0), stop=(tt == ntile - 1))
                tpv[st][k] = inc(op, "t_m")
            block.tensor(_)

        def emit_accum(st, k):
            hh, j, b, g0, g1, last = sblocks[k]
            h = st * 8 + hh
            ng = g1 - g0
            if b > 0:
                def _(v, st=st, k=k, h=h, j=j, last=last):
                    wt(v, "t_m", tpv[st][k])
                    wt(v, "a_p", expd[st][k])
                    v.scalar_tensor_tensor(DRUN[st], DRUN[st], ALPH[st],
                                           dtmp[:, st * 2: st * 2 + 1],
                                           op0=ALU.mult, op1=ALU.add)
                    v.scalar_tensor_tensor(YSB[st], YSB[st], ALPH[st],
                                           ybank(st),
                                           op0=ALU.mult, op1=ALU.add)
                    v.drain()
                    v.tensor_copy(MRUN[st], mnew[:, st:st + 1])
                    v.drain()
                    if last:
                        v.reciprocal(RDD[st], DRUN[st])
                        v.drain()
                        v.scalar_tensor_tensor(
                            a_tile(j)[:, h * 128:(h + 1) * 128],
                            YSB[st], RDD[st],
                            a_tile(j)[:, h * 128:(h + 1) * 128],
                            op0=ALU.mult, op1=ALU.add)
                        v.drain()
                    op = v.engine_nop()
                    vacc[st][k] = inc(op, "v_p")
                    yread_last[0] = cnt["v_p"]
                    if last:
                        y_last[j] = cnt["v_p"]
                block.vector(_)
            else:
                def _(v, st=st, k=k, h=h, j=j, ng=ng, last=last):
                    wt(v, "t_m", tpv[st][k])
                    wt(v, "a_p", expd[st][k])
                    v.tensor_copy(DRUN[st], dtmp[:, st * 2: st * 2 + 1])
                    v.tensor_copy(MRUN[st], mnew[:, st:st + 1])
                    v.tensor_copy(YSB[st], ybank(st))
                    v.drain()
                    if last:
                        v.reciprocal(RDD[st], DRUN[st])
                        v.drain()
                        v.scalar_tensor_tensor(
                            a_tile(j)[:, h * 128:(h + 1) * 128],
                            YSB[st], RDD[st],
                            a_tile(j)[:, h * 128:(h + 1) * 128],
                            op0=ALU.mult, op1=ALU.add)
                        v.drain()
                    op = v.engine_nop()
                    vacc[st][k] = inc(op, "v_p")
                    yread_last[0] = cnt["v_p"]
                    if last:
                        y_last[j] = cnt["v_p"]
                block.vector(_)

        emit_qmm(0, 0)
        emit_qmm(1, 0)
        emit_qk(0, 0)
        emit_qk(1, 0)
        for k in range(NSB):
            emit_maxbias(0, k)
            emit_exp(0, k)
            emit_maxbias(1, k)
            emit_exp(1, k)
            if k + 1 < NSB:
                if (k + 1) % BLOCKS_PER_HEAD == 0:
                    emit_qmm(0, (k + 1) // BLOCKS_PER_HEAD)
                    emit_qmm(1, (k + 1) // BLOCKS_PER_HEAD)
                emit_qk(0, k + 1)
            emit_ptpv(0, k)
            emit_accum(0, k)
            if k + 1 < NSB:
                emit_qk(1, k + 1)
            emit_ptpv(1, k)
            emit_accum(1, k)
        t_attn_end = cnt["t_m"]

        out_dma = []
        for j in range(NJ):
            vj = y_last[j]

            def _(v, j=j, vj=vj):
                wt(v, "v_p", vj)
                emit_lnstats(v, a_tile(j))
                op = v.engine_nop(); inc(op, "v_p")
            block.vector(_)
            v_st = cnt["v_p"]

            def _(s, v_st=v_st):
                wt(s, "v_p", v_st)
                op = s.activation(rstd[:], varb[:], AF.Sqrt, bias=0.0, scale=1.0)
                inc(op, "a_p")
            block.scalar(_)
            a_rs = cnt["a_p"]

            def _(v, j=j, a_rs=a_rs):
                wt(v, "a_p", a_rs)
                src = a_tile(j)
                v.reciprocal(rstd[:], rstd[:])
                v.drain()
                v.tensor_scalar(nmu[:], bnag[:, 0:1], rstd[:], -1.0,
                                op0=ALU.mult, op1=ALU.mult)
                v.drain()
                op = v.tensor_scalar(src, src, rstd[:], nmu[:],
                                     op0=ALU.mult, op1=ALU.add)
                inc(op, "v_p")
            block.vector(_)
            v_out = cnt["v_p"]

            def _(s, j=j, v_out=v_out):
                wt(s, "v_p", v_out)
                op = s.dma_start(out=out[j * 128:(j + 1) * 128, :],
                                 in_=a_tile(j))
                out_dma.append(inc(op, "s_o"))
            block.scalar(_)

        def _(s):
            s.wait_ge(sems["s_o"], out_dma[-1])
        block.scalar(_)

    return nc


_NC_CACHE = [None]


def kernel(L0_attn_x, L0_attn_time_mix_k, L0_attn_time_mix_r, L0_attn_Wk,
           L0_attn_Wr, L0_attn_Wv, L0_attn_time_decay, L0_attn_ln_gamma,
           L0_attn_ln_beta, L0_ffn_W_gate, L0_ffn_W_qkv):
    x = np.ascontiguousarray(np.asarray(L0_attn_x, dtype=np.float32))
    B, T, _ = x.shape
    wkTh = np.ascontiguousarray(np.asarray(L0_attn_Wk, np.float32).T)
    wrTh = np.ascontiguousarray(np.asarray(L0_attn_Wr, np.float32).T)
    wvTh = np.ascontiguousarray(np.asarray(L0_attn_Wv, np.float32).T)
    wq = np.asarray(L0_ffn_W_qkv, np.float32)
    wqTh = np.ascontiguousarray(wq[0:D, :].T)
    wkvgTh = np.ascontiguousarray(
        np.concatenate([wq[D:3 * D, :].T,
                        np.asarray(L0_ffn_W_gate, np.float32).T], axis=1))
    tmkh = np.ascontiguousarray(
        np.asarray(L0_attn_time_mix_k, np.float32).reshape(NCC, 128).T)
    tmrh = np.ascontiguousarray(
        np.asarray(L0_attn_time_mix_r, np.float32).reshape(NCC, 128).T)
    tdh = np.ascontiguousarray(
        np.asarray(L0_attn_time_decay, np.float32).reshape(NHC, 128).T)
    gamh = np.ascontiguousarray(
        np.asarray(L0_attn_ln_gamma, np.float32).reshape(1, D))
    beth = np.ascontiguousarray(
        np.asarray(L0_attn_ln_beta, np.float32).reshape(1, D))

    if _NC_CACHE[0] is None:
        _NC_CACHE[0] = build_nc()
    nc = _NC_CACHE[0]

    in_maps = []
    toks = []
    for core in range(8):
        b, r = core // 4, core % 4
        tok = np.concatenate(
            [np.arange(128 * g, 128 * g + 128) for g in range(r, 32, 4)])
        sh = tok - 1
        sh[tok == 0] = 1
        xb = x[b]
        in_maps.append({
            "xT": np.ascontiguousarray(xb.T[:, tok]),
            "xTs": np.ascontiguousarray(xb.T[:, sh]),
            "xtok": np.ascontiguousarray(xb[tok]),
            "wkT": wkTh, "wrT": wrTh, "wvT": wvTh, "wqT": wqTh,
            "wkvgT": wkvgTh, "tmk": tmkh, "tmr": tmrh, "td": tdh,
            "gam": gamh, "bet": beth,
        })
        toks.append((b, tok))

    res = run_bass_kernel_spmd(nc, in_maps, core_ids=list(range(8)))
    full = np.empty((B, T, D), dtype=np.float32)
    for core in range(8):
        b, tok = toks[core]
        full[b][tok] = res.results[core]["out"]
    return full


# version 12
# speedup vs baseline: 1.0076x; 1.0076x over previous
import math
import contextlib
import numpy as np
import concourse.bass as bass
import concourse.mybir as mybir
from concourse.bass_utils import run_bass_kernel_spmd

F32 = mybir.dt.float32
I32 = mybir.dt.int32
U32 = mybir.dt.uint32
ALU = mybir.AluOpType
AF = mybir.ActivationFunctionType
AX = mybir.AxisListType

D = 2048
HH = 2816
NCC = 16
NHC = 22
NJ = 8
TL = 1024
NH = 16
RG = [[0, 1, 2, 3], [4, 5, 6, 7]]
SS = 1.0 / math.sqrt(128.0)
EPS = 1e-5
NEG = -1e30


def build_nc():
    nc = bass.Bass("TRN2", target_bir_lowering=False)
    xT = nc.dram_tensor("xT", [D, TL], F32, kind="ExternalInput").ap()
    xTs = nc.dram_tensor("xTs", [D, TL], F32, kind="ExternalInput").ap()
    xtok = nc.dram_tensor("xtok", [TL, D], F32, kind="ExternalInput").ap()
    wkT = nc.dram_tensor("wkT", [D, HH], F32, kind="ExternalInput").ap()
    wrT = nc.dram_tensor("wrT", [D, HH], F32, kind="ExternalInput").ap()
    wvT = nc.dram_tensor("wvT", [HH, D], F32, kind="ExternalInput").ap()
    wqT = nc.dram_tensor("wqT", [D, D], F32, kind="ExternalInput").ap()
    wkvgT = nc.dram_tensor("wkvgT", [D, 4112], F32, kind="ExternalInput").ap()
    tmkI = nc.dram_tensor("tmk", [128, NCC], F32, kind="ExternalInput").ap()
    tmrI = nc.dram_tensor("tmr", [128, NCC], F32, kind="ExternalInput").ap()
    tdI = nc.dram_tensor("td", [128, NHC], F32, kind="ExternalInput").ap()
    gamI = nc.dram_tensor("gam", [1, D], F32, kind="ExternalInput").ap()
    betI = nc.dram_tensor("bet", [1, D], F32, kind="ExternalInput").ap()
    out = nc.dram_tensor("out", [TL, D], F32, kind="ExternalOutput").ap()
    rkd = nc.dram_tensor("rkd", [1, 1], I32).ap()
    ag1i = nc.dram_tensor("ag1i", [1, 176 * 128], F32).ap()
    ag1o = nc.dram_tensor("ag1o", [4, 176 * 128], F32).ap()
    ag2i = nc.dram_tensor("ag2i", [1, 2048 * 128], F32).ap()
    ag2o = nc.dram_tensor("ag2o", [4, 2048 * 128], F32).ap()

    with contextlib.ExitStack() as ctx:
        En = ctx.enter_context
        A = En(nc.sbuf_tensor("A", [128, 16384], F32))
        U = En(nc.sbuf_tensor("U", [128, 22528], F32))
        P = En(nc.sbuf_tensor("P", [128, 10240], F32))
        tmk = En(nc.sbuf_tensor("tmkS", [128, NCC], F32))
        tmr = En(nc.sbuf_tensor("tmrS", [128, NCC], F32))
        tdv = En(nc.sbuf_tensor("tdv", [128, NHC], F32))
        lnw = En(nc.sbuf_tensor("lnw", [128, NHC], F32))
        wvec = En(nc.sbuf_tensor("wvec", [128, NHC], F32))
        w128 = En(nc.sbuf_tensor("w128", [128, NHC], F32))
        wpow = En(nc.sbuf_tensor("wpow", [128, 256], F32))
        iocol = En(nc.sbuf_tensor("iocol", [128, 128], F32))
        iorow = En(nc.sbuf_tensor("iorow", [128, 1], F32))
        ident = En(nc.sbuf_tensor("ident", [128, 128], F32))
        dadd = En(nc.sbuf_tensor("dadd", [128, 128], F32))
        madd = En(nc.sbuf_tensor("madd", [128, 512], F32))
        rkb = En(nc.sbuf_tensor("rkb", [128, 1], I32))
        rmeq = En(nc.sbuf_tensor("rmeq", [128, 4], F32))
        rmsc = En(nc.sbuf_tensor("rmsc", [128, 4], F32))
        stage = En(nc.sbuf_tensor("stage", [128, 1024], F32))
        gbuf = En(nc.sbuf_tensor("gbuf", [128, 16], F32))
        gm8 = En(nc.sbuf_tensor("gm8", [128, 8], F32))
        gidx = En(nc.sbuf_tensor("gidx", [128, 8], U32))
        gmask = En(nc.sbuf_tensor("gmask", [128, NJ * 16], F32))
        vmask = En(nc.sbuf_tensor("vmask", [128, NJ * 16], F32))
        gnb = En(nc.sbuf_tensor("gnb", [128, 1], F32))
        gsum = En(nc.sbuf_tensor("gsum", [128, 1], F32))
        top1 = En(nc.sbuf_tensor("top1", [128, NJ], F32))
        bnst = En(nc.sbuf_tensor("bnst", [128, 24], F32))
        bnag = En(nc.sbuf_tensor("bnag", [128, 2], F32))
        rstd = En(nc.sbuf_tensor("rstd", [128, 1], F32))
        nmu = En(nc.sbuf_tensor("nmu", [128, 1], F32))
        varb = En(nc.sbuf_tensor("varb", [128, 1], F32))
        mrun = En(nc.sbuf_tensor("mrun", [128, 1], F32))
        mnew = En(nc.sbuf_tensor("mnew", [128, 2], F32))
        mtmp = En(nc.sbuf_tensor("mtmp", [128, 4], F32))
        sbias = En(nc.sbuf_tensor("sbias", [128, 2], F32))
        alpha = En(nc.sbuf_tensor("alpha", [128, 1], F32))
        drun = En(nc.sbuf_tensor("drun", [128, 1], F32))
        dtmp = En(nc.sbuf_tensor("dtmp", [128, 4], F32))
        dblk = En(nc.sbuf_tensor("dblk", [128, 1], F32))
        rdd = En(nc.sbuf_tensor("rdd", [128, 1], F32))
        ysb = En(nc.sbuf_tensor("ysb", [128, 128], F32))
        ysb2 = En(nc.sbuf_tensor("ysb2", [128, 128], F32))
        st1s = En(nc.sbuf_tensor("st1s", [128, 8], F32))
        PS = En(nc.psum_tensor("PS", [128, 4096], F32))

        sems, cnt = {}, {}
        for name in ("s_x0", "s_x1", "s_w0", "s_w1", "s_t0", "s_t1",
                     "s_g", "s_o", "s_q",
                     "t_m", "v_p", "a_p", "g_p", "c_c"):
            sems[name] = En(nc.semaphore(name))
            cnt[name] = 0

        def inc(op, name, n=None):
            if n is None:
                n = 16 if name.startswith("s_") else 1
            cnt[name] += n
            op.then_inc(sems[name], n)
            return cnt[name]

        def wt(eng, name, v):
            if v > 0:
                eng.wait_ge(sems[name], v)

        block = En(nc.Block())

        def u_m(m):
            return U[:, m * 1024:(m + 1) * 1024]

        def xk_cc(n, cc):
            return A[:, n * 8192 + cc * 512: n * 8192 + (cc + 1) * 512]

        def a_tile(j):
            return A[:, j * 2048:(j + 1) * 2048]

        def x1t_cc(cc):
            return U[:, cc * 1024:(cc + 1) * 1024]

        qh_v = lambda p: U[:, 16384 + p * 1024: 16384 + (p + 1) * 1024]
        P32 = U[:, 18432:20480]
        PT = U[:, 20480:22528]
        ks_loc = U[:, 18432:19456]
        vs_loc = U[:, 19456:20480]
        ksT_loc = U[:, 20480:21504]
        xs_v = lambda p: P[:, p * 1024:(p + 1) * 1024]
        wsl_v = lambda p: P[:, 2048 + p * 2816: 2048 + (p + 1) * 2816]
        lend = P[:, 7680:7856]
        ends = P[:, 7856:8560]
        chz = P[:, 8560:9286]
        cary = P[:, 9286:9462]
        eraw = P[:, 9462:10166]
        gamBC = P[:, 0:2048]
        betBC = P[:, 2048:4096]
        xtok_v = lambda p: P[:, 4096 + p * 2048: 4096 + (p + 1) * 2048]
        w2_v = lambda p: P[:, p * 4096:(p + 1) * 4096]
        wq_v = P[:, 0:2048]
        ksF = P[:, 2048:6144]
        vsF = P[:, 6144:10240]

        def bank(k, w=512):
            return PS[:, k * 512: k * 512 + w]

        def _(s):
            op = s.dma_start(out=tmk[:], in_=tmkI); inc(op, "s_w0")
            op = s.dma_start(out=tmr[:], in_=tmrI); inc(op, "s_w0")
            op = s.dma_start(out=tdv[:], in_=tdI); inc(op, "s_w0")
        block.sync(_)
        W_TD = cnt["s_w0"]

        def _(g):
            op = g.iota(iocol[:], pattern=[[1, 128]], base=0,
                        channel_multiplier=0,
                        allow_small_or_imprecise_dtypes=True)
            inc(op, "g_p")
            op = g.iota(iorow[:], pattern=[[1, 1]], base=0,
                        channel_multiplier=1,
                        allow_small_or_imprecise_dtypes=True)
            inc(op, "g_p")
            rk = g.cc_rank(RG)
            op = g.store(rkb[0:1, 0:1], rk); inc(op, "g_p")
            g.wait_ge(sems["g_p"], cnt["g_p"])
            op = g.dma_start(out=rkd, in_=rkb[0:1, :]); inc(op, "s_g")
            g.wait_ge(sems["s_g"], cnt["s_g"])
            op = g.dma_start(out=rkb[:], in_=rkd.partition_broadcast(128))
            inc(op, "s_g")
        block.gpsimd(_)
        G_IOTA = 3
        G_RANKB = cnt["s_g"]

        def _(s):
            wt(s, "s_w0", W_TD)
            s.activation(lnw[:], tdv[:], AF.Exp, bias=0.0, scale=1.0)
            op = s.drain(); inc(op, "a_p")
        block.scalar(_)
        A_ETD = cnt["a_p"]

        def _(v):
            wt(v, "a_p", A_ETD)
            v.tensor_scalar(lnw[:], lnw[:], -1.0, None, op0=ALU.mult)
            v.drain()
            op = v.engine_nop(); inc(op, "v_p")
        block.vector(_)
        V_LNW = cnt["v_p"]

        def _(s):
            wt(s, "v_p", V_LNW)
            s.activation(wvec[:], lnw[:], AF.Exp, bias=0.0, scale=1.0)
            s.activation(w128[:], lnw[:], AF.Exp, bias=0.0, scale=128.0)
            op = s.drain(); inc(op, "a_p")
        block.scalar(_)
        A_WV = cnt["a_p"]

        def _(v):
            wt(v, "g_p", G_IOTA)
            v.tensor_scalar(ident[:], iocol[:], iorow[:], None, op0=ALU.is_equal)
            v.tensor_scalar(dadd[:], iocol[:], iorow[:], NEG,
                            op0=ALU.is_gt, op1=ALU.mult)
            wt(v, "s_g", G_RANKB)
            for rp in range(4):
                v.tensor_scalar(rmeq[:, rp:rp + 1], rkb[:], rp, None,
                                op0=ALU.is_equal)
                v.tensor_scalar(rmsc[:, rp:rp + 1], rkb[:], rp, NEG,
                                op0=ALU.is_lt, op1=ALU.mult)
            v.drain()
            for rp in range(4):
                v.tensor_scalar(madd[:, rp * 128:(rp + 1) * 128], dadd[:],
                                rmeq[:, rp:rp + 1], rmsc[:, rp:rp + 1],
                                op0=ALU.mult, op1=ALU.add)
            v.drain()
            op = v.engine_nop(); inc(op, "v_p")
        block.vector(_)
        V_CONST = cnt["v_p"]

        def emit_xbuild(tm, first_wait_tm, first_wait_v):
            built = []
            xs_ld = []
            units = [(n, cc) for n in range(2) for cc in range(NCC)]
            for i, (n, cc) in enumerate(units):
                p = i % 2

                def _(s, n=n, cc=cc, p=p, i=i):
                    if i < 2 and first_wait_v:
                        wt(s, "v_p", first_wait_v)
                    if i >= 2:
                        wt(s, "v_p", built[i - 2])
                    op = s.dma_start(out=xs_v(p)[:, 0:512],
                                     in_=xT[cc * 128:(cc + 1) * 128,
                                            n * 512:(n + 1) * 512])
                    inc(op, "s_x%d" % p)
                    op = s.dma_start(out=xs_v(p)[:, 512:1024],
                                     in_=xTs[cc * 128:(cc + 1) * 128,
                                             n * 512:(n + 1) * 512])
                    xs_ld.append(inc(op, "s_x%d" % p))
                block.sync(_)

                def _(v, n=n, cc=cc, p=p, i=i):
                    wt(v, "s_x%d" % p, xs_ld[i])
                    if i == 0:
                        wt(v, "s_w0", W_TD)
                        if first_wait_tm:
                            wt(v, "t_m", first_wait_tm)
                    v.tensor_tensor(xk_cc(n, cc), xs_v(p)[:, 512:1024],
                                    xs_v(p)[:, 0:512], op=ALU.subtract)
                    v.drain()
                    op = v.scalar_tensor_tensor(xk_cc(n, cc), xk_cc(n, cc),
                                                tm[:, cc:cc + 1],
                                                xs_v(p)[:, 0:512],
                                                op0=ALU.mult, op1=ALU.add)
                    built.append(inc(op, "v_p"))
                block.vector(_)
            return built[-1]

        def emit_proj(wsrc, nm, nk, rhs_fn, evac_emit, evac_list,
                      ready_v=None, ready_extra=None, slab_cols=128,
                      init_tm=0):
            t_groups = []
            slab_ld = []
            for m in range(nm):
                p = m % 2

                def _(s, m=m, p=p):
                    if m < 2:
                        wt(s, "t_m", init_tm)
                    if m >= 2:
                        wt(s, "t_m", t_groups[(m - 2) * 2 + 1])
                    op = s.dma_start(
                        out=wsl_v(p)[:, 0:nk * 128].rearrange(
                            "p (kc q) -> p kc q", kc=nk),
                        in_=wsrc.rearrange("(kc p) q -> p kc q", p=128)[
                            :, :, m * slab_cols:(m + 1) * slab_cols])
                    slab_ld.append(inc(op, "s_w%d" % p))
                block.sync(_)

                for n in range(2):
                    gi = m * 2 + n

                    def _(t, m=m, n=n, p=p, gi=gi):
                        if n == 0:
                            wt(t, "s_w%d" % p, slab_ld[m])
                        if gi == 0:
                            if ready_v is not None:
                                wt(t, "v_p", ready_v)
                            if ready_extra is not None:
                                ready_extra(t)
                        if gi >= 2:
                            kind, val = evac_list[gi - 2]
                            wt(t, kind, val)
                        for kc in range(nk):
                            op = t.matmul(bank(gi % 2),
                                          wsl_v(p)[:, kc * 128:(kc + 1) * 128],
                                          rhs_fn(kc, n),
                                          start=(kc == 0), stop=(kc == nk - 1))
                        t_groups.append(inc(op, "t_m"))
                    block.tensor(_)
                    evac_emit(m, n, gi, t_groups)
            return t_groups

        xk_ready = emit_xbuild(tmk, 0, 0)

        k_ev = []

        def k_evac(m, n, gi, t_groups):
            def _(s, m=m, n=n, gi=gi):
                wt(s, "t_m", t_groups[gi])
                op = s.activation(u_m(m)[:, n * 512:(n + 1) * 512],
                                  bank(gi % 2), AF.Relu, bias=0.0, scale=1.0)
                k_ev.append(("a_p", inc(op, "a_p")))
            block.scalar(_)

        tg_k = emit_proj(wkT, NHC, NCC, lambda kc, n: xk_cc(n, kc),
                         k_evac, k_ev, ready_v=xk_ready)
        T_KALL = tg_k[-1]

        scan_done = []
        for m in range(NHC):
            def _(v, m=m):
                wt(v, "a_p", k_ev[m * 2 + 1][1])
                for j in range(NJ):
                    v.tensor_tensor_scan(
                        u_m(m)[:, j * 128:(j + 1) * 128],
                        wvec[:, m:m + 1].broadcast_to([128, 128]),
                        u_m(m)[:, j * 128:(j + 1) * 128], 0.0,
                        op0=ALU.mult, op1=ALU.add)
                v.drain()
                op = v.tensor_copy(lend[:, m * 8:(m + 1) * 8],
                                   u_m(m)[:, 127::128])
                scan_done.append(inc(op, "v_p"))
            block.vector(_)
        V_SCANALL = scan_done[-1]

        def _(g):
            wt(g, "v_p", V_SCANALL)
            op = g.dma_start(out=ag1i.rearrange("o (p c) -> p (o c)", p=128),
                             in_=lend[:])
            inc(op, "s_g")
            g.wait_ge(sems["s_g"], cnt["s_g"])
            op = g.collective_compute("AllGather", ALU.bypass,
                                      replica_groups=RG,
                                      ins=[ag1i.opt()], outs=[ag1o.opt()])
            inc(op, "c_c")
            g.wait_ge(sems["c_c"], cnt["c_c"])
            for r in range(4):
                op = g.dma_start(
                    out=eraw[:, r * 176:(r + 1) * 176],
                    in_=ag1o[r:r + 1, :].rearrange(
                        "o (p c) -> p (o c)", p=128))
                inc(op, "s_g")
        block.gpsimd(_)
        G_ENDS = cnt["s_g"]

        xr_ready = emit_xbuild(tmr, T_KALL, xk_ready)

        def _(v):
            wt(v, "s_g", G_ENDS)
            wt(v, "a_p", A_WV)
            for r in range(4):
                v.tensor_copy(
                    ends[:].rearrange("p (m j) -> p m j", m=NHC)[:, :, r::4],
                    eraw[:, r * 176:(r + 1) * 176].rearrange(
                        "p (m j) -> p m j", m=NHC))
            v.memset(chz[:], 0.0)
            v.drain()
            for m in range(NHC):
                v.tensor_tensor_scan(
                    chz[:, m * 33 + 1: m * 33 + 33],
                    w128[:, m:m + 1].broadcast_to([128, 32]),
                    ends[:, m * 32:(m + 1) * 32], 0.0,
                    op0=ALU.mult, op1=ALU.add)
            v.drain()
            for m in range(NHC):
                v.tensor_scalar(cary[:, m * 8:(m + 1) * 8],
                                chz[:, m * 33: m * 33 + 29: 4],
                                rmeq[:, 0:1], None, op0=ALU.mult)
            v.drain()
            for rp in range(1, 4):
                for m in range(NHC):
                    v.scalar_tensor_tensor(
                        cary[:, m * 8:(m + 1) * 8],
                        chz[:, m * 33 + rp: m * 33 + rp + 29: 4],
                        rmeq[:, rp:rp + 1],
                        cary[:, m * 8:(m + 1) * 8],
                        op0=ALU.mult, op1=ALU.add)
                v.drain()
            op = v.engine_nop(); inc(op, "v_p")
        block.vector(_)
        V_CARY = cnt["v_p"]

        wpow_done = []
        corr_done = []
        for m in range(NHC):
            def _(s, m=m):
                if m == 0:
                    wt(s, "a_p", A_WV)
                    wt(s, "g_p", G_IOTA)
                if m >= 2:
                    wt(s, "v_p", corr_done[m - 2])
                op = s.activation(wpow[:, (m % 2) * 128:(m % 2 + 1) * 128],
                                  iocol[:], AF.Exp, bias=lnw[:, m:m + 1],
                                  scale=lnw[:, m:m + 1])
                wpow_done.append(inc(op, "a_p"))
            block.scalar(_)

            def _(v, m=m):
                if m == 0:
                    wt(v, "v_p", V_CARY)
                wt(v, "a_p", wpow_done[m])
                for j in range(NJ):
                    v.scalar_tensor_tensor(
                        u_m(m)[:, j * 128:(j + 1) * 128],
                        wpow[:, (m % 2) * 128:(m % 2 + 1) * 128],
                        cary[:, m * 8 + j: m * 8 + j + 1],
                        u_m(m)[:, j * 128:(j + 1) * 128],
                        op0=ALU.mult, op1=ALU.add)
                v.drain()
                op = v.engine_nop()
                corr_done.append(inc(op, "v_p"))
            block.vector(_)


        r_ev = []

        def r_evac(m, n, gi, t_groups):
            def _(v, m=m, n=n, gi=gi):
                wt(v, "t_m", t_groups[gi])
                wt(v, "v_p", corr_done[m])
                v.drain()
                op = v.tensor_tensor(u_m(m)[:, n * 512:(n + 1) * 512],
                                     bank(gi % 2),
                                     u_m(m)[:, n * 512:(n + 1) * 512],
                                     op=ALU.mult)
                r_ev.append(("v_p", inc(op, "v_p")))
            block.vector(_)

        tg_r = emit_proj(wrT, NHC, NCC, lambda kc, n: xk_cc(n, kc),
                         r_evac, r_ev, ready_v=xr_ready, init_tm=T_KALL)
        T_RALL = tg_r[-1]
        V_RLAST = r_ev[-1][1]

        wv_ev = []
        wv_tr = []
        wv_at = []

        def wv_evac(m2, n, gi, t_groups):
            def _(v, gi=gi):
                wt(v, "t_m", t_groups[gi])
                if gi >= 2:
                    wt(v, "t_m", wv_tr[gi - 2])
                op = v.tensor_copy(stage[:, (gi % 2) * 512:(gi % 2 + 1) * 512],
                                   bank(gi % 2))
                wv_ev.append(("v_p", inc(op, "v_p")))
            block.vector(_)

            def _(t, m2=m2, n=n, gi=gi):
                wt(t, "v_p", wv_ev[gi][1])
                if gi == 0:
                    wt(t, "v_p", V_CONST)
                if gi >= 1:
                    wt(t, "a_p", wv_at[gi - 1])
                for jj in range(4):
                    op = t.transpose(bank(6)[:, jj * 128:(jj + 1) * 128],
                                     stage[:, (gi % 2) * 512 + jj * 128:
                                           (gi % 2) * 512 + (jj + 1) * 128],
                                     ident[:])
                wv_tr.append(inc(op, "t_m"))
            block.tensor(_)

            def _(s, m2=m2, n=n, gi=gi):
                wt(s, "t_m", wv_tr[gi])
                op = s.activation(
                    A[:, 0:16384].rearrange("p (j c) -> p j c", j=8)[
                        :, 4 * n: 4 * n + 4, m2 * 128:(m2 + 1) * 128],
                    bank(6).rearrange("p (jj c) -> p jj c", jj=4),
                    AF.Copy, bias=0.0, scale=1.0)
                wv_at.append(inc(op, "a_p"))
            block.scalar(_)

        def wv_first_wait(t):
            wt(t, "v_p", V_RLAST)

        tg_wv = emit_proj(wvT, NCC, NHC,
                          lambda hc, n: u_m(hc)[:, n * 512:(n + 1) * 512],
                          wv_evac, wv_ev, ready_extra=wv_first_wait,
                          init_tm=T_RALL)
        T_WVALL = tg_wv[-1]
        A_ATTNTOK = wv_at[-1]

        def _(g):
            wt(g, "v_p", xr_ready)
            wt(g, "t_m", T_WVALL)
            op = g.dma_start(out=gamBC, in_=gamI.partition_broadcast(128))
            inc(op, "s_g")
            op = g.dma_start(out=betBC, in_=betI.partition_broadcast(128))
            inc(op, "s_g")
        block.gpsimd(_)
        G_GB = cnt["s_g"]

        def emit_lnstats(v, src):
            for c4 in range(4):
                v.bn_stats(bnst[:, c4 * 6:(c4 + 1) * 6],
                           src[:, c4 * 512:(c4 + 1) * 512])
            v.drain()
            v.bn_aggr(bnag[:], bnst[:])
            v.drain()
            v.tensor_scalar(varb[:], bnag[:, 1:2], 1.0, EPS,
                            op0=ALU.mult, op1=ALU.add)
            v.drain()

        xtok_ld = []
        ln_done = []
        x1t_done = []
        for j in range(NJ):
            def _(g, j=j):
                if j == 0:
                    wt(g, "t_m", T_WVALL)
                if j >= 2:
                    wt(g, "v_p", ln_done[j - 2])
                op = g.dma_start(out=xtok_v(j % 2),
                                 in_=xtok[j * 128:(j + 1) * 128, :])
                xtok_ld.append(inc(op, "s_t%d" % (j % 2)))
            block.gpsimd(_)

            def _(v, j=j):
                wt(v, "a_p", A_ATTNTOK)
                emit_lnstats(v, a_tile(j))
                op = v.engine_nop(); inc(op, "v_p")
            block.vector(_)
            v_st1 = cnt["v_p"]

            def _(s, j=j, v_st1=v_st1):
                wt(s, "v_p", v_st1)
                op = s.activation(rstd[:], varb[:], AF.Sqrt, bias=0.0, scale=1.0)
                inc(op, "a_p")
            block.scalar(_)
            a_rs1 = cnt["a_p"]

            def _(v, j=j, a_rs1=a_rs1):
                wt(v, "a_p", a_rs1)
                src = a_tile(j)
                v.reciprocal(rstd[:], rstd[:])
                v.drain()
                v.tensor_scalar(nmu[:], bnag[:, 0:1], rstd[:], -1.0,
                                op0=ALU.mult, op1=ALU.mult)
                v.drain()
                v.tensor_scalar(src, src, rstd[:], nmu[:],
                                op0=ALU.mult, op1=ALU.add)
                v.drain()
                wt(v, "s_g", G_GB)
                v.tensor_tensor(src, src, gamBC, op=ALU.mult)
                v.drain()
                v.tensor_tensor(src, src, betBC, op=ALU.add)
                v.drain()
                wt(v, "s_t%d" % (j % 2), xtok_ld[j])
                v.tensor_tensor(src, src, xtok_v(j % 2), op=ALU.add)
                v.drain()
                emit_lnstats(v, src)
                op = v.engine_nop(); inc(op, "v_p")
            block.vector(_)
            v_st2 = cnt["v_p"]

            def _(s, j=j, v_st2=v_st2):
                wt(s, "v_p", v_st2)
                op = s.activation(rstd[:], varb[:], AF.Sqrt, bias=0.0, scale=1.0)
                inc(op, "a_p")
            block.scalar(_)
            a_rs2 = cnt["a_p"]

            def _(v, j=j, a_rs2=a_rs2):
                wt(v, "a_p", a_rs2)
                src = a_tile(j)
                v.reciprocal(rstd[:], rstd[:])
                v.drain()
                v.tensor_scalar(nmu[:], bnag[:, 0:1], rstd[:], -1.0,
                                op0=ALU.mult, op1=ALU.mult)
                v.drain()
                op = v.tensor_scalar(src, src, rstd[:], nmu[:],
                                     op0=ALU.mult, op1=ALU.add)
                ln_done.append(inc(op, "v_p"))
            block.vector(_)

            for q4 in range(4):
                def _(t, j=j, q4=q4):
                    if q4 == 0:
                        wt(t, "v_p", ln_done[j])
                    if x1t_done:
                        wt(t, "a_p", x1t_done[-1])
                    for mm in range(4):
                        m2 = q4 * 4 + mm
                        op = t.transpose(bank(6)[:, mm * 128:(mm + 1) * 128],
                                         a_tile(j)[:, m2 * 128:(m2 + 1) * 128],
                                         ident[:])
                    inc(op, "t_m")
                block.tensor(_)
                t_x1 = cnt["t_m"]

                def _(s, j=j, q4=q4, t_x1=t_x1):
                    wt(s, "t_m", t_x1)
                    op = s.activation(
                        U[:, 0:16384].rearrange("p (cc c) -> p cc c", cc=16)[
                            :, q4 * 4: q4 * 4 + 4, j * 128:(j + 1) * 128],
                        bank(6).rearrange("p (mm c) -> p mm c", mm=4),
                        AF.Copy, bias=0.0, scale=1.0)
                    x1t_done.append(inc(op, "a_p"))
                block.scalar(_)
        A_X1T = x1t_done[-1]
        V_LNALL = ln_done[-1]

        kv_chunks = [(4096, 16)] + [(i * 256, 256) for i in range(16)]
        kv_slab_ld = []
        kv_tg = []
        kv_ev = []
        for ci, (c0, w) in enumerate(kv_chunks):
            p = ci % 2

            def _(s, ci=ci, c0=c0, w=w, p=p):
                if ci < 2:
                    wt(s, "v_p", V_LNALL)
                if ci >= 2:
                    wt(s, "t_m", kv_tg[(ci - 2) * NJ + NJ - 1])
                op = s.dma_start(
                    out=w2_v(p)[:, 0:16 * w].rearrange(
                        "p (kc q) -> p kc q", kc=16),
                    in_=wkvgT.rearrange("(kc p) q -> p kc q", p=128)[
                        :, :, c0:c0 + w])
                kv_slab_ld.append(inc(op, "s_w%d" % p))
            block.sync(_)

            for j in range(NJ):
                def _(t, ci=ci, j=j, p=p, w=w):
                    if j == 0:
                        wt(t, "s_w%d" % p, kv_slab_ld[ci])
                        if ci == 0:
                            wt(t, "a_p", A_X1T)
                    if ci >= 1:
                        wt(t, "v_p", kv_ev[(ci - 1) * NJ + j])
                    for kc in range(NCC):
                        op = t.matmul(bank(j, w),
                                      x1t_cc(kc)[:, j * 128:(j + 1) * 128],
                                      w2_v(p)[:, kc * w:(kc + 1) * w],
                                      start=(kc == 0), stop=(kc == NCC - 1))
                    kv_tg.append(inc(op, "t_m"))
                block.tensor(_)
                t_here = kv_tg[-1]

                if ci == 0:
                    def _(v, j=j, t_here=t_here):
                        wt(v, "t_m", t_here)
                        v.tensor_copy(gbuf[:], bank(j, 16))
                        v.drain()
                        v.max(gm8[:], gbuf[:])
                        v.drain()
                        v.max_index(gidx[:], gm8[:], gbuf[:])
                        v.tensor_scalar(gnb[:], gm8[:, 0:1], -1.0, None,
                                        op0=ALU.mult)
                        v.drain()
                        op = v.engine_nop(); inc(op, "v_p")
                    block.vector(_)
                    v_g = cnt["v_p"]

                    def _(s, j=j, v_g=v_g):
                        wt(s, "v_p", v_g)
                        op = s.activation(gbuf[:], gbuf[:], AF.Exp,
                                          bias=gnb[:], scale=1.0,
                                          accum_out=gsum[:])
                        inc(op, "a_p")
                    block.scalar(_)
                    a_g = cnt["a_p"]

                    def _(v, j=j, a_g=a_g):
                        wt(v, "a_p", a_g)
                        v.reciprocal(top1[:, j:j + 1], gsum[:])
                        for h in range(16):
                            v.tensor_scalar(gmask[:, j * 16 + h: j * 16 + h + 1],
                                            gidx[:, 0:1], h, None,
                                            op0=ALU.is_equal)
                        v.drain()
                        op = v.tensor_scalar(vmask[:, j * 16:(j + 1) * 16],
                                             gmask[:, j * 16:(j + 1) * 16],
                                             top1[:, j:j + 1], None,
                                             op0=ALU.mult)
                        kv_ev.append(inc(op, "v_p"))
                    block.vector(_)
                else:
                    def _(v, ci=ci, j=j, t_here=t_here):
                        wt(v, "t_m", t_here)
                        nn = ci - 1
                        if nn < 8:
                            dst = ks_loc[:, j * 128:(j + 1) * 128]
                            msk = gmask
                            h0 = 2 * nn
                        else:
                            dst = vs_loc[:, j * 128:(j + 1) * 128]
                            msk = vmask
                            h0 = 2 * (nn - 8)
                        if h0 == 0:
                            v.tensor_scalar(dst, bank(j, 256)[:, 0:128],
                                            msk[:, j * 16: j * 16 + 1],
                                            None, op0=ALU.mult)
                        else:
                            v.scalar_tensor_tensor(
                                dst, bank(j, 256)[:, 0:128],
                                msk[:, j * 16 + h0: j * 16 + h0 + 1],
                                dst, op0=ALU.mult, op1=ALU.add)
                        v.drain()
                        op = v.scalar_tensor_tensor(
                            dst, bank(j, 256)[:, 128:256],
                            msk[:, j * 16 + h0 + 1: j * 16 + h0 + 2],
                            dst, op0=ALU.mult, op1=ALU.add)
                        kv_ev.append(inc(op, "v_p"))
                    block.vector(_)
        V_KVDONE = kv_ev[-1]
        T_KVALL = kv_tg[-1]

        ksT_tr = []
        ksT_ev = []
        for j in range(NJ):
            def _(t, j=j):
                if j == 0:
                    wt(t, "v_p", V_KVDONE)
                if j >= 4:
                    wt(t, "a_p", ksT_ev[j - 4])
                op = t.transpose(bank(6)[:, (j % 4) * 128:(j % 4 + 1) * 128],
                                 ks_loc[:, j * 128:(j + 1) * 128], ident[:])
                ksT_tr.append(inc(op, "t_m"))
            block.tensor(_)

            def _(s, j=j):
                wt(s, "t_m", ksT_tr[j])
                op = s.activation(ksT_loc[:, j * 128:(j + 1) * 128],
                                  bank(6)[:, (j % 4) * 128:(j % 4 + 1) * 128],
                                  AF.Copy, bias=0.0, scale=1.0)
                ksT_ev.append(inc(op, "a_p"))
            block.scalar(_)

        def _(g):
            wt(g, "a_p", ksT_ev[-1])
            op = g.dma_start(out=ag2i[:, 0:131072].rearrange(
                "o (p c) -> p (o c)", p=128), in_=ksT_loc[:])
            inc(op, "s_g")
            op = g.dma_start(out=ag2i[:, 131072:262144].rearrange(
                "o (p c) -> p (o c)", p=128), in_=vs_loc[:])
            inc(op, "s_g")
            g.wait_ge(sems["s_g"], cnt["s_g"])
            op = g.collective_compute("AllGather", ALU.bypass,
                                      replica_groups=RG,
                                      ins=[ag2i.opt()], outs=[ag2o.opt()])
            inc(op, "c_c")
            g.wait_ge(sems["c_c"], cnt["c_c"])
            wt(g, "t_m", T_KVALL)
            for r in range(4):
                op = g.dma_start(
                    out=ksF.rearrange("p (j q) -> p j q", j=32)[:, r::4, :],
                    in_=ag2o[r:r + 1, 0:131072].rearrange(
                        "o (p c) -> p (o c)", p=128).rearrange(
                        "p (j q) -> p j q", j=8))
                inc(op, "s_g")
                op = g.dma_start(
                    out=vsF.rearrange("p (j q) -> p j q", j=32)[:, r::4, :],
                    in_=ag2o[r:r + 1, 131072:262144].rearrange(
                        "o (p c) -> p (o c)", p=128).rearrange(
                        "p (j q) -> p j q", j=8))
                inc(op, "s_g")
        block.gpsimd(_)
        G_KSVS = cnt["s_g"]

        BLOCKS_PER_HEAD = 20
        wq_ld = []
        t_q_done = []
        y_last = [0] * NJ

        sblocks = []
        for hh in range(8):
            for j in range(NJ):
                G = j + 1
                nb = (G + 1) // 2
                for b in range(nb):
                    g0, g1 = 2 * b, min(2 * b + 2, G)
                    sblocks.append((hh, j, b, g0, g1, b == nb - 1))
        NSB = len(sblocks)

        def sb_par(st):
            return PS[:, 1024 + st * 1024: 2048 + st * 1024]

        def p32h(st):
            return P32[:, st * 1024:(st + 1) * 1024]

        def pth(st):
            return PT[:, st * 1024:(st + 1) * 1024]

        ybank = lambda st: PS[:, 3584 + st * 128: 3584 + (st + 1) * 128]

        MRUN = [mrun[:, 0:1], st1s[:, 0:1]]
        DRUN = [drun[:, 0:1], st1s[:, 1:2]]
        DBLK = [dblk[:, 0:1], st1s[:, 2:3]]
        ALPH = [alpha[:, 0:1], st1s[:, 3:4]]
        RDD = [rdd[:, 0:1], st1s[:, 4:5]]
        YSB = [ysb[:, :], ysb2[:, :]]
        MT = [mtmp[:, 0:2], mtmp[:, 2:4]]

        expd = [{}, {}]
        ptev = [{}, {}]
        tpt = [{}, {}]
        tpv = [{}, {}]
        vacc = [{}, {}]
        vmx = [{}, {}]
        tqk = [{}, {}]
        vq = [{}, {}]
        vq_last = [0]
        yread_last = [0]
        last_ptev = [0]
        last_banks6_user = [0]

        def emit_qmm(st, hh):
            h = st * 8 + hh
            def _(s, st=st, hh=hh, h=h):
                if not wq_ld:
                    wt(s, "t_m", T_KVALL)
                else:
                    wt(s, "t_m", t_q_done[-1])
                op = s.dma_start(
                    out=wq_v[:, :].rearrange("p (kc q) -> p kc q", kc=16),
                    in_=wqT.rearrange("(kc p) q -> p kc q", p=128)[
                        :, :, h * 128:(h + 1) * 128])
                wq_ld.append(inc(op, "s_q"))
            block.sync(_)
            for n in range(2):
                def _(t, st=st, hh=hh, n=n):
                    if n == 0:
                        wt(t, "s_q", wq_ld[-1])
                        if vq_last[0]:
                            wt(t, "v_p", vq_last[0])
                        if last_ptev[0]:
                            wt(t, "v_p", last_ptev[0])
                    for kc in range(NCC):
                        op = t.matmul(bank(n),
                                      wq_v[:, kc * 128:(kc + 1) * 128],
                                      x1t_cc(kc)[:, n * 512:(n + 1) * 512],
                                      start=(kc == 0), stop=(kc == NCC - 1))
                    inc(op, "t_m")
                block.tensor(_)
                t_qn = cnt["t_m"]

                def _(v, st=st, hh=hh, n=n, t_qn=t_qn):
                    wt(v, "t_m", t_qn)
                    if hh >= 1 and n == 0:
                        wt(v, "t_m", tqk[st][hh * BLOCKS_PER_HEAD - 1])
                    op = v.tensor_copy(qh_v(st)[:, n * 512:(n + 1) * 512],
                                       bank(n))
                    inc(op, "v_p")
                block.vector(_)
            t_q_done.append(cnt["t_m"])
            vq[st][hh] = cnt["v_p"]
            vq_last[0] = cnt["v_p"]

        def emit_qk(st, k):
            hh, j, b, g0, g1, last = sblocks[k]
            def _(t, st=st, k=k, hh=hh, j=j, g0=g0, g1=g1):
                if k == 0 and st == 0:
                    wt(t, "s_g", G_KSVS)
                wt(t, "v_p", vq[st][hh])
                if k >= 1:
                    wt(t, "a_p", expd[st][k - 1])
                for g in range(g0, g1):
                    op = t.matmul(sb_par(st)[:, (g - g0) * 512:(g - g0 + 1) * 512],
                                  qh_v(st)[:, j * 128:(j + 1) * 128],
                                  ksF[:, g * 512:(g + 1) * 512],
                                  start=True, stop=True)
                tqk[st][k] = inc(op, "t_m")
            block.tensor(_)

        def emit_maxbias(st, k):
            hh, j, b, g0, g1, last = sblocks[k]
            G = j + 1
            def _(v, st=st, k=k, g0=g0, g1=g1, b=b, G=G):
                wt(v, "t_m", tqk[st][k])
                if k >= 1:
                    wt(v, "a_p", expd[st][k - 1])
                ng = g1 - g0
                if g1 == G:
                    v.tensor_tensor(sb_par(st)[:, (ng - 1) * 512:ng * 512],
                                    sb_par(st)[:, (ng - 1) * 512:ng * 512],
                                    madd[:], op=ALU.add)
                    v.drain()
                v.reduce_max(mnew[:, st:st + 1], sb_par(st)[:, 0:ng * 512],
                             axis=AX.X)
                v.drain()
                if b > 0:
                    v.tensor_tensor(mnew[:, st:st + 1], mnew[:, st:st + 1],
                                    MRUN[st], op=ALU.max)
                    v.drain()
                op = v.tensor_scalar(sbias[:, st:st + 1], mnew[:, st:st + 1],
                                     -SS, None, op0=ALU.mult)
                vmx[st][k] = inc(op, "v_p")
            block.vector(_)

        def emit_exp_pair(k):
            hh, j, b, g0, g1, last = sblocks[k]
            ng = g1 - g0
            def _(s, k=k, ng=ng, b=b):
                for st in range(2):
                    wt(s, "v_p", vmx[st][k])
                    if k >= 1:
                        wt(s, "t_m", tpt[st][k - 1])
                    s.activation(
                        p32h(st)[:, 0:ng * 512], sb_par(st)[:, 0:ng * 512],
                        AF.Exp, bias=sbias[:, st:st + 1], scale=SS,
                        accum_out=dtmp[:, st * 2: st * 2 + 1])
                    if b > 0:
                        s.activation(ALPH[st], MRUN[st], AF.Exp,
                                     bias=sbias[:, st:st + 1], scale=SS)
                    op = s.drain()
                    expd[st][k] = inc(op, "a_p")
            block.scalar(_)

        def emit_ptpv(st, k):
            hh, j, b, g0, g1, last = sblocks[k]
            ng = g1 - g0
            ntile = ng * 4

            def _(t, st=st, k=k, ntile=ntile):
                wt(t, "a_p", expd[st][k])
                if last_ptev[0]:
                    wt(t, "v_p", last_ptev[0])
                if vq_last[0]:
                    wt(t, "v_p", vq_last[0])
                for tt in range(min(4, ntile)):
                    op = t.transpose(bank(6)[:, tt * 128:(tt + 1) * 128],
                                     p32h(st)[:, tt * 128:(tt + 1) * 128],
                                     ident[:])
                for tt in range(4, ntile):
                    op = t.transpose(bank(0)[:, (tt - 4) * 128:(tt - 3) * 128],
                                     p32h(st)[:, tt * 128:(tt + 1) * 128],
                                     ident[:])
                tpt[st][k] = inc(op, "t_m")
            block.tensor(_)

            def _(v, st=st, k=k, ntile=ntile):
                wt(v, "t_m", tpt[st][k])
                if k >= 1 and (k - 1) in tpv[st]:
                    wt(v, "t_m", tpv[st][k - 1])
                v.tensor_copy(pth(st)[:, 0:min(4, ntile) * 128],
                              bank(6)[:, 0:min(4, ntile) * 128])
                if ntile > 4:
                    op = v.tensor_copy(pth(st)[:, 512:ntile * 128],
                                       bank(0)[:, 0:(ntile - 4) * 128])
                else:
                    op = v.engine_nop()
                last_ptev[0] = inc(op, "v_p")
            block.vector(_)
            pt_ev_here = last_ptev[0]

            def _(t, st=st, k=k, ntile=ntile, b=b, pt_ev_here=pt_ev_here):
                wt(t, "v_p", pt_ev_here)
                if yread_last[0]:
                    wt(t, "v_p", yread_last[0])
                for tt in range(ntile):
                    kt = b * 8 + tt
                    op = t.matmul(ybank(st),
                                  pth(st)[:, tt * 128:(tt + 1) * 128],
                                  vsF[:, kt * 128:(kt + 1) * 128],
                                  start=(tt == 0), stop=(tt == ntile - 1))
                tpv[st][k] = inc(op, "t_m")
            block.tensor(_)

        def emit_accum(st, k):
            hh, j, b, g0, g1, last = sblocks[k]
            h = st * 8 + hh
            ng = g1 - g0
            if b > 0:
                def _(v, st=st, k=k, h=h, j=j, last=last):
                    wt(v, "t_m", tpv[st][k])
                    wt(v, "a_p", expd[st][k])
                    v.scalar_tensor_tensor(DRUN[st], DRUN[st], ALPH[st],
                                           dtmp[:, st * 2: st * 2 + 1],
                                           op0=ALU.mult, op1=ALU.add)
                    v.scalar_tensor_tensor(YSB[st], YSB[st], ALPH[st],
                                           ybank(st),
                                           op0=ALU.mult, op1=ALU.add)
                    v.drain()
                    v.tensor_copy(MRUN[st], mnew[:, st:st + 1])
                    v.drain()
                    if last:
                        v.reciprocal(RDD[st], DRUN[st])
                        v.drain()
                        v.scalar_tensor_tensor(
                            a_tile(j)[:, h * 128:(h + 1) * 128],
                            YSB[st], RDD[st],
                            a_tile(j)[:, h * 128:(h + 1) * 128],
                            op0=ALU.mult, op1=ALU.add)
                        v.drain()
                    op = v.engine_nop()
                    vacc[st][k] = inc(op, "v_p")
                    yread_last[0] = cnt["v_p"]
                    if last:
                        y_last[j] = cnt["v_p"]
                block.vector(_)
            else:
                def _(v, st=st, k=k, h=h, j=j, ng=ng, last=last):
                    wt(v, "t_m", tpv[st][k])
                    wt(v, "a_p", expd[st][k])
                    v.tensor_copy(DRUN[st], dtmp[:, st * 2: st * 2 + 1])
                    v.tensor_copy(MRUN[st], mnew[:, st:st + 1])
                    v.tensor_copy(YSB[st], ybank(st))
                    v.drain()
                    if last:
                        v.reciprocal(RDD[st], DRUN[st])
                        v.drain()
                        v.scalar_tensor_tensor(
                            a_tile(j)[:, h * 128:(h + 1) * 128],
                            YSB[st], RDD[st],
                            a_tile(j)[:, h * 128:(h + 1) * 128],
                            op0=ALU.mult, op1=ALU.add)
                        v.drain()
                    op = v.engine_nop()
                    vacc[st][k] = inc(op, "v_p")
                    yread_last[0] = cnt["v_p"]
                    if last:
                        y_last[j] = cnt["v_p"]
                block.vector(_)

        emit_qmm(0, 0)
        emit_qmm(1, 0)
        emit_qk(0, 0)
        emit_qk(1, 0)
        for k in range(NSB):
            emit_maxbias(0, k)
            emit_maxbias(1, k)
            emit_exp_pair(k)
            if k + 1 < NSB:
                if (k + 1) % BLOCKS_PER_HEAD == 0:
                    emit_qmm(0, (k + 1) // BLOCKS_PER_HEAD)
                    emit_qmm(1, (k + 1) // BLOCKS_PER_HEAD)
                emit_qk(0, k + 1)
            emit_ptpv(0, k)
            emit_accum(0, k)
            if k + 1 < NSB:
                emit_qk(1, k + 1)
            emit_ptpv(1, k)
            emit_accum(1, k)
        t_attn_end = cnt["t_m"]

        out_dma = []
        for j in range(NJ):
            vj = y_last[j]

            def _(v, j=j, vj=vj):
                wt(v, "v_p", vj)
                emit_lnstats(v, a_tile(j))
                op = v.engine_nop(); inc(op, "v_p")
            block.vector(_)
            v_st = cnt["v_p"]

            def _(s, v_st=v_st):
                wt(s, "v_p", v_st)
                op = s.activation(rstd[:], varb[:], AF.Sqrt, bias=0.0, scale=1.0)
                inc(op, "a_p")
            block.scalar(_)
            a_rs = cnt["a_p"]

            def _(v, j=j, a_rs=a_rs):
                wt(v, "a_p", a_rs)
                src = a_tile(j)
                v.reciprocal(rstd[:], rstd[:])
                v.drain()
                v.tensor_scalar(nmu[:], bnag[:, 0:1], rstd[:], -1.0,
                                op0=ALU.mult, op1=ALU.mult)
                v.drain()
                op = v.tensor_scalar(src, src, rstd[:], nmu[:],
                                     op0=ALU.mult, op1=ALU.add)
                inc(op, "v_p")
            block.vector(_)
            v_out = cnt["v_p"]

            def _(s, j=j, v_out=v_out):
                wt(s, "v_p", v_out)
                op = s.dma_start(out=out[j * 128:(j + 1) * 128, :],
                                 in_=a_tile(j))
                out_dma.append(inc(op, "s_o"))
            block.scalar(_)

        def _(s):
            s.wait_ge(sems["s_o"], out_dma[-1])
        block.scalar(_)

    return nc


_NC_CACHE = [None]


def kernel(L0_attn_x, L0_attn_time_mix_k, L0_attn_time_mix_r, L0_attn_Wk,
           L0_attn_Wr, L0_attn_Wv, L0_attn_time_decay, L0_attn_ln_gamma,
           L0_attn_ln_beta, L0_ffn_W_gate, L0_ffn_W_qkv):
    x = np.ascontiguousarray(np.asarray(L0_attn_x, dtype=np.float32))
    B, T, _ = x.shape
    wkTh = np.ascontiguousarray(np.asarray(L0_attn_Wk, np.float32).T)
    wrTh = np.ascontiguousarray(np.asarray(L0_attn_Wr, np.float32).T)
    wvTh = np.ascontiguousarray(np.asarray(L0_attn_Wv, np.float32).T)
    wq = np.asarray(L0_ffn_W_qkv, np.float32)
    wqTh = np.ascontiguousarray(wq[0:D, :].T)
    wkvgTh = np.ascontiguousarray(
        np.concatenate([wq[D:3 * D, :].T,
                        np.asarray(L0_ffn_W_gate, np.float32).T], axis=1))
    tmkh = np.ascontiguousarray(
        np.asarray(L0_attn_time_mix_k, np.float32).reshape(NCC, 128).T)
    tmrh = np.ascontiguousarray(
        np.asarray(L0_attn_time_mix_r, np.float32).reshape(NCC, 128).T)
    tdh = np.ascontiguousarray(
        np.asarray(L0_attn_time_decay, np.float32).reshape(NHC, 128).T)
    gamh = np.ascontiguousarray(
        np.asarray(L0_attn_ln_gamma, np.float32).reshape(1, D))
    beth = np.ascontiguousarray(
        np.asarray(L0_attn_ln_beta, np.float32).reshape(1, D))

    if _NC_CACHE[0] is None:
        _NC_CACHE[0] = build_nc()
    nc = _NC_CACHE[0]

    in_maps = []
    toks = []
    for core in range(8):
        b, r = core // 4, core % 4
        tok = np.concatenate(
            [np.arange(128 * g, 128 * g + 128) for g in range(r, 32, 4)])
        sh = tok - 1
        sh[tok == 0] = 1
        xb = x[b]
        in_maps.append({
            "xT": np.ascontiguousarray(xb.T[:, tok]),
            "xTs": np.ascontiguousarray(xb.T[:, sh]),
            "xtok": np.ascontiguousarray(xb[tok]),
            "wkT": wkTh, "wrT": wrTh, "wvT": wvTh, "wqT": wqTh,
            "wkvgT": wkvgTh, "tmk": tmkh, "tmr": tmrh, "td": tdh,
            "gam": gamh, "bet": beth,
        })
        toks.append((b, tok))

    res = run_bass_kernel_spmd(nc, in_maps, core_ids=list(range(8)))
    full = np.empty((B, T, D), dtype=np.float32)
    for core in range(8):
        b, tok = toks[core]
        full[b][tok] = res.results[core]["out"]
    return full


# version 13 (final)
# speedup vs baseline: 1.0090x; 1.0014x over previous
import math
import contextlib
import numpy as np
import concourse.bass as bass
import concourse.mybir as mybir
from concourse.bass_utils import run_bass_kernel_spmd

F32 = mybir.dt.float32
I32 = mybir.dt.int32
U32 = mybir.dt.uint32
ALU = mybir.AluOpType
AF = mybir.ActivationFunctionType
AX = mybir.AxisListType

D = 2048
HH = 2816
NCC = 16
NHC = 22
NJ = 8
TL = 1024
NH = 16
RG = [[0, 1, 2, 3], [4, 5, 6, 7]]
SS = 1.0 / math.sqrt(128.0)
EPS = 1e-5
NEG = -1e30


def build_nc():
    nc = bass.Bass("TRN2", target_bir_lowering=False)
    xT = nc.dram_tensor("xT", [D, TL], F32, kind="ExternalInput").ap()
    xTs = nc.dram_tensor("xTs", [D, TL], F32, kind="ExternalInput").ap()
    xtok = nc.dram_tensor("xtok", [TL, D], F32, kind="ExternalInput").ap()
    wkT = nc.dram_tensor("wkT", [D, HH], F32, kind="ExternalInput").ap()
    wrT = nc.dram_tensor("wrT", [D, HH], F32, kind="ExternalInput").ap()
    wvT = nc.dram_tensor("wvT", [HH, D], F32, kind="ExternalInput").ap()
    wqT = nc.dram_tensor("wqT", [D, D], F32, kind="ExternalInput").ap()
    wkvgT = nc.dram_tensor("wkvgT", [D, 4112], F32, kind="ExternalInput").ap()
    tmkI = nc.dram_tensor("tmk", [128, NCC], F32, kind="ExternalInput").ap()
    tmrI = nc.dram_tensor("tmr", [128, NCC], F32, kind="ExternalInput").ap()
    tdI = nc.dram_tensor("td", [128, NHC], F32, kind="ExternalInput").ap()
    gamI = nc.dram_tensor("gam", [1, D], F32, kind="ExternalInput").ap()
    betI = nc.dram_tensor("bet", [1, D], F32, kind="ExternalInput").ap()
    out = nc.dram_tensor("out", [TL, D], F32, kind="ExternalOutput").ap()
    rkd = nc.dram_tensor("rkd", [1, 1], I32).ap()
    ag1i = nc.dram_tensor("ag1i", [1, 176 * 128], F32).ap()
    ag1o = nc.dram_tensor("ag1o", [4, 176 * 128], F32).ap()
    ag2i = nc.dram_tensor("ag2i", [1, 2048 * 128], F32).ap()
    ag2o = nc.dram_tensor("ag2o", [4, 2048 * 128], F32).ap()

    with contextlib.ExitStack() as ctx:
        En = ctx.enter_context
        A = En(nc.sbuf_tensor("A", [128, 16384], F32))
        U = En(nc.sbuf_tensor("U", [128, 22528], F32))
        P = En(nc.sbuf_tensor("P", [128, 10240], F32))
        tmk = En(nc.sbuf_tensor("tmkS", [128, NCC], F32))
        tmr = En(nc.sbuf_tensor("tmrS", [128, NCC], F32))
        tdv = En(nc.sbuf_tensor("tdv", [128, NHC], F32))
        lnw = En(nc.sbuf_tensor("lnw", [128, NHC], F32))
        wvec = En(nc.sbuf_tensor("wvec", [128, NHC], F32))
        w128 = En(nc.sbuf_tensor("w128", [128, NHC], F32))
        wpow = En(nc.sbuf_tensor("wpow", [128, 256], F32))
        iocol = En(nc.sbuf_tensor("iocol", [128, 128], F32))
        iorow = En(nc.sbuf_tensor("iorow", [128, 1], F32))
        ident = En(nc.sbuf_tensor("ident", [128, 128], F32))
        dadd = En(nc.sbuf_tensor("dadd", [128, 128], F32))
        madd = En(nc.sbuf_tensor("madd", [128, 512], F32))
        rkb = En(nc.sbuf_tensor("rkb", [128, 1], I32))
        rmeq = En(nc.sbuf_tensor("rmeq", [128, 4], F32))
        rmsc = En(nc.sbuf_tensor("rmsc", [128, 4], F32))
        stage = En(nc.sbuf_tensor("stage", [128, 1024], F32))
        gbuf = En(nc.sbuf_tensor("gbuf", [128, 16], F32))
        gm8 = En(nc.sbuf_tensor("gm8", [128, 8], F32))
        gidx = En(nc.sbuf_tensor("gidx", [128, 8], U32))
        gmask = En(nc.sbuf_tensor("gmask", [128, NJ * 16], F32))
        vmask = En(nc.sbuf_tensor("vmask", [128, NJ * 16], F32))
        gnb = En(nc.sbuf_tensor("gnb", [128, 1], F32))
        gsum = En(nc.sbuf_tensor("gsum", [128, 1], F32))
        top1 = En(nc.sbuf_tensor("top1", [128, NJ], F32))
        bnst = En(nc.sbuf_tensor("bnst", [128, 24], F32))
        bnag = En(nc.sbuf_tensor("bnag", [128, 2], F32))
        rstd = En(nc.sbuf_tensor("rstd", [128, 1], F32))
        nmu = En(nc.sbuf_tensor("nmu", [128, 1], F32))
        varb = En(nc.sbuf_tensor("varb", [128, 1], F32))
        mrun = En(nc.sbuf_tensor("mrun", [128, 1], F32))
        mnew = En(nc.sbuf_tensor("mnew", [128, 2], F32))
        mtmp = En(nc.sbuf_tensor("mtmp", [128, 4], F32))
        sbias = En(nc.sbuf_tensor("sbias", [128, 2], F32))
        alpha = En(nc.sbuf_tensor("alpha", [128, 1], F32))
        drun = En(nc.sbuf_tensor("drun", [128, 1], F32))
        dtmp = En(nc.sbuf_tensor("dtmp", [128, 4], F32))
        dblk = En(nc.sbuf_tensor("dblk", [128, 1], F32))
        rdd = En(nc.sbuf_tensor("rdd", [128, 1], F32))
        ysb = En(nc.sbuf_tensor("ysb", [128, 128], F32))
        ysb2 = En(nc.sbuf_tensor("ysb2", [128, 128], F32))
        st1s = En(nc.sbuf_tensor("st1s", [128, 8], F32))
        PS = En(nc.psum_tensor("PS", [128, 4096], F32))

        sems, cnt = {}, {}
        for name in ("s_x0", "s_x1", "s_w0", "s_w1", "s_t0", "s_t1",
                     "s_g", "s_o", "s_q",
                     "t_m", "v_p", "a_p", "g_p", "c_c"):
            sems[name] = En(nc.semaphore(name))
            cnt[name] = 0

        def inc(op, name, n=None):
            if n is None:
                n = 16 if name.startswith("s_") else 1
            cnt[name] += n
            op.then_inc(sems[name], n)
            return cnt[name]

        def wt(eng, name, v):
            if v > 0:
                eng.wait_ge(sems[name], v)

        block = En(nc.Block())

        def u_m(m):
            return U[:, m * 1024:(m + 1) * 1024]

        def xk_cc(n, cc):
            return A[:, n * 8192 + cc * 512: n * 8192 + (cc + 1) * 512]

        def a_tile(j):
            return A[:, j * 2048:(j + 1) * 2048]

        def x1t_cc(cc):
            return U[:, cc * 1024:(cc + 1) * 1024]

        qh_v = lambda p: U[:, 16384 + p * 1024: 16384 + (p + 1) * 1024]
        P32 = U[:, 18432:20480]
        PT = U[:, 20480:22528]
        ks_loc = U[:, 18432:19456]
        vs_loc = U[:, 19456:20480]
        ksT_loc = U[:, 20480:21504]
        xs_v = lambda p: P[:, p * 1024:(p + 1) * 1024]
        wsl_v = lambda p: P[:, 2048 + p * 2816: 2048 + (p + 1) * 2816]
        lend = P[:, 7680:7856]
        ends = P[:, 7856:8560]
        chz = P[:, 8560:9286]
        cary = P[:, 9286:9462]
        eraw = P[:, 9462:10166]
        gamBC = P[:, 0:2048]
        betBC = P[:, 2048:4096]
        xtok_v = lambda p: P[:, 4096 + p * 2048: 4096 + (p + 1) * 2048]
        w2_v = lambda p: P[:, p * 4096:(p + 1) * 4096]
        wq_v = P[:, 0:2048]
        ksF = P[:, 2048:6144]
        vsF = P[:, 6144:10240]

        def bank(k, w=512):
            return PS[:, k * 512: k * 512 + w]

        def _(s):
            op = s.dma_start(out=tmk[:], in_=tmkI); inc(op, "s_w0")
            op = s.dma_start(out=tmr[:], in_=tmrI); inc(op, "s_w0")
            op = s.dma_start(out=tdv[:], in_=tdI); inc(op, "s_w0")
        block.sync(_)
        W_TD = cnt["s_w0"]

        def _(g):
            op = g.iota(iocol[:], pattern=[[1, 128]], base=0,
                        channel_multiplier=0,
                        allow_small_or_imprecise_dtypes=True)
            inc(op, "g_p")
            op = g.iota(iorow[:], pattern=[[1, 1]], base=0,
                        channel_multiplier=1,
                        allow_small_or_imprecise_dtypes=True)
            inc(op, "g_p")
            rk = g.cc_rank(RG)
            op = g.store(rkb[0:1, 0:1], rk); inc(op, "g_p")
            g.wait_ge(sems["g_p"], cnt["g_p"])
            op = g.dma_start(out=rkd, in_=rkb[0:1, :]); inc(op, "s_g")
            g.wait_ge(sems["s_g"], cnt["s_g"])
            op = g.dma_start(out=rkb[:], in_=rkd.partition_broadcast(128))
            inc(op, "s_g")
        block.gpsimd(_)
        G_IOTA = 3
        G_RANKB = cnt["s_g"]

        def _(s):
            wt(s, "s_w0", W_TD)
            s.activation(lnw[:], tdv[:], AF.Exp, bias=0.0, scale=1.0)
            op = s.drain(); inc(op, "a_p")
        block.scalar(_)
        A_ETD = cnt["a_p"]

        def _(v):
            wt(v, "a_p", A_ETD)
            v.tensor_scalar(lnw[:], lnw[:], -1.0, None, op0=ALU.mult)
            v.drain()
            op = v.engine_nop(); inc(op, "v_p")
        block.vector(_)
        V_LNW = cnt["v_p"]

        def _(s):
            wt(s, "v_p", V_LNW)
            s.activation(wvec[:], lnw[:], AF.Exp, bias=0.0, scale=1.0)
            s.activation(w128[:], lnw[:], AF.Exp, bias=0.0, scale=128.0)
            op = s.drain(); inc(op, "a_p")
        block.scalar(_)
        A_WV = cnt["a_p"]

        def _(v):
            wt(v, "g_p", G_IOTA)
            v.tensor_scalar(ident[:], iocol[:], iorow[:], None, op0=ALU.is_equal)
            v.tensor_scalar(dadd[:], iocol[:], iorow[:], NEG,
                            op0=ALU.is_gt, op1=ALU.mult)
            wt(v, "s_g", G_RANKB)
            for rp in range(4):
                v.tensor_scalar(rmeq[:, rp:rp + 1], rkb[:], rp, None,
                                op0=ALU.is_equal)
                v.tensor_scalar(rmsc[:, rp:rp + 1], rkb[:], rp, NEG,
                                op0=ALU.is_lt, op1=ALU.mult)
            v.drain()
            for rp in range(4):
                v.tensor_scalar(madd[:, rp * 128:(rp + 1) * 128], dadd[:],
                                rmeq[:, rp:rp + 1], rmsc[:, rp:rp + 1],
                                op0=ALU.mult, op1=ALU.add)
            v.drain()
            op = v.engine_nop(); inc(op, "v_p")
        block.vector(_)
        V_CONST = cnt["v_p"]

        def emit_xbuild(tm, first_wait_tm, first_wait_v):
            built = []
            xs_ld = []
            units = [(n, cc) for n in range(2) for cc in range(NCC)]
            for i, (n, cc) in enumerate(units):
                p = i % 2

                def _(s, n=n, cc=cc, p=p, i=i):
                    if i < 2 and first_wait_v:
                        wt(s, "v_p", first_wait_v)
                    if i >= 2:
                        wt(s, "v_p", built[i - 2])
                    op = s.dma_start(out=xs_v(p)[:, 0:512],
                                     in_=xT[cc * 128:(cc + 1) * 128,
                                            n * 512:(n + 1) * 512])
                    inc(op, "s_x%d" % p)
                    op = s.dma_start(out=xs_v(p)[:, 512:1024],
                                     in_=xTs[cc * 128:(cc + 1) * 128,
                                             n * 512:(n + 1) * 512])
                    xs_ld.append(inc(op, "s_x%d" % p))
                block.sync(_)

                def _(v, n=n, cc=cc, p=p, i=i):
                    wt(v, "s_x%d" % p, xs_ld[i])
                    if i == 0:
                        wt(v, "s_w0", W_TD)
                        if first_wait_tm:
                            wt(v, "t_m", first_wait_tm)
                    v.tensor_tensor(xk_cc(n, cc), xs_v(p)[:, 512:1024],
                                    xs_v(p)[:, 0:512], op=ALU.subtract)
                    v.drain()
                    op = v.scalar_tensor_tensor(xk_cc(n, cc), xk_cc(n, cc),
                                                tm[:, cc:cc + 1],
                                                xs_v(p)[:, 0:512],
                                                op0=ALU.mult, op1=ALU.add)
                    built.append(inc(op, "v_p"))
                block.vector(_)
            return built[-1]

        def emit_proj(wsrc, nm, nk, rhs_fn, evac_emit, evac_list,
                      ready_v=None, ready_extra=None, slab_cols=128,
                      init_tm=0):
            t_groups = []
            slab_ld = []
            for m in range(nm):
                p = m % 2

                def _(s, m=m, p=p):
                    if m < 2:
                        wt(s, "t_m", init_tm)
                    if m >= 2:
                        wt(s, "t_m", t_groups[(m - 2) * 2 + 1])
                    op = s.dma_start(
                        out=wsl_v(p)[:, 0:nk * 128].rearrange(
                            "p (kc q) -> p kc q", kc=nk),
                        in_=wsrc.rearrange("(kc p) q -> p kc q", p=128)[
                            :, :, m * slab_cols:(m + 1) * slab_cols])
                    slab_ld.append(inc(op, "s_w%d" % p))
                block.sync(_)

                for n in range(2):
                    gi = m * 2 + n

                    def _(t, m=m, n=n, p=p, gi=gi):
                        if n == 0:
                            wt(t, "s_w%d" % p, slab_ld[m])
                        if gi == 0:
                            if ready_v is not None:
                                wt(t, "v_p", ready_v)
                            if ready_extra is not None:
                                ready_extra(t)
                        if gi >= 2:
                            kind, val = evac_list[gi - 2]
                            wt(t, kind, val)
                        for kc in range(nk):
                            op = t.matmul(bank(gi % 2),
                                          wsl_v(p)[:, kc * 128:(kc + 1) * 128],
                                          rhs_fn(kc, n),
                                          start=(kc == 0), stop=(kc == nk - 1))
                        t_groups.append(inc(op, "t_m"))
                    block.tensor(_)
                    evac_emit(m, n, gi, t_groups)
            return t_groups

        xk_ready = emit_xbuild(tmk, 0, 0)

        k_ev = []

        def k_evac(m, n, gi, t_groups):
            def _(s, m=m, n=n, gi=gi):
                wt(s, "t_m", t_groups[gi])
                op = s.activation(u_m(m)[:, n * 512:(n + 1) * 512],
                                  bank(gi % 2), AF.Relu, bias=0.0, scale=1.0)
                k_ev.append(("a_p", inc(op, "a_p")))
            block.scalar(_)

        tg_k = emit_proj(wkT, NHC, NCC, lambda kc, n: xk_cc(n, kc),
                         k_evac, k_ev, ready_v=xk_ready)
        T_KALL = tg_k[-1]

        scan_done = []
        for m in range(NHC):
            def _(v, m=m):
                wt(v, "a_p", k_ev[m * 2 + 1][1])
                for j in range(NJ):
                    v.tensor_tensor_scan(
                        u_m(m)[:, j * 128:(j + 1) * 128],
                        wvec[:, m:m + 1].broadcast_to([128, 128]),
                        u_m(m)[:, j * 128:(j + 1) * 128], 0.0,
                        op0=ALU.mult, op1=ALU.add)
                v.drain()
                op = v.tensor_copy(lend[:, m * 8:(m + 1) * 8],
                                   u_m(m)[:, 127::128])
                scan_done.append(inc(op, "v_p"))
            block.vector(_)
        V_SCANALL = scan_done[-1]

        def _(g):
            wt(g, "v_p", V_SCANALL)
            op = g.dma_start(out=ag1i.rearrange("o (p c) -> p (o c)", p=128),
                             in_=lend[:])
            inc(op, "s_g")
            g.wait_ge(sems["s_g"], cnt["s_g"])
            op = g.collective_compute("AllGather", ALU.bypass,
                                      replica_groups=RG,
                                      ins=[ag1i.opt()], outs=[ag1o.opt()])
            inc(op, "c_c")
            g.wait_ge(sems["c_c"], cnt["c_c"])
            for r in range(4):
                op = g.dma_start(
                    out=eraw[:, r * 176:(r + 1) * 176],
                    in_=ag1o[r:r + 1, :].rearrange(
                        "o (p c) -> p (o c)", p=128))
                inc(op, "s_g")
        block.gpsimd(_)
        G_ENDS = cnt["s_g"]

        xr_ready = emit_xbuild(tmr, T_KALL, xk_ready)

        def _(v):
            wt(v, "s_g", G_ENDS)
            wt(v, "a_p", A_WV)
            for r in range(4):
                v.tensor_copy(
                    ends[:].rearrange("p (m j) -> p m j", m=NHC)[:, :, r::4],
                    eraw[:, r * 176:(r + 1) * 176].rearrange(
                        "p (m j) -> p m j", m=NHC))
            v.memset(chz[:], 0.0)
            v.drain()
            for m in range(NHC):
                v.tensor_tensor_scan(
                    chz[:, m * 33 + 1: m * 33 + 33],
                    w128[:, m:m + 1].broadcast_to([128, 32]),
                    ends[:, m * 32:(m + 1) * 32], 0.0,
                    op0=ALU.mult, op1=ALU.add)
            v.drain()
            for m in range(NHC):
                v.tensor_scalar(cary[:, m * 8:(m + 1) * 8],
                                chz[:, m * 33: m * 33 + 29: 4],
                                rmeq[:, 0:1], None, op0=ALU.mult)
            v.drain()
            for rp in range(1, 4):
                for m in range(NHC):
                    v.scalar_tensor_tensor(
                        cary[:, m * 8:(m + 1) * 8],
                        chz[:, m * 33 + rp: m * 33 + rp + 29: 4],
                        rmeq[:, rp:rp + 1],
                        cary[:, m * 8:(m + 1) * 8],
                        op0=ALU.mult, op1=ALU.add)
                v.drain()
            op = v.engine_nop(); inc(op, "v_p")
        block.vector(_)
        V_CARY = cnt["v_p"]

        wpow_done = []
        corr_done = []
        for m in range(NHC):
            def _(s, m=m):
                if m == 0:
                    wt(s, "a_p", A_WV)
                    wt(s, "g_p", G_IOTA)
                if m >= 2:
                    wt(s, "v_p", corr_done[m - 2])
                op = s.activation(wpow[:, (m % 2) * 128:(m % 2 + 1) * 128],
                                  iocol[:], AF.Exp, bias=lnw[:, m:m + 1],
                                  scale=lnw[:, m:m + 1])
                wpow_done.append(inc(op, "a_p"))
            block.scalar(_)

            def _(v, m=m):
                if m == 0:
                    wt(v, "v_p", V_CARY)
                wt(v, "a_p", wpow_done[m])
                for j in range(NJ):
                    v.scalar_tensor_tensor(
                        u_m(m)[:, j * 128:(j + 1) * 128],
                        wpow[:, (m % 2) * 128:(m % 2 + 1) * 128],
                        cary[:, m * 8 + j: m * 8 + j + 1],
                        u_m(m)[:, j * 128:(j + 1) * 128],
                        op0=ALU.mult, op1=ALU.add)
                v.drain()
                op = v.engine_nop()
                corr_done.append(inc(op, "v_p"))
            block.vector(_)


        r_ev = []

        def r_evac(m, n, gi, t_groups):
            def _(v, m=m, n=n, gi=gi):
                wt(v, "t_m", t_groups[gi])
                wt(v, "v_p", corr_done[m])
                v.drain()
                op = v.tensor_tensor(u_m(m)[:, n * 512:(n + 1) * 512],
                                     bank(gi % 2),
                                     u_m(m)[:, n * 512:(n + 1) * 512],
                                     op=ALU.mult)
                r_ev.append(("v_p", inc(op, "v_p")))
            block.vector(_)

        tg_r = emit_proj(wrT, NHC, NCC, lambda kc, n: xk_cc(n, kc),
                         r_evac, r_ev, ready_v=xr_ready, init_tm=T_KALL)
        T_RALL = tg_r[-1]
        V_RLAST = r_ev[-1][1]

        wv_ev = []
        wv_tr = []
        wv_at = []

        def wv_evac(m2, n, gi, t_groups):
            def _(v, gi=gi):
                wt(v, "t_m", t_groups[gi])
                if gi >= 2:
                    wt(v, "t_m", wv_tr[gi - 2])
                op = v.tensor_copy(stage[:, (gi % 2) * 512:(gi % 2 + 1) * 512],
                                   bank(gi % 2))
                wv_ev.append(("v_p", inc(op, "v_p")))
            block.vector(_)

            def _(t, m2=m2, n=n, gi=gi):
                wt(t, "v_p", wv_ev[gi][1])
                if gi == 0:
                    wt(t, "v_p", V_CONST)
                if gi >= 1:
                    wt(t, "a_p", wv_at[gi - 1])
                for jj in range(4):
                    op = t.transpose(bank(6)[:, jj * 128:(jj + 1) * 128],
                                     stage[:, (gi % 2) * 512 + jj * 128:
                                           (gi % 2) * 512 + (jj + 1) * 128],
                                     ident[:])
                wv_tr.append(inc(op, "t_m"))
            block.tensor(_)

            def _(s, m2=m2, n=n, gi=gi):
                wt(s, "t_m", wv_tr[gi])
                op = s.activation(
                    A[:, 0:16384].rearrange("p (j c) -> p j c", j=8)[
                        :, 4 * n: 4 * n + 4, m2 * 128:(m2 + 1) * 128],
                    bank(6).rearrange("p (jj c) -> p jj c", jj=4),
                    AF.Copy, bias=0.0, scale=1.0)
                wv_at.append(inc(op, "a_p"))
            block.scalar(_)

        def wv_first_wait(t):
            wt(t, "v_p", V_RLAST)

        tg_wv = emit_proj(wvT, NCC, NHC,
                          lambda hc, n: u_m(hc)[:, n * 512:(n + 1) * 512],
                          wv_evac, wv_ev, ready_extra=wv_first_wait,
                          init_tm=T_RALL)
        T_WVALL = tg_wv[-1]
        A_ATTNTOK = wv_at[-1]

        def _(g):
            wt(g, "v_p", xr_ready)
            wt(g, "t_m", T_WVALL)
            op = g.dma_start(out=gamBC, in_=gamI.partition_broadcast(128))
            inc(op, "s_g")
            op = g.dma_start(out=betBC, in_=betI.partition_broadcast(128))
            inc(op, "s_g")
        block.gpsimd(_)
        G_GB = cnt["s_g"]

        def emit_lnstats(v, src):
            for c4 in range(4):
                v.bn_stats(bnst[:, c4 * 6:(c4 + 1) * 6],
                           src[:, c4 * 512:(c4 + 1) * 512])
            v.drain()
            v.bn_aggr(bnag[:], bnst[:])
            v.drain()
            v.tensor_scalar(varb[:], bnag[:, 1:2], 1.0, EPS,
                            op0=ALU.mult, op1=ALU.add)
            v.drain()

        xtok_ld = []
        ln_done = []
        x1t_done = []
        for j in range(NJ):
            def _(g, j=j):
                if j == 0:
                    wt(g, "t_m", T_WVALL)
                if j >= 2:
                    wt(g, "v_p", ln_done[j - 2])
                op = g.dma_start(out=xtok_v(j % 2),
                                 in_=xtok[j * 128:(j + 1) * 128, :])
                xtok_ld.append(inc(op, "s_t%d" % (j % 2)))
            block.gpsimd(_)

            def _(v, j=j):
                wt(v, "a_p", A_ATTNTOK)
                emit_lnstats(v, a_tile(j))
                op = v.engine_nop(); inc(op, "v_p")
            block.vector(_)
            v_st1 = cnt["v_p"]

            def _(s, j=j, v_st1=v_st1):
                wt(s, "v_p", v_st1)
                op = s.activation(rstd[:], varb[:], AF.Sqrt, bias=0.0, scale=1.0)
                inc(op, "a_p")
            block.scalar(_)
            a_rs1 = cnt["a_p"]

            def _(v, j=j, a_rs1=a_rs1):
                wt(v, "a_p", a_rs1)
                src = a_tile(j)
                v.reciprocal(rstd[:], rstd[:])
                v.drain()
                v.tensor_scalar(nmu[:], bnag[:, 0:1], rstd[:], -1.0,
                                op0=ALU.mult, op1=ALU.mult)
                v.drain()
                v.tensor_scalar(src, src, rstd[:], nmu[:],
                                op0=ALU.mult, op1=ALU.add)
                v.drain()
                wt(v, "s_g", G_GB)
                v.tensor_tensor(src, src, gamBC, op=ALU.mult)
                v.drain()
                v.tensor_tensor(src, src, betBC, op=ALU.add)
                v.drain()
                wt(v, "s_t%d" % (j % 2), xtok_ld[j])
                v.tensor_tensor(src, src, xtok_v(j % 2), op=ALU.add)
                v.drain()
                emit_lnstats(v, src)
                op = v.engine_nop(); inc(op, "v_p")
            block.vector(_)
            v_st2 = cnt["v_p"]

            def _(s, j=j, v_st2=v_st2):
                wt(s, "v_p", v_st2)
                op = s.activation(rstd[:], varb[:], AF.Sqrt, bias=0.0, scale=1.0)
                inc(op, "a_p")
            block.scalar(_)
            a_rs2 = cnt["a_p"]

            def _(v, j=j, a_rs2=a_rs2):
                wt(v, "a_p", a_rs2)
                src = a_tile(j)
                v.reciprocal(rstd[:], rstd[:])
                v.drain()
                v.tensor_scalar(nmu[:], bnag[:, 0:1], rstd[:], -1.0,
                                op0=ALU.mult, op1=ALU.mult)
                v.drain()
                op = v.tensor_scalar(src, src, rstd[:], nmu[:],
                                     op0=ALU.mult, op1=ALU.add)
                ln_done.append(inc(op, "v_p"))
            block.vector(_)

            for q4 in range(4):
                def _(t, j=j, q4=q4):
                    if q4 == 0:
                        wt(t, "v_p", ln_done[j])
                    if x1t_done:
                        wt(t, "a_p", x1t_done[-1])
                    for mm in range(4):
                        m2 = q4 * 4 + mm
                        op = t.transpose(bank(6)[:, mm * 128:(mm + 1) * 128],
                                         a_tile(j)[:, m2 * 128:(m2 + 1) * 128],
                                         ident[:])
                    inc(op, "t_m")
                block.tensor(_)
                t_x1 = cnt["t_m"]

                def _(s, j=j, q4=q4, t_x1=t_x1):
                    wt(s, "t_m", t_x1)
                    op = s.activation(
                        U[:, 0:16384].rearrange("p (cc c) -> p cc c", cc=16)[
                            :, q4 * 4: q4 * 4 + 4, j * 128:(j + 1) * 128],
                        bank(6).rearrange("p (mm c) -> p mm c", mm=4),
                        AF.Copy, bias=0.0, scale=1.0)
                    x1t_done.append(inc(op, "a_p"))
                block.scalar(_)
        A_X1T = x1t_done[-1]
        V_LNALL = ln_done[-1]

        kv_chunks = [(4096, 16)] + [(i * 256, 256) for i in range(16)]
        kv_slab_ld = []
        kv_tg = []
        kv_ev = []
        for ci, (c0, w) in enumerate(kv_chunks):
            p = ci % 2

            def _(s, ci=ci, c0=c0, w=w, p=p):
                if ci < 2:
                    wt(s, "v_p", V_LNALL)
                if ci >= 2:
                    wt(s, "t_m", kv_tg[(ci - 2) * NJ + NJ - 1])
                op = s.dma_start(
                    out=w2_v(p)[:, 0:16 * w].rearrange(
                        "p (kc q) -> p kc q", kc=16),
                    in_=wkvgT.rearrange("(kc p) q -> p kc q", p=128)[
                        :, :, c0:c0 + w])
                kv_slab_ld.append(inc(op, "s_w%d" % p))
            block.sync(_)

            for j in range(NJ):
                def _(t, ci=ci, j=j, p=p, w=w):
                    if j == 0:
                        wt(t, "s_w%d" % p, kv_slab_ld[ci])
                        if ci == 0:
                            wt(t, "a_p", A_X1T)
                    if ci >= 1:
                        wt(t, "v_p", kv_ev[(ci - 1) * NJ + j])
                    for kc in range(NCC):
                        op = t.matmul(bank(j, w),
                                      x1t_cc(kc)[:, j * 128:(j + 1) * 128],
                                      w2_v(p)[:, kc * w:(kc + 1) * w],
                                      start=(kc == 0), stop=(kc == NCC - 1))
                    kv_tg.append(inc(op, "t_m"))
                block.tensor(_)
                t_here = kv_tg[-1]

                if ci == 0:
                    def _(v, j=j, t_here=t_here):
                        wt(v, "t_m", t_here)
                        v.tensor_copy(gbuf[:], bank(j, 16))
                        v.drain()
                        v.max(gm8[:], gbuf[:])
                        v.drain()
                        v.max_index(gidx[:], gm8[:], gbuf[:])
                        v.tensor_scalar(gnb[:], gm8[:, 0:1], -1.0, None,
                                        op0=ALU.mult)
                        v.drain()
                        op = v.engine_nop(); inc(op, "v_p")
                    block.vector(_)
                    v_g = cnt["v_p"]

                    def _(s, j=j, v_g=v_g):
                        wt(s, "v_p", v_g)
                        op = s.activation(gbuf[:], gbuf[:], AF.Exp,
                                          bias=gnb[:], scale=1.0,
                                          accum_out=gsum[:])
                        inc(op, "a_p")
                    block.scalar(_)
                    a_g = cnt["a_p"]

                    def _(v, j=j, a_g=a_g):
                        wt(v, "a_p", a_g)
                        v.reciprocal(top1[:, j:j + 1], gsum[:])
                        for h in range(16):
                            v.tensor_scalar(gmask[:, j * 16 + h: j * 16 + h + 1],
                                            gidx[:, 0:1], h, None,
                                            op0=ALU.is_equal)
                        v.drain()
                        op = v.tensor_scalar(vmask[:, j * 16:(j + 1) * 16],
                                             gmask[:, j * 16:(j + 1) * 16],
                                             top1[:, j:j + 1], None,
                                             op0=ALU.mult)
                        kv_ev.append(inc(op, "v_p"))
                    block.vector(_)
                else:
                    def _(v, ci=ci, j=j, t_here=t_here):
                        wt(v, "t_m", t_here)
                        nn = ci - 1
                        if nn < 8:
                            dst = ks_loc[:, j * 128:(j + 1) * 128]
                            msk = gmask
                            h0 = 2 * nn
                        else:
                            dst = vs_loc[:, j * 128:(j + 1) * 128]
                            msk = vmask
                            h0 = 2 * (nn - 8)
                        if h0 == 0:
                            v.tensor_scalar(dst, bank(j, 256)[:, 0:128],
                                            msk[:, j * 16: j * 16 + 1],
                                            None, op0=ALU.mult)
                        else:
                            v.scalar_tensor_tensor(
                                dst, bank(j, 256)[:, 0:128],
                                msk[:, j * 16 + h0: j * 16 + h0 + 1],
                                dst, op0=ALU.mult, op1=ALU.add)
                        v.drain()
                        op = v.scalar_tensor_tensor(
                            dst, bank(j, 256)[:, 128:256],
                            msk[:, j * 16 + h0 + 1: j * 16 + h0 + 2],
                            dst, op0=ALU.mult, op1=ALU.add)
                        kv_ev.append(inc(op, "v_p"))
                    block.vector(_)
        V_KVDONE = kv_ev[-1]
        T_KVALL = kv_tg[-1]

        ksT_tr = []
        ksT_ev = []
        for j in range(NJ):
            def _(t, j=j):
                if j == 0:
                    wt(t, "v_p", V_KVDONE)
                if j >= 4:
                    wt(t, "a_p", ksT_ev[j - 4])
                op = t.transpose(bank(6)[:, (j % 4) * 128:(j % 4 + 1) * 128],
                                 ks_loc[:, j * 128:(j + 1) * 128], ident[:])
                ksT_tr.append(inc(op, "t_m"))
            block.tensor(_)

            def _(s, j=j):
                wt(s, "t_m", ksT_tr[j])
                op = s.activation(ksT_loc[:, j * 128:(j + 1) * 128],
                                  bank(6)[:, (j % 4) * 128:(j % 4 + 1) * 128],
                                  AF.Copy, bias=0.0, scale=1.0)
                ksT_ev.append(inc(op, "a_p"))
            block.scalar(_)

        def _(g):
            wt(g, "a_p", ksT_ev[-1])
            op = g.dma_start(out=ag2i[:, 0:131072].rearrange(
                "o (p c) -> p (o c)", p=128), in_=ksT_loc[:])
            inc(op, "s_g")
            op = g.dma_start(out=ag2i[:, 131072:262144].rearrange(
                "o (p c) -> p (o c)", p=128), in_=vs_loc[:])
            inc(op, "s_g")
            g.wait_ge(sems["s_g"], cnt["s_g"])
            op = g.collective_compute("AllGather", ALU.bypass,
                                      replica_groups=RG,
                                      ins=[ag2i.opt()], outs=[ag2o.opt()])
            inc(op, "c_c")
            g.wait_ge(sems["c_c"], cnt["c_c"])
            wt(g, "t_m", T_KVALL)
            for r in range(4):
                op = g.dma_start(
                    out=ksF.rearrange("p (j q) -> p j q", j=32)[:, r::4, :],
                    in_=ag2o[r:r + 1, 0:131072].rearrange(
                        "o (p c) -> p (o c)", p=128).rearrange(
                        "p (j q) -> p j q", j=8))
                inc(op, "s_g")
                op = g.dma_start(
                    out=vsF.rearrange("p (j q) -> p j q", j=32)[:, r::4, :],
                    in_=ag2o[r:r + 1, 131072:262144].rearrange(
                        "o (p c) -> p (o c)", p=128).rearrange(
                        "p (j q) -> p j q", j=8))
                inc(op, "s_g")
        block.gpsimd(_)
        G_KSVS = cnt["s_g"]

        BLOCKS_PER_HEAD = 20
        wq_ld = []
        t_q_done = []
        y_last = [0] * NJ

        sblocks = []
        for hh in range(8):
            for j in range(NJ):
                G = j + 1
                nb = (G + 1) // 2
                for b in range(nb):
                    g0, g1 = 2 * b, min(2 * b + 2, G)
                    sblocks.append((hh, j, b, g0, g1, b == nb - 1))
        NSB = len(sblocks)

        def sb_par(st):
            return PS[:, 1024 + st * 1024: 2048 + st * 1024]

        def p32h(st):
            return P32[:, st * 1024:(st + 1) * 1024]

        def pth(st):
            return PT[:, st * 1024:(st + 1) * 1024]

        ybank = lambda st: PS[:, 3584 + st * 128: 3584 + (st + 1) * 128]

        MRUN = [mrun[:, 0:1], st1s[:, 0:1]]
        DRUN = [drun[:, 0:1], st1s[:, 1:2]]
        DBLK = [dblk[:, 0:1], st1s[:, 2:3]]
        ALPH = [alpha[:, 0:1], st1s[:, 3:4]]
        RDD = [rdd[:, 0:1], st1s[:, 4:5]]
        YSB = [ysb[:, :], ysb2[:, :]]
        MT = [mtmp[:, 0:2], mtmp[:, 2:4]]

        expd = [{}, {}]
        ptev = [{}, {}]
        tpt = [{}, {}]
        tpv = [{}, {}]
        vacc = [{}, {}]
        vmx = [{}, {}]
        tqk = [{}, {}]
        vq = [{}, {}]
        vq_last = [0]
        yread_last = [0]
        last_ptev = [0]
        last_banks6_user = [0]

        def emit_qmm(st, hh):
            h = st * 8 + hh
            def _(s, st=st, hh=hh, h=h):
                if not wq_ld:
                    wt(s, "t_m", T_KVALL)
                else:
                    wt(s, "t_m", t_q_done[-1])
                op = s.dma_start(
                    out=wq_v[:, :].rearrange("p (kc q) -> p kc q", kc=16),
                    in_=wqT.rearrange("(kc p) q -> p kc q", p=128)[
                        :, :, h * 128:(h + 1) * 128])
                wq_ld.append(inc(op, "s_q"))
            block.sync(_)
            for n in range(2):
                def _(t, st=st, hh=hh, n=n):
                    if n == 0:
                        wt(t, "s_q", wq_ld[-1])
                        if vq_last[0]:
                            wt(t, "v_p", vq_last[0])
                        if last_ptev[0]:
                            wt(t, "v_p", last_ptev[0])
                    for kc in range(NCC):
                        op = t.matmul(bank(n),
                                      wq_v[:, kc * 128:(kc + 1) * 128],
                                      x1t_cc(kc)[:, n * 512:(n + 1) * 512],
                                      start=(kc == 0), stop=(kc == NCC - 1))
                    inc(op, "t_m")
                block.tensor(_)
                t_qn = cnt["t_m"]

                def _(v, st=st, hh=hh, n=n, t_qn=t_qn):
                    wt(v, "t_m", t_qn)
                    if hh >= 1 and n == 0:
                        wt(v, "t_m", tqk[st][hh * BLOCKS_PER_HEAD - 1])
                    op = v.tensor_copy(qh_v(st)[:, n * 512:(n + 1) * 512],
                                       bank(n))
                    inc(op, "v_p")
                block.vector(_)
            t_q_done.append(cnt["t_m"])
            vq[st][hh] = cnt["v_p"]
            vq_last[0] = cnt["v_p"]

        def emit_qk(st, k):
            hh, j, b, g0, g1, last = sblocks[k]
            def _(t, st=st, k=k, hh=hh, j=j, g0=g0, g1=g1):
                if k == 0 and st == 0:
                    wt(t, "s_g", G_KSVS)
                wt(t, "v_p", vq[st][hh])
                if k >= 1:
                    wt(t, "a_p", expd[st][k - 1])
                for g in range(g0, g1):
                    op = t.matmul(sb_par(st)[:, (g - g0) * 512:(g - g0 + 1) * 512],
                                  qh_v(st)[:, j * 128:(j + 1) * 128],
                                  ksF[:, g * 512:(g + 1) * 512],
                                  start=True, stop=True)
                tqk[st][k] = inc(op, "t_m")
            block.tensor(_)

        def emit_maxbias_pair(k):
            hh, j, b, g0, g1, last = sblocks[k]
            G = j + 1
            def _(v, k=k, g0=g0, g1=g1, b=b, G=G):
                ng = g1 - g0
                for st in range(2):
                    wt(v, "t_m", tqk[st][k])
                    if k >= 1:
                        wt(v, "a_p", expd[st][k - 1])
                    if g1 == G:
                        v.tensor_tensor(sb_par(st)[:, (ng - 1) * 512:ng * 512],
                                        sb_par(st)[:, (ng - 1) * 512:ng * 512],
                                        madd[:], op=ALU.add)
                        v.drain()
                    v.reduce_max(mnew[:, st:st + 1], sb_par(st)[:, 0:ng * 512],
                                 axis=AX.X)
                    v.drain()
                    if b > 0:
                        v.tensor_tensor(mnew[:, st:st + 1], mnew[:, st:st + 1],
                                        MRUN[st], op=ALU.max)
                        v.drain()
                    op = v.tensor_scalar(sbias[:, st:st + 1],
                                         mnew[:, st:st + 1],
                                         -SS, None, op0=ALU.mult)
                    vmx[st][k] = inc(op, "v_p")
            block.vector(_)

        def emit_exp_pair(k):
            hh, j, b, g0, g1, last = sblocks[k]
            ng = g1 - g0
            def _(s, k=k, ng=ng, b=b):
                for st in range(2):
                    wt(s, "v_p", vmx[st][k])
                    if k >= 1:
                        wt(s, "t_m", tpt[st][k - 1])
                    s.activation(
                        p32h(st)[:, 0:ng * 512], sb_par(st)[:, 0:ng * 512],
                        AF.Exp, bias=sbias[:, st:st + 1], scale=SS,
                        accum_out=dtmp[:, st * 2: st * 2 + 1])
                    if b > 0:
                        s.activation(ALPH[st], MRUN[st], AF.Exp,
                                     bias=sbias[:, st:st + 1], scale=SS)
                    op = s.drain()
                    expd[st][k] = inc(op, "a_p")
            block.scalar(_)

        def emit_ptpv(st, k):
            hh, j, b, g0, g1, last = sblocks[k]
            ng = g1 - g0
            ntile = ng * 4

            def _(t, st=st, k=k, ntile=ntile):
                wt(t, "a_p", expd[st][k])
                if last_ptev[0]:
                    wt(t, "v_p", last_ptev[0])
                if vq_last[0]:
                    wt(t, "v_p", vq_last[0])
                for tt in range(min(4, ntile)):
                    op = t.transpose(bank(6)[:, tt * 128:(tt + 1) * 128],
                                     p32h(st)[:, tt * 128:(tt + 1) * 128],
                                     ident[:])
                for tt in range(4, ntile):
                    op = t.transpose(bank(0)[:, (tt - 4) * 128:(tt - 3) * 128],
                                     p32h(st)[:, tt * 128:(tt + 1) * 128],
                                     ident[:])
                tpt[st][k] = inc(op, "t_m")
            block.tensor(_)

            def _(v, st=st, k=k, ntile=ntile):
                wt(v, "t_m", tpt[st][k])
                if k >= 1 and (k - 1) in tpv[st]:
                    wt(v, "t_m", tpv[st][k - 1])
                v.tensor_copy(pth(st)[:, 0:min(4, ntile) * 128],
                              bank(6)[:, 0:min(4, ntile) * 128])
                if ntile > 4:
                    op = v.tensor_copy(pth(st)[:, 512:ntile * 128],
                                       bank(0)[:, 0:(ntile - 4) * 128])
                else:
                    op = v.engine_nop()
                last_ptev[0] = inc(op, "v_p")
            block.vector(_)
            pt_ev_here = last_ptev[0]

            def _(t, st=st, k=k, ntile=ntile, b=b, pt_ev_here=pt_ev_here):
                wt(t, "v_p", pt_ev_here)
                if yread_last[0]:
                    wt(t, "v_p", yread_last[0])
                for tt in range(ntile):
                    kt = b * 8 + tt
                    op = t.matmul(ybank(st),
                                  pth(st)[:, tt * 128:(tt + 1) * 128],
                                  vsF[:, kt * 128:(kt + 1) * 128],
                                  start=(tt == 0), stop=(tt == ntile - 1))
                tpv[st][k] = inc(op, "t_m")
            block.tensor(_)

        def emit_accum(st, k):
            hh, j, b, g0, g1, last = sblocks[k]
            h = st * 8 + hh
            ng = g1 - g0
            if b > 0:
                def _(v, st=st, k=k, h=h, j=j, last=last):
                    wt(v, "t_m", tpv[st][k])
                    wt(v, "a_p", expd[st][k])
                    v.scalar_tensor_tensor(DRUN[st], DRUN[st], ALPH[st],
                                           dtmp[:, st * 2: st * 2 + 1],
                                           op0=ALU.mult, op1=ALU.add)
                    v.scalar_tensor_tensor(YSB[st], YSB[st], ALPH[st],
                                           ybank(st),
                                           op0=ALU.mult, op1=ALU.add)
                    v.drain()
                    v.tensor_copy(MRUN[st], mnew[:, st:st + 1])
                    v.drain()
                    if last:
                        v.reciprocal(RDD[st], DRUN[st])
                        v.drain()
                        v.scalar_tensor_tensor(
                            a_tile(j)[:, h * 128:(h + 1) * 128],
                            YSB[st], RDD[st],
                            a_tile(j)[:, h * 128:(h + 1) * 128],
                            op0=ALU.mult, op1=ALU.add)
                        v.drain()
                    op = v.engine_nop()
                    vacc[st][k] = inc(op, "v_p")
                    yread_last[0] = cnt["v_p"]
                    if last:
                        y_last[j] = cnt["v_p"]
                block.vector(_)
            else:
                def _(v, st=st, k=k, h=h, j=j, ng=ng, last=last):
                    wt(v, "t_m", tpv[st][k])
                    wt(v, "a_p", expd[st][k])
                    v.tensor_copy(DRUN[st], dtmp[:, st * 2: st * 2 + 1])
                    v.tensor_copy(MRUN[st], mnew[:, st:st + 1])
                    v.tensor_copy(YSB[st], ybank(st))
                    v.drain()
                    if last:
                        v.reciprocal(RDD[st], DRUN[st])
                        v.drain()
                        v.scalar_tensor_tensor(
                            a_tile(j)[:, h * 128:(h + 1) * 128],
                            YSB[st], RDD[st],
                            a_tile(j)[:, h * 128:(h + 1) * 128],
                            op0=ALU.mult, op1=ALU.add)
                        v.drain()
                    op = v.engine_nop()
                    vacc[st][k] = inc(op, "v_p")
                    yread_last[0] = cnt["v_p"]
                    if last:
                        y_last[j] = cnt["v_p"]
                block.vector(_)

        emit_qmm(0, 0)
        emit_qmm(1, 0)
        emit_qk(0, 0)
        emit_qk(1, 0)
        for k in range(NSB):
            emit_maxbias_pair(k)
            emit_exp_pair(k)
            if k + 1 < NSB:
                if (k + 1) % BLOCKS_PER_HEAD == 0:
                    emit_qmm(0, (k + 1) // BLOCKS_PER_HEAD)
                    emit_qmm(1, (k + 1) // BLOCKS_PER_HEAD)
                emit_qk(0, k + 1)
            emit_ptpv(0, k)
            emit_accum(0, k)
            if k + 1 < NSB:
                emit_qk(1, k + 1)
            emit_ptpv(1, k)
            emit_accum(1, k)
        t_attn_end = cnt["t_m"]

        out_dma = []
        for j in range(NJ):
            vj = y_last[j]

            def _(v, j=j, vj=vj):
                wt(v, "v_p", vj)
                emit_lnstats(v, a_tile(j))
                op = v.engine_nop(); inc(op, "v_p")
            block.vector(_)
            v_st = cnt["v_p"]

            def _(s, v_st=v_st):
                wt(s, "v_p", v_st)
                op = s.activation(rstd[:], varb[:], AF.Sqrt, bias=0.0, scale=1.0)
                inc(op, "a_p")
            block.scalar(_)
            a_rs = cnt["a_p"]

            def _(v, j=j, a_rs=a_rs):
                wt(v, "a_p", a_rs)
                src = a_tile(j)
                v.reciprocal(rstd[:], rstd[:])
                v.drain()
                v.tensor_scalar(nmu[:], bnag[:, 0:1], rstd[:], -1.0,
                                op0=ALU.mult, op1=ALU.mult)
                v.drain()
                op = v.tensor_scalar(src, src, rstd[:], nmu[:],
                                     op0=ALU.mult, op1=ALU.add)
                inc(op, "v_p")
            block.vector(_)
            v_out = cnt["v_p"]

            def _(s, j=j, v_out=v_out):
                wt(s, "v_p", v_out)
                op = s.dma_start(out=out[j * 128:(j + 1) * 128, :],
                                 in_=a_tile(j))
                out_dma.append(inc(op, "s_o"))
            block.scalar(_)

        def _(s):
            s.wait_ge(sems["s_o"], out_dma[-1])
        block.scalar(_)

    return nc


_NC_CACHE = [None]


def kernel(L0_attn_x, L0_attn_time_mix_k, L0_attn_time_mix_r, L0_attn_Wk,
           L0_attn_Wr, L0_attn_Wv, L0_attn_time_decay, L0_attn_ln_gamma,
           L0_attn_ln_beta, L0_ffn_W_gate, L0_ffn_W_qkv):
    x = np.ascontiguousarray(np.asarray(L0_attn_x, dtype=np.float32))
    B, T, _ = x.shape
    wkTh = np.ascontiguousarray(np.asarray(L0_attn_Wk, np.float32).T)
    wrTh = np.ascontiguousarray(np.asarray(L0_attn_Wr, np.float32).T)
    wvTh = np.ascontiguousarray(np.asarray(L0_attn_Wv, np.float32).T)
    wq = np.asarray(L0_ffn_W_qkv, np.float32)
    wqTh = np.ascontiguousarray(wq[0:D, :].T)
    wkvgTh = np.ascontiguousarray(
        np.concatenate([wq[D:3 * D, :].T,
                        np.asarray(L0_ffn_W_gate, np.float32).T], axis=1))
    tmkh = np.ascontiguousarray(
        np.asarray(L0_attn_time_mix_k, np.float32).reshape(NCC, 128).T)
    tmrh = np.ascontiguousarray(
        np.asarray(L0_attn_time_mix_r, np.float32).reshape(NCC, 128).T)
    tdh = np.ascontiguousarray(
        np.asarray(L0_attn_time_decay, np.float32).reshape(NHC, 128).T)
    gamh = np.ascontiguousarray(
        np.asarray(L0_attn_ln_gamma, np.float32).reshape(1, D))
    beth = np.ascontiguousarray(
        np.asarray(L0_attn_ln_beta, np.float32).reshape(1, D))

    if _NC_CACHE[0] is None:
        _NC_CACHE[0] = build_nc()
    nc = _NC_CACHE[0]

    in_maps = []
    toks = []
    for core in range(8):
        b, r = core // 4, core % 4
        tok = np.concatenate(
            [np.arange(128 * g, 128 * g + 128) for g in range(r, 32, 4)])
        sh = tok - 1
        sh[tok == 0] = 1
        xb = x[b]
        in_maps.append({
            "xT": np.ascontiguousarray(xb.T[:, tok]),
            "xTs": np.ascontiguousarray(xb.T[:, sh]),
            "xtok": np.ascontiguousarray(xb[tok]),
            "wkT": wkTh, "wrT": wrTh, "wvT": wvTh, "wqT": wqTh,
            "wkvgT": wkvgTh, "tmk": tmkh, "tmr": tmrh, "td": tdh,
            "gam": gamh, "bet": beth,
        })
        toks.append((b, tok))

    res = run_bass_kernel_spmd(nc, in_maps, core_ids=list(range(8)))
    full = np.empty((B, T, D), dtype=np.float32)
    for core in range(8):
        b, tok = toks[core]
        full[b][tok] = res.results[core]["out"]
    return full


# version 14
# speedup vs baseline: 1.0405x; 1.0312x over previous
import math
import contextlib
import numpy as np
import concourse.bass as bass
import concourse.mybir as mybir
from concourse.bass_utils import run_bass_kernel_spmd

F32 = mybir.dt.float32
I32 = mybir.dt.int32
U32 = mybir.dt.uint32
ALU = mybir.AluOpType
AF = mybir.ActivationFunctionType
AX = mybir.AxisListType

D = 2048
HH = 2816
NCC = 16
NHC = 22
NJ = 8
TL = 1024
NH = 16
RG = [[0, 1, 2, 3], [4, 5, 6, 7]]
SS = 1.0 / math.sqrt(128.0)
EPS = 1e-5
NEG = -1e30


def build_nc():
    nc = bass.Bass("TRN2", target_bir_lowering=False)
    xT = nc.dram_tensor("xT", [D, TL], F32, kind="ExternalInput").ap()
    xTs = nc.dram_tensor("xTs", [D, TL], F32, kind="ExternalInput").ap()
    xtok = nc.dram_tensor("xtok", [TL, D], F32, kind="ExternalInput").ap()
    wkT = nc.dram_tensor("wkT", [D, HH], F32, kind="ExternalInput").ap()
    wrT = nc.dram_tensor("wrT", [D, HH], F32, kind="ExternalInput").ap()
    wvT = nc.dram_tensor("wvT", [HH, D], F32, kind="ExternalInput").ap()
    wqT = nc.dram_tensor("wqT", [D, D], F32, kind="ExternalInput").ap()
    wkvgT = nc.dram_tensor("wkvgT", [D, 4112], F32, kind="ExternalInput").ap()
    tmkI = nc.dram_tensor("tmk", [128, NCC], F32, kind="ExternalInput").ap()
    tmrI = nc.dram_tensor("tmr", [128, NCC], F32, kind="ExternalInput").ap()
    tdI = nc.dram_tensor("td", [128, NHC], F32, kind="ExternalInput").ap()
    gamI = nc.dram_tensor("gam", [1, D], F32, kind="ExternalInput").ap()
    betI = nc.dram_tensor("bet", [1, D], F32, kind="ExternalInput").ap()
    out = nc.dram_tensor("out", [TL, D], F32, kind="ExternalOutput").ap()
    rkd = nc.dram_tensor("rkd", [1, 1], I32).ap()
    ag1i = nc.dram_tensor("ag1i", [1, 176 * 128], F32).ap()
    ag1o = nc.dram_tensor("ag1o", [4, 176 * 128], F32).ap()
    ag2i = nc.dram_tensor("ag2i", [1, 2048 * 128], F32).ap()
    ag2o = nc.dram_tensor("ag2o", [4, 2048 * 128], F32).ap()

    with contextlib.ExitStack() as ctx:
        En = ctx.enter_context
        A = En(nc.sbuf_tensor("A", [128, 16384], F32))
        U = En(nc.sbuf_tensor("U", [128, 22528], F32))
        P = En(nc.sbuf_tensor("P", [128, 10240], F32))
        tmk = En(nc.sbuf_tensor("tmkS", [128, NCC], F32))
        tmr = En(nc.sbuf_tensor("tmrS", [128, NCC], F32))
        tdv = En(nc.sbuf_tensor("tdv", [128, NHC], F32))
        lnw = En(nc.sbuf_tensor("lnw", [128, NHC], F32))
        wvec = En(nc.sbuf_tensor("wvec", [128, NHC], F32))
        w128 = En(nc.sbuf_tensor("w128", [128, NHC], F32))
        wpow = En(nc.sbuf_tensor("wpow", [128, 256], F32))
        iocol = En(nc.sbuf_tensor("iocol", [128, 128], F32))
        iorow = En(nc.sbuf_tensor("iorow", [128, 1], F32))
        ident = En(nc.sbuf_tensor("ident", [128, 128], F32))
        dadd = En(nc.sbuf_tensor("dadd", [128, 128], F32))
        madd = En(nc.sbuf_tensor("madd", [128, 512], F32))
        rkb = En(nc.sbuf_tensor("rkb", [128, 1], I32))
        rmeq = En(nc.sbuf_tensor("rmeq", [128, 4], F32))
        rmsc = En(nc.sbuf_tensor("rmsc", [128, 4], F32))
        stage = En(nc.sbuf_tensor("stage", [128, 1024], F32))
        gbuf = En(nc.sbuf_tensor("gbuf", [128, 16], F32))
        gm8 = En(nc.sbuf_tensor("gm8", [128, 8], F32))
        gidx = En(nc.sbuf_tensor("gidx", [128, 8], U32))
        gmask = En(nc.sbuf_tensor("gmask", [128, NJ * 16], F32))
        vmask = En(nc.sbuf_tensor("vmask", [128, NJ * 16], F32))
        gnb = En(nc.sbuf_tensor("gnb", [128, 1], F32))
        gsum = En(nc.sbuf_tensor("gsum", [128, 1], F32))
        top1 = En(nc.sbuf_tensor("top1", [128, NJ], F32))
        bnst = En(nc.sbuf_tensor("bnst", [128, 24], F32))
        bnag = En(nc.sbuf_tensor("bnag", [128, 2], F32))
        rstd = En(nc.sbuf_tensor("rstd", [128, 1], F32))
        nmu = En(nc.sbuf_tensor("nmu", [128, 1], F32))
        varb = En(nc.sbuf_tensor("varb", [128, 1], F32))
        mrun = En(nc.sbuf_tensor("mrun", [128, 1], F32))
        mnew = En(nc.sbuf_tensor("mnew", [128, 2], F32))
        mtmp = En(nc.sbuf_tensor("mtmp", [128, 4], F32))
        sbias = En(nc.sbuf_tensor("sbias", [128, 2], F32))
        alpha = En(nc.sbuf_tensor("alpha", [128, 1], F32))
        drun = En(nc.sbuf_tensor("drun", [128, 1], F32))
        dtmp = En(nc.sbuf_tensor("dtmp", [128, 4], F32))
        dblk = En(nc.sbuf_tensor("dblk", [128, 1], F32))
        rdd = En(nc.sbuf_tensor("rdd", [128, 1], F32))
        ysb = En(nc.sbuf_tensor("ysb", [128, 128], F32))
        ysb2 = En(nc.sbuf_tensor("ysb2", [128, 128], F32))
        st1s = En(nc.sbuf_tensor("st1s", [128, 8], F32))
        PS = En(nc.psum_tensor("PS", [128, 4096], F32))

        sems, cnt = {}, {}
        for name in ("s_x0", "s_x1", "s_w0", "s_w1", "s_t0", "s_t1",
                     "s_g", "s_o", "s_q",
                     "t_m", "v_p", "a_p", "g_p", "c_c"):
            sems[name] = En(nc.semaphore(name))
            cnt[name] = 0

        def inc(op, name, n=None):
            if n is None:
                n = 16 if name.startswith("s_") else 1
            cnt[name] += n
            op.then_inc(sems[name], n)
            return cnt[name]

        def wt(eng, name, v):
            if v > 0:
                eng.wait_ge(sems[name], v)

        block = En(nc.Block())

        def u_m(m):
            return U[:, m * 1024:(m + 1) * 1024]

        def xk_cc(n, cc):
            return A[:, n * 8192 + cc * 512: n * 8192 + (cc + 1) * 512]

        def a_tile(j):
            return A[:, j * 2048:(j + 1) * 2048]

        def x1t_cc(cc):
            return U[:, cc * 1024:(cc + 1) * 1024]

        qh_v = lambda p: U[:, 16384 + p * 1024: 16384 + (p + 1) * 1024]
        P32 = U[:, 18432:20480]
        PT = U[:, 20480:22528]
        ks_loc = U[:, 18432:19456]
        vs_loc = U[:, 19456:20480]
        ksT_loc = U[:, 20480:21504]
        xs_v = lambda p: P[:, p * 1024:(p + 1) * 1024]
        wsl_v = lambda p: P[:, 2048 + p * 2816: 2048 + (p + 1) * 2816]
        lend = P[:, 7680:7856]
        ends = P[:, 7856:8560]
        chz = P[:, 8560:9286]
        cary = P[:, 9286:9462]
        eraw = P[:, 9462:10166]
        gamBC = P[:, 0:2048]
        betBC = P[:, 2048:4096]
        xtok_v = lambda p: P[:, 4096 + p * 2048: 4096 + (p + 1) * 2048]
        w2_v = lambda p: P[:, p * 4096:(p + 1) * 4096]
        wq_v = P[:, 0:2048]
        ksF = P[:, 2048:6144]
        vsF = P[:, 6144:10240]

        def bank(k, w=512):
            return PS[:, k * 512: k * 512 + w]

        def _(s):
            op = s.dma_start(out=tmk[:], in_=tmkI); inc(op, "s_w0")
            op = s.dma_start(out=tmr[:], in_=tmrI); inc(op, "s_w0")
            op = s.dma_start(out=tdv[:], in_=tdI); inc(op, "s_w0")
        block.sync(_)
        W_TD = cnt["s_w0"]

        def _(g):
            op = g.iota(iocol[:], pattern=[[1, 128]], base=0,
                        channel_multiplier=0,
                        allow_small_or_imprecise_dtypes=True)
            inc(op, "g_p")
            op = g.iota(iorow[:], pattern=[[1, 1]], base=0,
                        channel_multiplier=1,
                        allow_small_or_imprecise_dtypes=True)
            inc(op, "g_p")
            rk = g.cc_rank(RG)
            op = g.store(rkb[0:1, 0:1], rk); inc(op, "g_p")
            g.wait_ge(sems["g_p"], cnt["g_p"])
            op = g.dma_start(out=rkd, in_=rkb[0:1, :]); inc(op, "s_g")
            g.wait_ge(sems["s_g"], cnt["s_g"])
            op = g.dma_start(out=rkb[:], in_=rkd.partition_broadcast(128))
            inc(op, "s_g")
        block.gpsimd(_)
        G_IOTA = 3
        G_RANKB = cnt["s_g"]

        def _(s):
            wt(s, "s_w0", W_TD)
            s.activation(lnw[:], tdv[:], AF.Exp, bias=0.0, scale=1.0)
            op = s.drain(); inc(op, "a_p")
        block.scalar(_)
        A_ETD = cnt["a_p"]

        def _(v):
            wt(v, "a_p", A_ETD)
            v.tensor_scalar(lnw[:], lnw[:], -1.0, None, op0=ALU.mult)
            v.drain()
            op = v.engine_nop(); inc(op, "v_p")
        block.vector(_)
        V_LNW = cnt["v_p"]

        def _(s):
            wt(s, "v_p", V_LNW)
            s.activation(wvec[:], lnw[:], AF.Exp, bias=0.0, scale=1.0)
            s.activation(w128[:], lnw[:], AF.Exp, bias=0.0, scale=128.0)
            op = s.drain(); inc(op, "a_p")
        block.scalar(_)
        A_WV = cnt["a_p"]

        def _(v):
            wt(v, "g_p", G_IOTA)
            v.tensor_scalar(ident[:], iocol[:], iorow[:], None, op0=ALU.is_equal)
            v.tensor_scalar(dadd[:], iocol[:], iorow[:], NEG,
                            op0=ALU.is_gt, op1=ALU.mult)
            wt(v, "s_g", G_RANKB)
            for rp in range(4):
                v.tensor_scalar(rmeq[:, rp:rp + 1], rkb[:], rp, None,
                                op0=ALU.is_equal)
                v.tensor_scalar(rmsc[:, rp:rp + 1], rkb[:], rp, NEG,
                                op0=ALU.is_lt, op1=ALU.mult)
            v.drain()
            for rp in range(4):
                v.tensor_scalar(madd[:, rp * 128:(rp + 1) * 128], dadd[:],
                                rmeq[:, rp:rp + 1], rmsc[:, rp:rp + 1],
                                op0=ALU.mult, op1=ALU.add)
            v.drain()
            op = v.engine_nop(); inc(op, "v_p")
        block.vector(_)
        V_CONST = cnt["v_p"]

        def emit_xbuild(tm, first_wait_tm, first_wait_v):
            built = []
            xs_ld = []
            units = [(n, cc) for n in range(2) for cc in range(NCC)]
            for i, (n, cc) in enumerate(units):
                p = i % 2

                def _(s, n=n, cc=cc, p=p, i=i):
                    if i < 2 and first_wait_v:
                        wt(s, "v_p", first_wait_v)
                    if i >= 2:
                        wt(s, "v_p", built[i - 2])
                    op = s.dma_start(out=xs_v(p)[:, 0:512],
                                     in_=xT[cc * 128:(cc + 1) * 128,
                                            n * 512:(n + 1) * 512])
                    inc(op, "s_x%d" % p)
                    op = s.dma_start(out=xs_v(p)[:, 512:1024],
                                     in_=xTs[cc * 128:(cc + 1) * 128,
                                             n * 512:(n + 1) * 512])
                    xs_ld.append(inc(op, "s_x%d" % p))
                block.sync(_)

                def _(v, n=n, cc=cc, p=p, i=i):
                    wt(v, "s_x%d" % p, xs_ld[i])
                    if i == 0:
                        wt(v, "s_w0", W_TD)
                        if first_wait_tm:
                            wt(v, "t_m", first_wait_tm)
                    v.tensor_tensor(xk_cc(n, cc), xs_v(p)[:, 512:1024],
                                    xs_v(p)[:, 0:512], op=ALU.subtract)
                    v.drain()
                    op = v.scalar_tensor_tensor(xk_cc(n, cc), xk_cc(n, cc),
                                                tm[:, cc:cc + 1],
                                                xs_v(p)[:, 0:512],
                                                op0=ALU.mult, op1=ALU.add)
                    built.append(inc(op, "v_p"))
                block.vector(_)
            return built[NCC - 1], built[-1]

        def emit_proj(wsrc, nm, nk, rhs_fn, evac_emit, evac_list,
                      ready_v=None, ready_extra=None, slab_cols=128,
                      init_tm=0):
            t_groups = []
            slab_ld = []
            for m in range(nm):
                p = m % 2

                def _(s, m=m, p=p):
                    if m < 2:
                        wt(s, "t_m", init_tm)
                    if m >= 2:
                        wt(s, "t_m", t_groups[(m - 2) * 2 + 1])
                    op = s.dma_start(
                        out=wsl_v(p)[:, 0:nk * 128].rearrange(
                            "p (kc q) -> p kc q", kc=nk),
                        in_=wsrc.rearrange("(kc p) q -> p kc q", p=128)[
                            :, :, m * slab_cols:(m + 1) * slab_cols])
                    slab_ld.append(inc(op, "s_w%d" % p))
                block.sync(_)

                for n in range(2):
                    gi = m * 2 + n

                    def _(t, m=m, n=n, p=p, gi=gi):
                        if n == 0:
                            wt(t, "s_w%d" % p, slab_ld[m])
                        if gi == 0:
                            if ready_v is not None:
                                wt(t, "v_p", ready_v[0])
                            if ready_extra is not None:
                                ready_extra(t)
                        if gi == 1 and ready_v is not None:
                            wt(t, "v_p", ready_v[1])
                        if gi >= 2:
                            kind, val = evac_list[gi - 2]
                            wt(t, kind, val)
                        for kc in range(nk):
                            op = t.matmul(bank(gi % 2),
                                          wsl_v(p)[:, kc * 128:(kc + 1) * 128],
                                          rhs_fn(kc, n),
                                          start=(kc == 0), stop=(kc == nk - 1))
                        t_groups.append(inc(op, "t_m"))
                    block.tensor(_)
                    evac_emit(m, n, gi, t_groups)
            return t_groups

        xk_ready = emit_xbuild(tmk, 0, 0)

        k_ev = []

        def k_evac(m, n, gi, t_groups):
            def _(s, m=m, n=n, gi=gi):
                wt(s, "t_m", t_groups[gi])
                op = s.activation(u_m(m)[:, n * 512:(n + 1) * 512],
                                  bank(gi % 2), AF.Relu, bias=0.0, scale=1.0)
                k_ev.append(("a_p", inc(op, "a_p")))
            block.scalar(_)

        tg_k = emit_proj(wkT, NHC, NCC, lambda kc, n: xk_cc(n, kc),
                         k_evac, k_ev, ready_v=xk_ready)
        T_KALL = tg_k[-1]

        scan_done = []
        for m in range(NHC):
            def _(v, m=m):
                wt(v, "a_p", k_ev[m * 2 + 1][1])
                for j in range(NJ):
                    v.tensor_tensor_scan(
                        u_m(m)[:, j * 128:(j + 1) * 128],
                        wvec[:, m:m + 1].broadcast_to([128, 128]),
                        u_m(m)[:, j * 128:(j + 1) * 128], 0.0,
                        op0=ALU.mult, op1=ALU.add)
                v.drain()
                op = v.tensor_copy(lend[:, m * 8:(m + 1) * 8],
                                   u_m(m)[:, 127::128])
                scan_done.append(inc(op, "v_p"))
            block.vector(_)
        V_SCANALL = scan_done[-1]

        def _(g):
            wt(g, "v_p", V_SCANALL)
            op = g.dma_start(out=ag1i.rearrange("o (p c) -> p (o c)", p=128),
                             in_=lend[:])
            inc(op, "s_g")
            g.wait_ge(sems["s_g"], cnt["s_g"])
            op = g.collective_compute("AllGather", ALU.bypass,
                                      replica_groups=RG,
                                      ins=[ag1i.opt()], outs=[ag1o.opt()])
            inc(op, "c_c")
            g.wait_ge(sems["c_c"], cnt["c_c"])
            for r in range(4):
                op = g.dma_start(
                    out=eraw[:, r * 176:(r + 1) * 176],
                    in_=ag1o[r:r + 1, :].rearrange(
                        "o (p c) -> p (o c)", p=128))
                inc(op, "s_g")
        block.gpsimd(_)
        G_ENDS = cnt["s_g"]

        xr_ready = emit_xbuild(tmr, T_KALL, xk_ready[1])

        def _(v):
            wt(v, "s_g", G_ENDS)
            wt(v, "a_p", A_WV)
            for r in range(4):
                v.tensor_copy(
                    ends[:].rearrange("p (m j) -> p m j", m=NHC)[:, :, r::4],
                    eraw[:, r * 176:(r + 1) * 176].rearrange(
                        "p (m j) -> p m j", m=NHC))
            v.memset(chz[:], 0.0)
            v.drain()
            for m in range(NHC):
                v.tensor_tensor_scan(
                    chz[:, m * 33 + 1: m * 33 + 33],
                    w128[:, m:m + 1].broadcast_to([128, 32]),
                    ends[:, m * 32:(m + 1) * 32], 0.0,
                    op0=ALU.mult, op1=ALU.add)
            v.drain()
            for m in range(NHC):
                v.tensor_scalar(cary[:, m * 8:(m + 1) * 8],
                                chz[:, m * 33: m * 33 + 29: 4],
                                rmeq[:, 0:1], None, op0=ALU.mult)
            v.drain()
            for rp in range(1, 4):
                for m in range(NHC):
                    v.scalar_tensor_tensor(
                        cary[:, m * 8:(m + 1) * 8],
                        chz[:, m * 33 + rp: m * 33 + rp + 29: 4],
                        rmeq[:, rp:rp + 1],
                        cary[:, m * 8:(m + 1) * 8],
                        op0=ALU.mult, op1=ALU.add)
                v.drain()
            op = v.engine_nop(); inc(op, "v_p")
        block.vector(_)
        V_CARY = cnt["v_p"]

        wpow_done = []
        corr_done = []
        for m in range(NHC):
            def _(s, m=m):
                if m == 0:
                    wt(s, "a_p", A_WV)
                    wt(s, "g_p", G_IOTA)
                if m >= 2:
                    wt(s, "v_p", corr_done[m - 2])
                op = s.activation(wpow[:, (m % 2) * 128:(m % 2 + 1) * 128],
                                  iocol[:], AF.Exp, bias=lnw[:, m:m + 1],
                                  scale=lnw[:, m:m + 1])
                wpow_done.append(inc(op, "a_p"))
            block.scalar(_)

            def _(v, m=m):
                if m == 0:
                    wt(v, "v_p", V_CARY)
                wt(v, "a_p", wpow_done[m])
                for j in range(NJ):
                    v.scalar_tensor_tensor(
                        u_m(m)[:, j * 128:(j + 1) * 128],
                        wpow[:, (m % 2) * 128:(m % 2 + 1) * 128],
                        cary[:, m * 8 + j: m * 8 + j + 1],
                        u_m(m)[:, j * 128:(j + 1) * 128],
                        op0=ALU.mult, op1=ALU.add)
                v.drain()
                op = v.engine_nop()
                corr_done.append(inc(op, "v_p"))
            block.vector(_)


        r_ev = []

        def r_evac(m, n, gi, t_groups):
            def _(v, m=m, n=n, gi=gi):
                wt(v, "t_m", t_groups[gi])
                wt(v, "v_p", corr_done[m])
                v.drain()
                op = v.tensor_tensor(u_m(m)[:, n * 512:(n + 1) * 512],
                                     bank(gi % 2),
                                     u_m(m)[:, n * 512:(n + 1) * 512],
                                     op=ALU.mult)
                r_ev.append(("v_p", inc(op, "v_p")))
            block.vector(_)

        tg_r = emit_proj(wrT, NHC, NCC, lambda kc, n: xk_cc(n, kc),
                         r_evac, r_ev, ready_v=xr_ready, init_tm=T_KALL)
        T_RALL = tg_r[-1]
        V_RLAST = r_ev[-1][1]

        wv_ev = []
        wv_tr = []
        wv_at = []

        def wv_evac(m2, n, gi, t_groups):
            def _(v, gi=gi):
                wt(v, "t_m", t_groups[gi])
                if gi >= 2:
                    wt(v, "t_m", wv_tr[gi - 2])
                op = v.tensor_copy(stage[:, (gi % 2) * 512:(gi % 2 + 1) * 512],
                                   bank(gi % 2))
                wv_ev.append(("v_p", inc(op, "v_p")))
            block.vector(_)

            def _(t, m2=m2, n=n, gi=gi):
                wt(t, "v_p", wv_ev[gi][1])
                if gi == 0:
                    wt(t, "v_p", V_CONST)
                if gi >= 1:
                    wt(t, "a_p", wv_at[gi - 1])
                for jj in range(4):
                    op = t.transpose(bank(6)[:, jj * 128:(jj + 1) * 128],
                                     stage[:, (gi % 2) * 512 + jj * 128:
                                           (gi % 2) * 512 + (jj + 1) * 128],
                                     ident[:])
                wv_tr.append(inc(op, "t_m"))
            block.tensor(_)

            def _(s, m2=m2, n=n, gi=gi):
                wt(s, "t_m", wv_tr[gi])
                op = s.activation(
                    A[:, 0:16384].rearrange("p (j c) -> p j c", j=8)[
                        :, 4 * n: 4 * n + 4, m2 * 128:(m2 + 1) * 128],
                    bank(6).rearrange("p (jj c) -> p jj c", jj=4),
                    AF.Copy, bias=0.0, scale=1.0)
                wv_at.append(inc(op, "a_p"))
            block.scalar(_)

        def wv_first_wait(t):
            wt(t, "v_p", V_RLAST)

        tg_wv = emit_proj(wvT, NCC, NHC,
                          lambda hc, n: u_m(hc)[:, n * 512:(n + 1) * 512],
                          wv_evac, wv_ev, ready_extra=wv_first_wait,
                          init_tm=T_RALL)
        T_WVALL = tg_wv[-1]
        A_ATTNTOK = wv_at[-1]

        def _(g):
            wt(g, "v_p", xr_ready[1])
            wt(g, "t_m", T_WVALL)
            op = g.dma_start(out=gamBC, in_=gamI.partition_broadcast(128))
            inc(op, "s_g")
            op = g.dma_start(out=betBC, in_=betI.partition_broadcast(128))
            inc(op, "s_g")
        block.gpsimd(_)
        G_GB = cnt["s_g"]

        def emit_lnstats(v, src):
            for c4 in range(4):
                v.bn_stats(bnst[:, c4 * 6:(c4 + 1) * 6],
                           src[:, c4 * 512:(c4 + 1) * 512])
            v.drain()
            v.bn_aggr(bnag[:], bnst[:])
            v.drain()
            v.tensor_scalar(varb[:], bnag[:, 1:2], 1.0, EPS,
                            op0=ALU.mult, op1=ALU.add)
            v.drain()

        xtok_ld = []
        ln_done = []
        x1t_done = []
        for j in range(NJ):
            def _(g, j=j):
                if j == 0:
                    wt(g, "t_m", T_WVALL)
                if j >= 2:
                    wt(g, "v_p", ln_done[j - 2])
                op = g.dma_start(out=xtok_v(j % 2),
                                 in_=xtok[j * 128:(j + 1) * 128, :])
                xtok_ld.append(inc(op, "s_t%d" % (j % 2)))
            block.gpsimd(_)

            def _(v, j=j):
                wt(v, "a_p", A_ATTNTOK)
                emit_lnstats(v, a_tile(j))
                op = v.engine_nop(); inc(op, "v_p")
            block.vector(_)
            v_st1 = cnt["v_p"]

            def _(s, j=j, v_st1=v_st1):
                wt(s, "v_p", v_st1)
                op = s.activation(rstd[:], varb[:], AF.Sqrt, bias=0.0, scale=1.0)
                inc(op, "a_p")
            block.scalar(_)
            a_rs1 = cnt["a_p"]

            def _(v, j=j, a_rs1=a_rs1):
                wt(v, "a_p", a_rs1)
                src = a_tile(j)
                v.reciprocal(rstd[:], rstd[:])
                v.drain()
                v.tensor_scalar(nmu[:], bnag[:, 0:1], rstd[:], -1.0,
                                op0=ALU.mult, op1=ALU.mult)
                v.drain()
                v.tensor_scalar(src, src, rstd[:], nmu[:],
                                op0=ALU.mult, op1=ALU.add)
                v.drain()
                wt(v, "s_g", G_GB)
                v.tensor_tensor(src, src, gamBC, op=ALU.mult)
                v.drain()
                v.tensor_tensor(src, src, betBC, op=ALU.add)
                v.drain()
                wt(v, "s_t%d" % (j % 2), xtok_ld[j])
                v.tensor_tensor(src, src, xtok_v(j % 2), op=ALU.add)
                v.drain()
                emit_lnstats(v, src)
                op = v.engine_nop(); inc(op, "v_p")
            block.vector(_)
            v_st2 = cnt["v_p"]

            def _(s, j=j, v_st2=v_st2):
                wt(s, "v_p", v_st2)
                op = s.activation(rstd[:], varb[:], AF.Sqrt, bias=0.0, scale=1.0)
                inc(op, "a_p")
            block.scalar(_)
            a_rs2 = cnt["a_p"]

            def _(v, j=j, a_rs2=a_rs2):
                wt(v, "a_p", a_rs2)
                src = a_tile(j)
                v.reciprocal(rstd[:], rstd[:])
                v.drain()
                v.tensor_scalar(nmu[:], bnag[:, 0:1], rstd[:], -1.0,
                                op0=ALU.mult, op1=ALU.mult)
                v.drain()
                op = v.tensor_scalar(src, src, rstd[:], nmu[:],
                                     op0=ALU.mult, op1=ALU.add)
                ln_done.append(inc(op, "v_p"))
            block.vector(_)

            for q4 in range(4):
                def _(t, j=j, q4=q4):
                    if q4 == 0:
                        wt(t, "v_p", ln_done[j])
                    if x1t_done:
                        wt(t, "a_p", x1t_done[-1])
                    for mm in range(4):
                        m2 = q4 * 4 + mm
                        op = t.transpose(bank(6)[:, mm * 128:(mm + 1) * 128],
                                         a_tile(j)[:, m2 * 128:(m2 + 1) * 128],
                                         ident[:])
                    inc(op, "t_m")
                block.tensor(_)
                t_x1 = cnt["t_m"]

                def _(s, j=j, q4=q4, t_x1=t_x1):
                    wt(s, "t_m", t_x1)
                    op = s.activation(
                        U[:, 0:16384].rearrange("p (cc c) -> p cc c", cc=16)[
                            :, q4 * 4: q4 * 4 + 4, j * 128:(j + 1) * 128],
                        bank(6).rearrange("p (mm c) -> p mm c", mm=4),
                        AF.Copy, bias=0.0, scale=1.0)
                    x1t_done.append(inc(op, "a_p"))
                block.scalar(_)
        A_X1T = x1t_done[-1]
        V_LNALL = ln_done[-1]

        kv_chunks = [(4096, 16)] + [(i * 256, 256) for i in range(16)]
        kv_slab_ld = []
        kv_tg = []
        kv_ev = []
        for ci, (c0, w) in enumerate(kv_chunks):
            p = ci % 2

            def _(s, ci=ci, c0=c0, w=w, p=p):
                if ci < 2:
                    wt(s, "v_p", V_LNALL)
                if ci >= 2:
                    wt(s, "t_m", kv_tg[(ci - 2) * NJ + NJ - 1])
                op = s.dma_start(
                    out=w2_v(p)[:, 0:16 * w].rearrange(
                        "p (kc q) -> p kc q", kc=16),
                    in_=wkvgT.rearrange("(kc p) q -> p kc q", p=128)[
                        :, :, c0:c0 + w])
                kv_slab_ld.append(inc(op, "s_w%d" % p))
            block.sync(_)

            for j in range(NJ):
                def _(t, ci=ci, j=j, p=p, w=w):
                    if j == 0:
                        wt(t, "s_w%d" % p, kv_slab_ld[ci])
                        if ci == 0:
                            wt(t, "a_p", A_X1T)
                    if ci >= 1:
                        wt(t, "v_p", kv_ev[(ci - 1) * NJ + j])
                    for kc in range(NCC):
                        op = t.matmul(bank(j, w),
                                      x1t_cc(kc)[:, j * 128:(j + 1) * 128],
                                      w2_v(p)[:, kc * w:(kc + 1) * w],
                                      start=(kc == 0), stop=(kc == NCC - 1))
                    kv_tg.append(inc(op, "t_m"))
                block.tensor(_)
                t_here = kv_tg[-1]

                if ci == 0:
                    def _(v, j=j, t_here=t_here):
                        wt(v, "t_m", t_here)
                        v.tensor_copy(gbuf[:], bank(j, 16))
                        v.drain()
                        v.max(gm8[:], gbuf[:])
                        v.drain()
                        v.max_index(gidx[:], gm8[:], gbuf[:])
                        v.tensor_scalar(gnb[:], gm8[:, 0:1], -1.0, None,
                                        op0=ALU.mult)
                        v.drain()
                        op = v.engine_nop(); inc(op, "v_p")
                    block.vector(_)
                    v_g = cnt["v_p"]

                    def _(s, j=j, v_g=v_g):
                        wt(s, "v_p", v_g)
                        op = s.activation(gbuf[:], gbuf[:], AF.Exp,
                                          bias=gnb[:], scale=1.0,
                                          accum_out=gsum[:])
                        inc(op, "a_p")
                    block.scalar(_)
                    a_g = cnt["a_p"]

                    def _(v, j=j, a_g=a_g):
                        wt(v, "a_p", a_g)
                        v.reciprocal(top1[:, j:j + 1], gsum[:])
                        for h in range(16):
                            v.tensor_scalar(gmask[:, j * 16 + h: j * 16 + h + 1],
                                            gidx[:, 0:1], h, None,
                                            op0=ALU.is_equal)
                        v.drain()
                        op = v.tensor_scalar(vmask[:, j * 16:(j + 1) * 16],
                                             gmask[:, j * 16:(j + 1) * 16],
                                             top1[:, j:j + 1], None,
                                             op0=ALU.mult)
                        kv_ev.append(inc(op, "v_p"))
                    block.vector(_)
                else:
                    def _(v, ci=ci, j=j, t_here=t_here):
                        wt(v, "t_m", t_here)
                        nn = ci - 1
                        if nn < 8:
                            dst = ks_loc[:, j * 128:(j + 1) * 128]
                            msk = gmask
                            h0 = 2 * nn
                        else:
                            dst = vs_loc[:, j * 128:(j + 1) * 128]
                            msk = vmask
                            h0 = 2 * (nn - 8)
                        if h0 == 0:
                            v.tensor_scalar(dst, bank(j, 256)[:, 0:128],
                                            msk[:, j * 16: j * 16 + 1],
                                            None, op0=ALU.mult)
                        else:
                            v.scalar_tensor_tensor(
                                dst, bank(j, 256)[:, 0:128],
                                msk[:, j * 16 + h0: j * 16 + h0 + 1],
                                dst, op0=ALU.mult, op1=ALU.add)
                        v.drain()
                        op = v.scalar_tensor_tensor(
                            dst, bank(j, 256)[:, 128:256],
                            msk[:, j * 16 + h0 + 1: j * 16 + h0 + 2],
                            dst, op0=ALU.mult, op1=ALU.add)
                        kv_ev.append(inc(op, "v_p"))
                    block.vector(_)
        V_KVDONE = kv_ev[-1]
        T_KVALL = kv_tg[-1]

        ksT_tr = []
        ksT_ev = []
        for j in range(NJ):
            def _(t, j=j):
                if j == 0:
                    wt(t, "v_p", V_KVDONE)
                if j >= 4:
                    wt(t, "a_p", ksT_ev[j - 4])
                op = t.transpose(bank(6)[:, (j % 4) * 128:(j % 4 + 1) * 128],
                                 ks_loc[:, j * 128:(j + 1) * 128], ident[:])
                ksT_tr.append(inc(op, "t_m"))
            block.tensor(_)

            def _(s, j=j):
                wt(s, "t_m", ksT_tr[j])
                op = s.activation(ksT_loc[:, j * 128:(j + 1) * 128],
                                  bank(6)[:, (j % 4) * 128:(j % 4 + 1) * 128],
                                  AF.Copy, bias=0.0, scale=1.0)
                ksT_ev.append(inc(op, "a_p"))
            block.scalar(_)

        def _(g):
            wt(g, "a_p", ksT_ev[-1])
            op = g.dma_start(out=ag2i[:, 0:131072].rearrange(
                "o (p c) -> p (o c)", p=128), in_=ksT_loc[:])
            inc(op, "s_g")
            op = g.dma_start(out=ag2i[:, 131072:262144].rearrange(
                "o (p c) -> p (o c)", p=128), in_=vs_loc[:])
            inc(op, "s_g")
            g.wait_ge(sems["s_g"], cnt["s_g"])
            op = g.collective_compute("AllGather", ALU.bypass,
                                      replica_groups=RG,
                                      ins=[ag2i.opt()], outs=[ag2o.opt()])
            inc(op, "c_c")
            g.wait_ge(sems["c_c"], cnt["c_c"])
            wt(g, "t_m", T_KVALL)
            for r in range(4):
                op = g.dma_start(
                    out=ksF.rearrange("p (j q) -> p j q", j=32)[:, r::4, :],
                    in_=ag2o[r:r + 1, 0:131072].rearrange(
                        "o (p c) -> p (o c)", p=128).rearrange(
                        "p (j q) -> p j q", j=8))
                inc(op, "s_g")
                op = g.dma_start(
                    out=vsF.rearrange("p (j q) -> p j q", j=32)[:, r::4, :],
                    in_=ag2o[r:r + 1, 131072:262144].rearrange(
                        "o (p c) -> p (o c)", p=128).rearrange(
                        "p (j q) -> p j q", j=8))
                inc(op, "s_g")
        block.gpsimd(_)
        G_KSVS = cnt["s_g"]

        BLOCKS_PER_HEAD = 20
        wq_ld = []
        t_q_done = []
        y_last = [0] * NJ

        sblocks = []
        for hh in range(8):
            for j in range(NJ):
                G = j + 1
                nb = (G + 1) // 2
                for b in range(nb):
                    g0, g1 = 2 * b, min(2 * b + 2, G)
                    sblocks.append((hh, j, b, g0, g1, b == nb - 1))
        NSB = len(sblocks)

        def sb_par(st):
            return PS[:, 1024 + st * 1024: 2048 + st * 1024]

        def p32h(st):
            return P32[:, st * 1024:(st + 1) * 1024]

        def pth(st):
            return PT[:, st * 1024:(st + 1) * 1024]

        ybank = lambda st: PS[:, 3584 + st * 128: 3584 + (st + 1) * 128]

        MRUN = [mrun[:, 0:1], st1s[:, 0:1]]
        DRUN = [drun[:, 0:1], st1s[:, 1:2]]
        DBLK = [dblk[:, 0:1], st1s[:, 2:3]]
        ALPH = [alpha[:, 0:1], st1s[:, 3:4]]
        RDD = [rdd[:, 0:1], st1s[:, 4:5]]
        YSB = [ysb[:, :], ysb2[:, :]]
        MT = [mtmp[:, 0:2], mtmp[:, 2:4]]

        expd = [{}, {}]
        ptev = [{}, {}]
        tpt = [{}, {}]
        tpv = [{}, {}]
        vacc = [{}, {}]
        vmx = [{}, {}]
        tqk = [{}, {}]
        vq = [{}, {}]
        vq_last = [0]
        yread_last = [0]
        last_ptev = [0]
        last_banks6_user = [0]

        def emit_qmm(st, hh):
            h = st * 8 + hh
            def _(s, st=st, hh=hh, h=h):
                if not wq_ld:
                    wt(s, "t_m", T_KVALL)
                else:
                    wt(s, "t_m", t_q_done[-1])
                op = s.dma_start(
                    out=wq_v[:, :].rearrange("p (kc q) -> p kc q", kc=16),
                    in_=wqT.rearrange("(kc p) q -> p kc q", p=128)[
                        :, :, h * 128:(h + 1) * 128])
                wq_ld.append(inc(op, "s_q"))
            block.sync(_)
            for n in range(2):
                def _(t, st=st, hh=hh, n=n):
                    if n == 0:
                        wt(t, "s_q", wq_ld[-1])
                        if vq_last[0]:
                            wt(t, "v_p", vq_last[0])
                        if last_ptev[0]:
                            wt(t, "v_p", last_ptev[0])
                    for kc in range(NCC):
                        op = t.matmul(bank(n),
                                      wq_v[:, kc * 128:(kc + 1) * 128],
                                      x1t_cc(kc)[:, n * 512:(n + 1) * 512],
                                      start=(kc == 0), stop=(kc == NCC - 1))
                    inc(op, "t_m")
                block.tensor(_)
                t_qn = cnt["t_m"]

                def _(v, st=st, hh=hh, n=n, t_qn=t_qn):
                    wt(v, "t_m", t_qn)
                    if hh >= 1 and n == 0:
                        wt(v, "t_m", tqk[st][hh * BLOCKS_PER_HEAD - 1])
                    op = v.tensor_copy(qh_v(st)[:, n * 512:(n + 1) * 512],
                                       bank(n))
                    inc(op, "v_p")
                block.vector(_)
            t_q_done.append(cnt["t_m"])
            vq[st][hh] = cnt["v_p"]
            vq_last[0] = cnt["v_p"]

        def emit_qk(st, k):
            hh, j, b, g0, g1, last = sblocks[k]
            def _(t, st=st, k=k, hh=hh, j=j, g0=g0, g1=g1):
                if k == 0 and st == 0:
                    wt(t, "s_g", G_KSVS)
                wt(t, "v_p", vq[st][hh])
                if k >= 1:
                    wt(t, "a_p", expd[st][k - 1])
                for g in range(g0, g1):
                    op = t.matmul(sb_par(st)[:, (g - g0) * 512:(g - g0 + 1) * 512],
                                  qh_v(st)[:, j * 128:(j + 1) * 128],
                                  ksF[:, g * 512:(g + 1) * 512],
                                  start=True, stop=True)
                tqk[st][k] = inc(op, "t_m")
            block.tensor(_)

        def emit_maxbias_pair(k):
            hh, j, b, g0, g1, last = sblocks[k]
            G = j + 1
            def _(v, k=k, g0=g0, g1=g1, b=b, G=G):
                ng = g1 - g0
                for st in range(2):
                    wt(v, "t_m", tqk[st][k])
                    if k >= 1:
                        wt(v, "a_p", expd[st][k - 1])
                    if g1 == G:
                        v.tensor_tensor(sb_par(st)[:, (ng - 1) * 512:ng * 512],
                                        sb_par(st)[:, (ng - 1) * 512:ng * 512],
                                        madd[:], op=ALU.add)
                        v.drain()
                    v.reduce_max(mnew[:, st:st + 1], sb_par(st)[:, 0:ng * 512],
                                 axis=AX.X)
                    v.drain()
                    if b > 0:
                        v.tensor_tensor(mnew[:, st:st + 1], mnew[:, st:st + 1],
                                        MRUN[st], op=ALU.max)
                        v.drain()
                    op = v.tensor_scalar(sbias[:, st:st + 1],
                                         mnew[:, st:st + 1],
                                         -SS, None, op0=ALU.mult)
                    vmx[st][k] = inc(op, "v_p")
            block.vector(_)

        def emit_exp_pair(k):
            hh, j, b, g0, g1, last = sblocks[k]
            ng = g1 - g0
            def _(s, k=k, ng=ng, b=b):
                for st in range(2):
                    wt(s, "v_p", vmx[st][k])
                    if k >= 1:
                        wt(s, "t_m", tpt[st][k - 1])
                    s.activation(
                        p32h(st)[:, 0:ng * 512], sb_par(st)[:, 0:ng * 512],
                        AF.Exp, bias=sbias[:, st:st + 1], scale=SS,
                        accum_out=dtmp[:, st * 2: st * 2 + 1])
                    if b > 0:
                        s.activation(ALPH[st], MRUN[st], AF.Exp,
                                     bias=sbias[:, st:st + 1], scale=SS)
                    op = s.drain()
                    expd[st][k] = inc(op, "a_p")
            block.scalar(_)

        def emit_ptpv(st, k):
            hh, j, b, g0, g1, last = sblocks[k]
            ng = g1 - g0
            ntile = ng * 4

            def _(t, st=st, k=k, ntile=ntile):
                wt(t, "a_p", expd[st][k])
                if last_ptev[0]:
                    wt(t, "v_p", last_ptev[0])
                if vq_last[0]:
                    wt(t, "v_p", vq_last[0])
                for tt in range(min(4, ntile)):
                    op = t.transpose(bank(6)[:, tt * 128:(tt + 1) * 128],
                                     p32h(st)[:, tt * 128:(tt + 1) * 128],
                                     ident[:])
                for tt in range(4, ntile):
                    op = t.transpose(bank(0)[:, (tt - 4) * 128:(tt - 3) * 128],
                                     p32h(st)[:, tt * 128:(tt + 1) * 128],
                                     ident[:])
                tpt[st][k] = inc(op, "t_m")
            block.tensor(_)

            def _(v, st=st, k=k, ntile=ntile):
                wt(v, "t_m", tpt[st][k])
                if k >= 1 and (k - 1) in tpv[st]:
                    wt(v, "t_m", tpv[st][k - 1])
                v.tensor_copy(pth(st)[:, 0:min(4, ntile) * 128],
                              bank(6)[:, 0:min(4, ntile) * 128])
                if ntile > 4:
                    op = v.tensor_copy(pth(st)[:, 512:ntile * 128],
                                       bank(0)[:, 0:(ntile - 4) * 128])
                else:
                    op = v.engine_nop()
                last_ptev[0] = inc(op, "v_p")
            block.vector(_)
            pt_ev_here = last_ptev[0]

            def _(t, st=st, k=k, ntile=ntile, b=b, pt_ev_here=pt_ev_here):
                wt(t, "v_p", pt_ev_here)
                if yread_last[0]:
                    wt(t, "v_p", yread_last[0])
                for tt in range(ntile):
                    kt = b * 8 + tt
                    op = t.matmul(ybank(st),
                                  pth(st)[:, tt * 128:(tt + 1) * 128],
                                  vsF[:, kt * 128:(kt + 1) * 128],
                                  start=(tt == 0), stop=(tt == ntile - 1))
                tpv[st][k] = inc(op, "t_m")
            block.tensor(_)

        def emit_accum(st, k):
            hh, j, b, g0, g1, last = sblocks[k]
            h = st * 8 + hh
            ng = g1 - g0
            if b > 0:
                def _(v, st=st, k=k, h=h, j=j, last=last):
                    wt(v, "t_m", tpv[st][k])
                    wt(v, "a_p", expd[st][k])
                    v.scalar_tensor_tensor(DRUN[st], DRUN[st], ALPH[st],
                                           dtmp[:, st * 2: st * 2 + 1],
                                           op0=ALU.mult, op1=ALU.add)
                    v.scalar_tensor_tensor(YSB[st], YSB[st], ALPH[st],
                                           ybank(st),
                                           op0=ALU.mult, op1=ALU.add)
                    v.drain()
                    v.tensor_copy(MRUN[st], mnew[:, st:st + 1])
                    v.drain()
                    if last:
                        v.reciprocal(RDD[st], DRUN[st])
                        v.drain()
                        v.scalar_tensor_tensor(
                            a_tile(j)[:, h * 128:(h + 1) * 128],
                            YSB[st], RDD[st],
                            a_tile(j)[:, h * 128:(h + 1) * 128],
                            op0=ALU.mult, op1=ALU.add)
                        v.drain()
                    op = v.engine_nop()
                    vacc[st][k] = inc(op, "v_p")
                    yread_last[0] = cnt["v_p"]
                    if last:
                        y_last[j] = cnt["v_p"]
                block.vector(_)
            else:
                def _(v, st=st, k=k, h=h, j=j, ng=ng, last=last):
                    wt(v, "t_m", tpv[st][k])
                    wt(v, "a_p", expd[st][k])
                    v.tensor_copy(DRUN[st], dtmp[:, st * 2: st * 2 + 1])
                    v.tensor_copy(MRUN[st], mnew[:, st:st + 1])
                    v.tensor_copy(YSB[st], ybank(st))
                    v.drain()
                    if last:
                        v.reciprocal(RDD[st], DRUN[st])
                        v.drain()
                        v.scalar_tensor_tensor(
                            a_tile(j)[:, h * 128:(h + 1) * 128],
                            YSB[st], RDD[st],
                            a_tile(j)[:, h * 128:(h + 1) * 128],
                            op0=ALU.mult, op1=ALU.add)
                        v.drain()
                    op = v.engine_nop()
                    vacc[st][k] = inc(op, "v_p")
                    yread_last[0] = cnt["v_p"]
                    if last:
                        y_last[j] = cnt["v_p"]
                block.vector(_)

        emit_qmm(0, 0)
        emit_qmm(1, 0)
        emit_qk(0, 0)
        emit_qk(1, 0)
        for k in range(NSB):
            emit_maxbias_pair(k)
            emit_exp_pair(k)
            if k + 1 < NSB:
                if (k + 1) % BLOCKS_PER_HEAD == 0:
                    emit_qmm(0, (k + 1) // BLOCKS_PER_HEAD)
                    emit_qmm(1, (k + 1) // BLOCKS_PER_HEAD)
                emit_qk(0, k + 1)
            emit_ptpv(0, k)
            emit_accum(0, k)
            if k + 1 < NSB:
                emit_qk(1, k + 1)
            emit_ptpv(1, k)
            emit_accum(1, k)
        t_attn_end = cnt["t_m"]

        out_dma = []
        for j in range(NJ):
            vj = y_last[j]

            def _(v, j=j, vj=vj):
                wt(v, "v_p", vj)
                emit_lnstats(v, a_tile(j))
                op = v.engine_nop(); inc(op, "v_p")
            block.vector(_)
            v_st = cnt["v_p"]

            def _(s, v_st=v_st):
                wt(s, "v_p", v_st)
                op = s.activation(rstd[:], varb[:], AF.Sqrt, bias=0.0, scale=1.0)
                inc(op, "a_p")
            block.scalar(_)
            a_rs = cnt["a_p"]

            def _(v, j=j, a_rs=a_rs):
                wt(v, "a_p", a_rs)
                src = a_tile(j)
                v.reciprocal(rstd[:], rstd[:])
                v.drain()
                v.tensor_scalar(nmu[:], bnag[:, 0:1], rstd[:], -1.0,
                                op0=ALU.mult, op1=ALU.mult)
                v.drain()
                op = v.tensor_scalar(src, src, rstd[:], nmu[:],
                                     op0=ALU.mult, op1=ALU.add)
                inc(op, "v_p")
            block.vector(_)
            v_out = cnt["v_p"]

            def _(s, j=j, v_out=v_out):
                wt(s, "v_p", v_out)
                op = s.dma_start(out=out[j * 128:(j + 1) * 128, :],
                                 in_=a_tile(j))
                out_dma.append(inc(op, "s_o"))
            block.scalar(_)

        def _(s):
            s.wait_ge(sems["s_o"], out_dma[-1])
        block.scalar(_)

    return nc


_NC_CACHE = [None]


def kernel(L0_attn_x, L0_attn_time_mix_k, L0_attn_time_mix_r, L0_attn_Wk,
           L0_attn_Wr, L0_attn_Wv, L0_attn_time_decay, L0_attn_ln_gamma,
           L0_attn_ln_beta, L0_ffn_W_gate, L0_ffn_W_qkv):
    x = np.ascontiguousarray(np.asarray(L0_attn_x, dtype=np.float32))
    B, T, _ = x.shape
    wkTh = np.ascontiguousarray(np.asarray(L0_attn_Wk, np.float32).T)
    wrTh = np.ascontiguousarray(np.asarray(L0_attn_Wr, np.float32).T)
    wvTh = np.ascontiguousarray(np.asarray(L0_attn_Wv, np.float32).T)
    wq = np.asarray(L0_ffn_W_qkv, np.float32)
    wqTh = np.ascontiguousarray(wq[0:D, :].T)
    wkvgTh = np.ascontiguousarray(
        np.concatenate([wq[D:3 * D, :].T,
                        np.asarray(L0_ffn_W_gate, np.float32).T], axis=1))
    tmkh = np.ascontiguousarray(
        np.asarray(L0_attn_time_mix_k, np.float32).reshape(NCC, 128).T)
    tmrh = np.ascontiguousarray(
        np.asarray(L0_attn_time_mix_r, np.float32).reshape(NCC, 128).T)
    tdh = np.ascontiguousarray(
        np.asarray(L0_attn_time_decay, np.float32).reshape(NHC, 128).T)
    gamh = np.ascontiguousarray(
        np.asarray(L0_attn_ln_gamma, np.float32).reshape(1, D))
    beth = np.ascontiguousarray(
        np.asarray(L0_attn_ln_beta, np.float32).reshape(1, D))

    if _NC_CACHE[0] is None:
        _NC_CACHE[0] = build_nc()
    nc = _NC_CACHE[0]

    in_maps = []
    toks = []
    for core in range(8):
        b, r = core // 4, core % 4
        tok = np.concatenate(
            [np.arange(128 * g, 128 * g + 128) for g in range(r, 32, 4)])
        sh = tok - 1
        sh[tok == 0] = 1
        xb = x[b]
        in_maps.append({
            "xT": np.ascontiguousarray(xb.T[:, tok]),
            "xTs": np.ascontiguousarray(xb.T[:, sh]),
            "xtok": np.ascontiguousarray(xb[tok]),
            "wkT": wkTh, "wrT": wrTh, "wvT": wvTh, "wqT": wqTh,
            "wkvgT": wkvgTh, "tmk": tmkh, "tmr": tmrh, "td": tdh,
            "gam": gamh, "bet": beth,
        })
        toks.append((b, tok))

    res = run_bass_kernel_spmd(nc, in_maps, core_ids=list(range(8)))
    full = np.empty((B, T, D), dtype=np.float32)
    for core in range(8):
        b, tok = toks[core]
        full[b][tok] = res.results[core]["out"]
    return full
